# version 07
# baseline (speedup 1.0000x reference)
import contextlib
import math

import numpy as np
import ml_dtypes

import concourse.bass as bass
import concourse.mybir as mybir
from concourse.bass_utils import run_bass_kernel_spmd

F32 = mybir.dt.float32
BF16 = mybir.dt.bfloat16
AF = mybir.ActivationFunctionType
ALU = mybir.AluOpType
AX = mybir.AxisListType
bfdt = ml_dtypes.bfloat16

H = 2048
FC = 16
NH = 16
DK = 128
CH = 128
NCH = 4
NT = 4096
MAIN0 = 1536
T_MAIN = 2560
OWN0 = 2048
OWN = 2048
E = 16
IM = 1408
JM = 11
ISH = 5632
JSH = 44
LN_EPS = 1e-5
AA_EPS = 1e-5
LNC = math.log(DK ** -0.5)
BQS = 1.0 / math.sqrt(DK)
BIGNEG = 1.0e30
SH = 52 * math.log(2.0)
SC_MASK = 2.0 ** 104

ROLES = ["SP", "POOL", "PE", "ACT", "DVE"]
SEMS = (["ld" + c for c in "ABCD"] + ["st" + c for c in "ABC"] +
        ["wl" + c for c in "ABCDEFGHIJ"] + ["mm", "ev", "dv", "bar"])


def _region(ap):
    h = ap.tensor if hasattr(ap, "tensor") else ap
    if not hasattr(ap, "ap"):
        shp = h.shape
        return (h.name, 0, 128, 0, int(np.prod(shp)))
    name = h.name
    shp = list(h.shape)
    off = ap.offset
    dims = [list(x) for x in ap.ap]
    memtype = getattr(h, "__class__", None).__name__
    if memtype in ("SBTensorHandle", "PSumTensorHandle") and len(shp) == 2:
        rowlen = shp[1]
        p0 = off // rowlen
        c0 = off % rowlen
        pspan = 0
        cspan = 0
        for (st, cnt) in dims:
            if cnt <= 1:
                continue
            if st != 0 and st % rowlen == 0:
                pspan += (st // rowlen) * (cnt - 1)
            else:
                cspan += abs(st) * (cnt - 1)
        c1 = c0 + cspan + 1
        if memtype == "PSumTensorHandle":
            bank = 512 if h.dtype == mybir.dt.float32 else 1024
            c0 = (c0 // bank) * bank
            c1 = ((c1 + bank - 1) // bank) * bank
            return (name, 0, 128, c0, c1)
        return (name, p0, p0 + pspan + 1, c0, c1)
    span = 0
    for (st, cnt) in dims:
        if cnt > 1:
            span += abs(st) * (cnt - 1)
    return (name, 0, 1 << 30, off, off + span + 1)


def _overlap(a, b):
    return (a[0] == b[0] and a[1] < b[2] and b[1] < a[2]
            and a[3] < b[4] and b[3] < a[4])


class Ctx:
    ENG_SEM = {"PE": "mm", "ACT": "ev", "DVE": "dv"}

    def __init__(self, nc, block, sems):
        self.nc = nc
        self.block = block
        self.sem = sems
        self.cnt = {k: 0 for k in SEMS}
        self.role = None
        self.eng = None
        self.hiw = {}
        self.twr = {}
        self.trd = {}
        self.pe_pending = []
        self.dma_owner = {}

    def t_clear(self):
        self.twr.clear()
        self.trd.clear()
        self.pe_pending = []

    def _dep_read(self, ap, deps):
        r = _region(ap)
        for w in self.twr.get(r[0], ()):
            if _overlap(r, (r[0], w[0], w[1], w[2], w[3])):
                if w[5] is None:
                    raise RuntimeError(
                        f"read of {r[0]} depends on unmarked PE write")
                deps.append((w[4], w[5]))

    def _note_read(self, ap, sem, cntv):
        r = _region(ap)
        ent = [r[1], r[2], r[3], r[4], sem, cntv]
        self.trd.setdefault(r[0], []).append(ent)
        return ent

    def _dep_write(self, ap, deps, is_pe):
        r = _region(ap)
        ws = self.twr.get(r[0], ())
        for w in ws:
            if _overlap(r, (r[0], w[0], w[1], w[2], w[3])):
                if is_pe and w[6]:
                    continue
                if w[5] is None:
                    raise RuntimeError(
                        f"write of {r[0]} overlaps unmarked PE write")
                deps.append((w[4], w[5]))
        for rd in self.trd.get(r[0], ()):
            if _overlap(r, (r[0], rd[0], rd[1], rd[2], rd[3])):
                if rd[5] is None:
                    raise RuntimeError(
                        f"write of {r[0]} overlaps PE read not yet covered "
                        f"by a PE mark")
                deps.append((rd[4], rd[5]))

    def _note_write(self, ap, sem, cntv, is_pe):
        r = _region(ap)
        lst = self.twr.setdefault(r[0], [])
        lst[:] = [w for w in lst
                  if not (r[1] <= w[0] and w[1] <= r[2]
                          and r[3] <= w[2] and w[3] <= r[4])]
        ent = [r[1], r[2], r[3], r[4], sem, cntv, is_pe]
        lst.append(ent)
        rl = self.trd.get(r[0])
        if rl:
            rl[:] = [x for x in rl
                     if not (r[1] <= x[0] and x[1] <= r[2]
                             and r[3] <= x[2] and x[3] <= r[4])]
        return ent

    def _emit_op(self, engine, reads, writes, waits, mark, emit_fn,
                 is_dma=False):
        own = self.ENG_SEM.get(engine)
        deps = list(waits)
        for ap in reads:
            if ap is not None and not isinstance(ap, (int, float)):
                self._dep_read(ap, deps)
        for ap in writes:
            self._dep_write(ap, deps, engine == "PE")
        if is_dma:
            sem = mark
            prev = self.dma_owner.setdefault(sem, engine)
            assert prev == engine, f"dma sem {sem} used from {prev} and {engine}"
            self.cnt[sem] += 16
            cntv = self.cnt[sem]
        elif engine == "PE":
            if mark:
                self.cnt["mm"] += 1
                cntv = self.cnt["mm"]
            else:
                cntv = None
            sem = "mm"
        else:
            sem = own
            self.cnt[sem] += 1
            cntv = self.cnt[sem]
        for ap in reads:
            if ap is not None and not isinstance(ap, (int, float)):
                e = self._note_read(ap, sem,
                                    None if (engine == "PE" and not is_dma)
                                    else cntv)
                if engine == "PE" and not is_dma:
                    self.pe_pending.append(e)
        ents = [self._note_write(ap, sem, cntv, engine == "PE" and not is_dma)
                for ap in writes]
        if engine == "PE" and not is_dma:
            if cntv is None:
                self.pe_pending.extend(ents)
            else:
                for e in self.pe_pending:
                    if e[5] is None:
                        e[5] = cntv
                self.pe_pending = []
                for e in ents:
                    e[5] = cntv
        if self.role == engine:
            self.w(*[(s, v) for (s, v) in deps if v])
            ins = emit_fn()
            if is_dma:
                ins.then_inc(self.sem[sem], 16)
            elif engine == "PE":
                if cntv is not None:
                    ins.then_inc(self.sem[sem], 1)
            else:
                ins.then_inc(self.sem[sem], 1)
        return cntv if cntv is not None else self.cnt.get(sem, 0)

    def w(self, *waits):
        for (s, v) in waits:
            if v <= 0:
                continue
            if self.hiw.get(s, -1) >= v:
                continue
            self.eng.wait_ge(self.sem[s], v)
            self.hiw[s] = v

    def now(self):
        return [("mm", self.cnt["mm"]), ("ev", self.cnt["ev"]),
                ("dv", self.cnt["dv"])]

    def dma(self, q, out, in_, sem, waits=()):
        return self._emit_op(q, [in_], [out], waits, sem,
                             lambda: self.eng.dma_start(out=out, in_=in_),
                             is_dma=True)

    def ldw(self, wt, waits=()):
        deps = list(waits)
        self._dep_read(wt, deps)
        self.pe_pending.append(self._note_read(wt, "mm", None))
        if self.role == "PE":
            self.w(*[(s, v) for (s, v) in deps if v])
            self.eng.ldweights(wt)

    def mm(self, out, lhsT, rhs, start, stop, mark=None, waits=()):
        return self._emit_op(
            "PE", [lhsT, rhs], [out], waits, mark,
            lambda: self.eng.matmul(out, lhsT, rhs, start=start, stop=stop))

    def tr(self, out, in_, ident, mark=None, waits=()):
        return self._emit_op(
            "PE", [in_, ident], [out], waits, mark,
            lambda: self.eng.transpose(out, in_, ident))

    def act(self, out, in_, func=AF.Copy, bias=None, scale=None, mark=None,
            waits=()):
        kw = {}
        if bias is not None:
            kw["bias"] = bias
        if scale is not None:
            kw["scale"] = scale
        return self._emit_op(
            "ACT", [in_, bias if hasattr(bias, "tensor") else None], [out],
            waits, mark, lambda: self.eng.activation(out, in_, func, **kw))

    def amul(self, out, in_, mul, mark=None, waits=()):
        return self._emit_op(
            "ACT", [in_, mul if hasattr(mul, "tensor") else None], [out],
            waits, mark, lambda: self.eng.mul(out, in_, mul))

    def aadd(self, out, in_, add, mark=None, waits=()):
        return self._emit_op(
            "ACT", [in_, add if hasattr(add, "tensor") else None], [out],
            waits, mark, lambda: self.eng.add(out, in_, add))

    def tt(self, out, a, b, op, mark=None, waits=()):
        return self._emit_op(
            "DVE", [a, b], [out], waits, mark,
            lambda: self.eng.tensor_tensor(out, a, b, op))

    def ts(self, out, in0, s1, op0, mark=None, waits=()):
        return self._emit_op(
            "DVE", [in0, s1 if hasattr(s1, "tensor") else None], [out],
            waits, mark,
            lambda: self.eng.tensor_scalar(out, in0, s1, None, op0))

    def red(self, out, in_, op, mark=None, waits=()):
        return self._emit_op(
            "DVE", [in_], [out], waits, mark,
            lambda: self.eng.tensor_reduce(out, in_, AX.X, op))

    def scan(self, out, d0, d1, init, op0, op1, mark=None, waits=()):
        return self._emit_op(
            "DVE", [d0, d1], [out], waits, mark,
            lambda: self.eng.tensor_tensor_scan(out, d0, d1, init, op0, op1))

    def recip(self, out, in_, mark=None, waits=()):
        return self._emit_op(
            "DVE", [in_], [out], waits, mark,
            lambda: self.eng.reciprocal(out, in_))

    def sel(self, out, mask, on_true, on_false, mark=None, waits=()):
        return self._emit_op(
            "DVE", [mask, on_true, on_false], [out], waits, mark,
            lambda: self.eng.select(out, mask, on_true, on_false, add_drain=True))

    def vmemset(self, ap, val, mark=None, waits=()):
        return self._emit_op(
            "DVE", [], [ap], waits, mark,
            lambda: self.eng.memset(ap, val))

    def c(self, sem):
        return self.cnt[sem]


def run_phase(ctx, fn, name=""):
    start = dict(ctx.cnt)
    final = None
    for role in ROLES:
        ctx.cnt = dict(start)
        ctx.hiw = {}
        ctx.t_clear()
        ctx.dma_owner = {}
        ctx.role = role

        def body(eng):
            ctx.eng = eng
            fn(ctx)
            own = {"PE": ["mm"], "ACT": ["ev"], "DVE": ["dv"],
                   "SP": [], "POOL": []}[ctx.role]
            own = own + [s for (s, r) in sorted(ctx.dma_owner.items())
                         if r == ctx.role]
            for s in own:
                if ctx.cnt[s]:
                    eng.wait_ge(ctx.sem[s], ctx.cnt[s])
            tgt = ctx.cnt["bar"] + 5
            eng.nop().then_inc(ctx.sem["bar"], 1)
            eng.wait_ge(ctx.sem["bar"], tgt)
            ctx.cnt["bar"] = tgt

        getattr(ctx.block, {"SP": "sync", "POOL": "gpsimd", "PE": "tensor",
                            "ACT": "scalar", "DVE": "vector"}[role])(body)
        if final is None:
            final = dict(ctx.cnt)
        else:
            assert final == ctx.cnt, f"role divergence in {name}"
    ctx.cnt = final


def hdl(t):
    return t.tensor if hasattr(t, "tensor") else t


def ap_cols_rep(sb, col0, ncols, rep):
    h = hdl(sb)
    return bass.AP(h, col0, [[h.shape[1], 128], [1, ncols], [0, rep]])


def ap_stride(sb, off, step, count):
    h = hdl(sb)
    return bass.AP(h, off, [[h.shape[1], 128], [step, count]])


def wload(ctx, q, dst_sb, dst_free0, src, Ka, j, sem, k0=0, k1=None):
    if k1 is None:
        k1 = Ka
    s0 = j * Ka * 128 + k0 * 128
    return ctx.dma(q, dst_sb[:, dst_free0 + 0:dst_free0 + (k1 - k0) * 128],
                   src[:, s0:s0 + (k1 - k0) * 128], sem)


def emit_ln(ctx, b, xsrc, c0, ncols):
    ps, rows = b.ps, b.rows
    s0, s1, c_inv, c_mean = getattr(b, "lnb", (0, 512, 1024, 1536))
    ldsem = getattr(b, "ldsem", "ldA")
    n = ncols
    blk = c0 // 512
    ctx.dma("SP", b.xblk[:, 0:FC * n], xsrc[:, blk * 8192:blk * 8192 + FC * n],
            ldsem)
    ctx.act(b.st16[:, :FC * n], b.xblk[:, :FC * n], AF.Square)
    for a in range(FC):
        ctx.mm(ps[0:1, s1:s1 + n], b.ones_col[:, :],
               b.st16[:, a * n:(a + 1) * n],
               start=(a == 0), stop=(a == FC - 1),
               mark=("mm" if a == FC - 1 else None))
    ctx.act(b.st16[:, :FC * n], b.xblk[:, :FC * n], AF.Copy)
    for a in range(FC):
        ctx.mm(ps[0:1, s0:s0 + n], b.ones_col[:, :], b.st16[:, a * n:(a + 1) * n],
               start=(a == 0), stop=(a == FC - 1),
               mark=("mm" if a == FC - 1 else None))
    inv_h = 1.0 / H
    ctx.amul(rows[0:1, 0:n], ps[0:1, s0:s0 + n], inv_h)
    ctx.amul(rows[0:1, n:2 * n], ps[0:1, s1:s1 + n], inv_h)
    ctx.tt(rows[0:1, 2 * n:3 * n], rows[0:1, 0:n], rows[0:1, 0:n], ALU.mult)
    ctx.tt(rows[0:1, n:2 * n], rows[0:1, n:2 * n], rows[0:1, 2 * n:3 * n],
           ALU.subtract)
    ctx.act(rows[0:1, 2 * n:3 * n], rows[0:1, n:2 * n], AF.Ln, bias=LN_EPS)
    ctx.act(rows[0:1, n:2 * n], rows[0:1, 2 * n:3 * n], AF.Exp, scale=-0.5)
    ctx.tt(rows[0:1, 2 * n:3 * n], rows[0:1, 0:n], rows[0:1, n:2 * n],
           ALU.mult)
    ctx.mm(ps[:, c_inv:c_inv + n], b.ones_row[:, :], rows[0:1, n:2 * n],
           start=True, stop=True, mark="mm")
    ctx.mm(ps[:, c_mean:c_mean + n], b.ones_row[:, :], rows[0:1, 2 * n:3 * n],
           start=True, stop=True, mark="mm")
    dvv = 0
    for a in range(FC):
        ctx.tt(b.tmp[:, 0:n], b.xblk[:, a * n:(a + 1) * n],
               ps[:, c_inv:c_inv + n], ALU.mult)
        dvv = ctx.tt(b.xln[:, a * n:(a + 1) * n], b.tmp[:, 0:n],
                     ps[:, c_mean:c_mean + n], ALU.subtract)
    return dvv


def alloc_aam(ctx, es, cst):
    nc = ctx.nc
    sb = nc.sbuf_tensor
    b = type("B", (), {})()
    b.__dict__.update(cst.__dict__)
    b.xlnb = [es.enter_context(sb(f"aam_xln{i}", [128, FC * 512], BF16))
              for i in range(2)]
    b.wqf = es.enter_context(sb("aam_wqf", [128, 4 * 3 * H], BF16))
    b.Hst = es.enter_context(sb("aam_H", [128, NH * 128], F32))
    b.vtok = [es.enter_context(sb(f"aam_vtok{i}", [128, 512], BF16))
              for i in range(2)]
    b.osbp = [es.enter_context(sb(f"aam_osbp{i}", [128, 512], BF16))
              for i in range(2)]
    b.S = []
    for i in range(2):
        S = type("S", (), {})()
        for nm in ["qs", "fs", "gs", "As", "tmp", "tm2", "F1", "ks"]:
            setattr(S, nm, es.enter_context(
                sb(f"aam_{nm}{i}", [128, 512], F32)))
        for nm in ["qt", "kt", "kh", "khT", "sm", "vfm"]:
            setattr(S, nm, es.enter_context(
                sb(f"aam_{nm}{i}", [128, 512], BF16)))
        S.Hb = es.enter_context(sb(f"aam_Hb{i}", [128, 128], BF16))
        S.cm = es.enter_context(sb(f"aam_cm{i}", [128, 16], F32))
        b.S.append(S)
    b.ps = es.enter_context(nc.psum_tensor("aam_ps", [128, 7 * 512], F32))
    b.psb = es.enter_context(nc.psum_tensor("aam_psb", [128, 1024], BF16))
    return b


def aam_B(ctx, b, h, is_main):
    S = b.S[h % 2]
    ctx.act(S.tmp[:, :], S.fs[:, :], AF.Abs)
    ctx.act(S.tmp[:, :], S.tmp[:, :], AF.Exp, scale=-1.0)
    ctx.act(S.tm2[:, :], S.tmp[:, :], AF.Ln, bias=1.0)
    ctx.ts(S.gs[:, :], S.fs[:, :], 0.0, ALU.min)
    ctx.tt(S.gs[:, :], S.gs[:, :], S.tm2[:, :], ALU.subtract)
    for cc in range(NCH):
        ctx.scan(S.As[:, cc * CH:(cc + 1) * CH],
                 S.gs[:, cc * CH:(cc + 1) * CH],
                 S.gs[:, cc * CH:(cc + 1) * CH], 0.0, ALU.add, ALU.bypass)
    ACm, mcol = S.cm[:, 0:NCH], S.cm[:, NCH:2 * NCH]
    dec, expm = S.cm[:, 2 * NCH:3 * NCH], S.cm[:, 3 * NCH:4 * NCH]
    ctx.act(ACm, ap_stride(S.As, CH - 1, CH, NCH), AF.Copy)
    ctx.amul(mcol, ACm, 0.5)
    ctx.act(dec, ACm, AF.Exp)
    ctx.act(expm, mcol, AF.Exp, bias=SH)
    t3 = S.tmp[:, :].rearrange("p (a n) -> p a n", a=NCH)
    a3 = S.As[:, :].rearrange("p (a n) -> p a n", a=NCH)
    ctx.tt(t3, a3, ap_cols_rep(S.cm, NCH, NCH, CH), ALU.subtract)
    if is_main:
        ctx.act(S.F1[:, :], S.tmp[:, :], AF.Exp, bias=LNC - SH)
        ctx.tt(S.qt[:, :], S.qs[:, :], S.F1[:, :], ALU.mult)
    ctx.act(S.F1[:, :], S.tmp[:, :], AF.Exp, scale=-1.0, bias=-SH)
    ctx.act(S.tm2[:, :], S.gs[:, :], AF.Exp)
    ctx.act(S.ks[:, :], S.tm2[:, :], AF.Copy, scale=-1.0, bias=1.0)
    ctx.tt(S.kt[:, :], S.ks[:, :], S.F1[:, :], ALU.mult)
    k3 = S.kh[:, :].rearrange("p (a n) -> p a n", a=NCH)
    kt3 = S.kt[:, :].rearrange("p (a n) -> p a n", a=NCH)
    ctx.tt(k3, kt3, ap_cols_rep(S.cm, 3 * NCH, NCH, CH), ALU.mult)


def aam_C(ctx, b, d, h, blk, is_main):
    S = b.S[h % 2]
    hb = h % 2
    dec = S.cm[:, 2 * NCH:3 * NCH]
    expm = S.cm[:, 3 * NCH:4 * NCH]
    for cc in range(NCH):
        ctx.tr(b.psb[:, 512 + cc * 128:512 + (cc + 1) * 128],
               S.kh[:, cc * CH:(cc + 1) * CH], b.ident[:, :], mark="mm")
        ctx.act(S.khT[:, cc * 128:(cc + 1) * 128],
                b.psb[:, 512 + cc * 128:512 + (cc + 1) * 128], AF.Copy)
    Hh = b.Hst[:, h * 128:(h + 1) * 128]
    ao = (2 + hb) * 512
    st = hb * 512
    for cc in range(NCH):
        vt = b.vtok[h % 2][:, cc * 128:(cc + 1) * 128]
        if is_main:
            ctx.amul(S.Hb[:, :], Hh, expm[:, cc:cc + 1])
            ctx.ldw(S.Hb[:, :])
            ctx.mm(b.ps[:, ao:ao + 128], S.Hb[:, :],
                   S.qt[:, cc * CH:(cc + 1) * CH], start=True, stop=False)
            ctx.ldw(S.kt[:, cc * CH:(cc + 1) * CH])
            ctx.mm(b.ps[:, 4 * 512:4 * 512 + 128],
                   S.kt[:, cc * CH:(cc + 1) * CH],
                   S.qt[:, cc * CH:(cc + 1) * CH],
                   start=True, stop=True, mark="mm")
            ctx.tt(S.sm[:, cc * CH:(cc + 1) * CH],
                   b.ps[:, 4 * 512:4 * 512 + 128],
                   b.maskaa[:, :], ALU.mult)
            ctx.ldw(vt)
            ctx.mm(b.ps[:, ao:ao + 128], vt,
                   S.sm[:, cc * CH:(cc + 1) * CH],
                   start=False, stop=True, mark="mm")
            ctx.act(b.osbp[hb][:, cc * CH:(cc + 1) * CH],
                    b.ps[:, ao:ao + 128], AF.Copy)
        ctx.ldw(S.khT[:, cc * 128:(cc + 1) * 128])
        ctx.mm(b.ps[:, st:st + 128],
               S.khT[:, cc * 128:(cc + 1) * 128], vt,
               start=True, stop=True, mark="mm")
        ctx.amul(Hh, Hh, dec[:, cc:cc + 1])
        ctx.tt(Hh, Hh, b.ps[:, st:st + 128], ALU.add)
    if is_main:
        ctx.dma("SP", d.osb_t[:, (blk - 3) * 8192 + h * 512:
                              (blk - 3) * 8192 + (h + 1) * 512],
                b.osbp[hb][:, :], "stA" if hb == 0 else "stB")


def emit_aam(ctx, b, d):
    ctx.vmemset(b.Hst[:, :], 0.0)
    pbc = [0]

    def pb():
        pbc[0] ^= 1
        return 5 + pbc[0]
    b.pb = pb

    def load_w(hp):
        wslot = (hp % 2) * (6 * H)
        for hh in range(2):
            h = hp * 2 + hh
            wload(ctx, "POOL", b.wqf, wslot + hh * 2048, d.wa_q, FC, h,
                  "wlA" if hp % 2 == 0 else "wlB")
            wload(ctx, "POOL", b.wqf, wslot + 4096 + hh * 2048, d.wa_f, FC, h,
                  "wlC" if hp % 2 == 0 else "wlD")
            wload(ctx, "POOL", b.wqf, wslot + 8192 + hh * 2048, d.wa_i, FC, h,
                  "wlE" if hp % 2 == 0 else "wlF")

    ctx.dma("SP", b.xlnb[0][:, :], d.xln_t[:, 0:8192], "ldA")
    for hp in range(NH // 2):
        load_w(hp)
        for blk in range(8):
            is_main = blk >= 3
            xs = blk % 2
            for hh in range(2):
                aam_A2(ctx, b, hp * 2 + hh, hp, is_main, xs)
            if blk + 1 < 8:
                nxt = blk + 1
                ctx.dma("SP", b.xlnb[nxt % 2][:, :],
                        d.xln_t[:, nxt * 8192:(nxt + 1) * 8192],
                        "ldA" if nxt % 2 == 0 else "ldB")
            elif hp + 1 < NH // 2:
                ctx.dma("SP", b.xlnb[0][:, :], d.xln_t[:, 0:8192], "ldA")
            for hh in range(2):
                h = hp * 2 + hh
                aam_B(ctx, b, h, is_main)
                aam_C(ctx, b, d, h, blk, is_main)


def aam_A2(ctx, b, h, hp, is_main, xs):
    S = b.S[h % 2]
    wslot = (hp % 2) * (6 * H)
    hoff = (h % 2) * 2048

    def proj(woff, dst_ev, func):
        pbk = b.pb()
        for i in range(FC):
            wt = b.wqf[:, wslot + woff + i * 128:
                       wslot + woff + (i + 1) * 128]
            ctx.ldw(wt)
            ctx.mm(b.ps[:, pbk * 512:(pbk + 1) * 512], wt,
                   b.xlnb[xs][:, i * 512:(i + 1) * 512],
                   start=(i == 0), stop=(i == FC - 1),
                   mark=("mm" if i == FC - 1 else None))
        ctx.act(dst_ev, b.ps[:, pbk * 512:(pbk + 1) * 512], func)

    if is_main:
        proj(hoff, S.qs[:, :], AF.Silu)
    proj(4096 + hoff, S.fs[:, :], AF.Copy)
    proj(8192 + hoff, S.vfm[:, :], AF.Copy)
    for cc in range(NCH):
        ctx.tr(b.psb[:, cc * 128:(cc + 1) * 128],
               S.vfm[:, cc * 128:(cc + 1) * 128], b.ident[:, :], mark="mm")
        ctx.act(b.vtok[h % 2][:, cc * 128:(cc + 1) * 128],
                b.psb[:, cc * 128:(cc + 1) * 128], AF.Copy)


def alloc_awo(ctx, es, cst):
    nc = ctx.nc
    sb = nc.sbuf_tensor
    b = type("B", (), {})()
    b.__dict__.update(cst.__dict__)
    b.osb = [es.enter_context(sb(f"awo_osb{i}", [128, FC * 512], BF16))
             for i in range(2)]
    b.st16 = es.enter_context(sb("awo_st16", [128, FC * 512], BF16))
    b.wo = es.enter_context(sb("awo_wo", [128, 2 * H], BF16))
    b.xr = [es.enter_context(sb(f"awo_xr{i}", [128, 512], F32))
            for i in range(2)]
    b.ob = [es.enter_context(sb(f"awo_ob{i}", [128, 512], F32))
            for i in range(2)]
    b.rows = es.enter_context(sb("awo_rows", [1, 2 * 512], F32))
    b.ps = es.enter_context(nc.psum_tensor("awo_ps", [128, 7 * 512], F32))
    return b


def emit_awo(ctx, b, d):
    ctx.dma("SP", b.osb[0][:, :], d.osb_t[:, 0:8192], "ldA")
    for blk in range(5):
        par = blk % 2
        if blk + 1 < 5:
            ctx.dma("SP", b.osb[(blk + 1) % 2][:, :],
                    d.osb_t[:, (blk + 1) * 8192:(blk + 2) * 8192],
                    "ldA" if (blk + 1) % 2 == 0 else "ldB")
        osb = b.osb[par]
        ctx.act(b.st16[:, :], osb[:, :], AF.Square)
        for a in range(FC):
            ctx.mm(b.ps[0:1, 0:512], b.ones_col[:, :],
                   b.st16[:, a * 512:(a + 1) * 512],
                   start=(a == 0), stop=(a == FC - 1),
                   mark=("mm" if a == FC - 1 else None))
        ctx.amul(b.rows[0:1, 0:512], b.ps[0:1, 0:512], 1.0 / H)
        ctx.act(b.rows[0:1, 512:1024], b.rows[0:1, 0:512], AF.Ln, bias=AA_EPS)
        ctx.act(b.rows[0:1, 0:512], b.rows[0:1, 512:1024], AF.Exp, scale=-0.5)
        ctx.mm(b.ps[:, 4 * 512:5 * 512], b.ones_row[:, :],
               b.rows[0:1, 0:512], start=True, stop=True, mark="mm")
        o3 = osb[:, :].rearrange("p (a n) -> p a n", a=FC)
        ctx.tt(o3, o3, bc_mid(b.ps, 4 * 512, FC, 512), ALU.mult)
        ctx.tt(o3, o3, ap_cols_rep(b.gat, 0, FC, 512), ALU.mult)
        for j in range(FC):
            jp = j % 2
            wload(ctx, "POOL", b.wo, jp * H, d.wa_o, FC, j,
                  "wlG" if jp == 0 else "wlH")
            pbk = 5 + jp
            for i in range(FC):
                wt = b.wo[:, jp * H + i * 128: jp * H + (i + 1) * 128]
                ctx.ldw(wt)
                ctx.mm(b.ps[:, pbk * 512:(pbk + 1) * 512], wt,
                       osb[:, i * 512:(i + 1) * 512],
                       start=(i == 0), stop=(i == FC - 1),
                       mark=("mm" if i == FC - 1 else None))
            ctx.dma("ACT", b.xr[jp][:, :],
                    d.x_t[:, (blk + 3) * 8192 + j * 512:
                          (blk + 3) * 8192 + (j + 1) * 512],
                    "ldC" if jp == 0 else "ldD")
            ctx.tt(b.ob[jp][:, :], b.xr[jp][:, :],
                   b.ps[:, pbk * 512:(pbk + 1) * 512], ALU.add)
            ctx.dma("SP", d.x1_t[:, blk * 8192 + j * 512:
                                 blk * 8192 + (j + 1) * 512],
                    b.ob[jp][:, :], "stA" if jp == 0 else "stB")


def ap3_m(cm):
    h = hdl(cm)
    return bass.AP(h, NCH, [[h.shape[1], 128], [1, NCH], [0, CH]])


def bc_mid(t, off, rep, n):
    h = hdl(t)
    return bass.AP(h, off, [[h.shape[1], 128], [0, rep], [1, n]])


def alloc_ffn_ln(ctx, es, cst, tag):
    nc = ctx.nc
    sb = nc.sbuf_tensor
    com = type("C", (), {})()
    com.ps = es.enter_context(nc.psum_tensor(tag + "_ps", [128, 7 * 512], F32))
    com.psb = es.enter_context(nc.psum_tensor(tag + "_psb", [128, 1024], BF16))
    com.tmpall = es.enter_context(sb(tag + "_tmpa", [128, FC * 512], F32))
    sets = []
    for par in range(2):
        b = type("B", (), {})()
        b.__dict__.update(cst.__dict__)
        t = tag + str(par)
        b.xblk = es.enter_context(sb(t + "_xblk", [128, FC * 512], F32))
        b.st16 = es.enter_context(sb(t + "_st16", [128, FC * 512], BF16))
        b.xln = es.enter_context(sb(t + "_xln", [128, FC * 512], BF16))
        b.rows = es.enter_context(sb(t + "_rows", [1, 3 * 512], F32))
        b.lgb = es.enter_context(sb(t + "_lgb", [16, 512], BF16))
        b.et64 = es.enter_context(sb(t + "_et64", [128, 64], F32))
        b.wk64 = es.enter_context(sb(t + "_wk64", [128, 64], F32))
        b.eq64 = es.enter_context(sb(t + "_eq64", [128, 64], F32))
        b.mxa = es.enter_context(sb(t + "_mxa", [128, 24], F32))
        b.cmb64 = es.enter_context(sb(t + "_cmb64", [128, 64], BF16))
        b.ps = com.ps
        b.psb = com.psb
        b.tmpall = com.tmpall
        b.lnb = (0, 512, 4 * 512, 5 * 512) if par == 0 else \
                (1024, 1536, 4 * 512, 5 * 512)
        b.ldsem = "ldA" if par == 0 else "ldB"
        b.stsem = "wlH" if par == 0 else "wlI"
        b.rtb = 6 * 512
        sets.append(b)
    return sets


def ln_s1(ctx, B, src, blk):
    s0, s1 = B.lnb[0], B.lnb[1]
    n = 512
    ctx.dma("SP", B.xblk[:, 0:FC * n],
            src[:, blk * 8192:blk * 8192 + FC * n], B.ldsem)
    ctx.act(B.st16[:, :FC * n], B.xblk[:, :FC * n], AF.Square)
    for a in range(FC):
        ctx.mm(B.ps[0:1, s1:s1 + n], B.ones_col[:, :],
               B.st16[:, a * n:(a + 1) * n],
               start=(a == 0), stop=(a == FC - 1),
               mark=("mm" if a == FC - 1 else None))
    ctx.act(B.st16[:, :FC * n], B.xblk[:, :FC * n], AF.Copy)
    for a in range(FC):
        ctx.mm(B.ps[0:1, s0:s0 + n], B.ones_col[:, :],
               B.st16[:, a * n:(a + 1) * n],
               start=(a == 0), stop=(a == FC - 1),
               mark=("mm" if a == FC - 1 else None))


def ln_s2(ctx, B):
    s0, s1, c_inv, c_mean = B.lnb
    rows = B.rows
    n = 512
    ctx.amul(rows[0:1, 0:n], B.ps[0:1, s0:s0 + n], 1.0 / H)
    ctx.amul(rows[0:1, n:2 * n], B.ps[0:1, s1:s1 + n], 1.0 / H)
    ctx.tt(rows[0:1, 2 * n:3 * n], rows[0:1, 0:n], rows[0:1, 0:n], ALU.mult)
    ctx.tt(rows[0:1, n:2 * n], rows[0:1, n:2 * n], rows[0:1, 2 * n:3 * n],
           ALU.subtract)
    ctx.act(rows[0:1, 2 * n:3 * n], rows[0:1, n:2 * n], AF.Ln, bias=LN_EPS)
    ctx.act(rows[0:1, n:2 * n], rows[0:1, 2 * n:3 * n], AF.Exp, scale=-0.5)
    ctx.tt(rows[0:1, 2 * n:3 * n], rows[0:1, 0:n], rows[0:1, n:2 * n],
           ALU.mult)
    ctx.mm(B.ps[:, c_inv:c_inv + n], B.ones_row[:, :], rows[0:1, n:2 * n],
           start=True, stop=True, mark="mm")
    ctx.mm(B.ps[:, c_mean:c_mean + n], B.ones_row[:, :],
           rows[0:1, 2 * n:3 * n], start=True, stop=True, mark="mm")


def ln_s3(ctx, B):
    c_inv, c_mean = B.lnb[2], B.lnb[3]
    x3 = B.xblk[:, :].rearrange("p (a n) -> p a n", a=FC)
    t3 = B.tmpall[:, :].rearrange("p (a n) -> p a n", a=FC)
    o3 = B.xln[:, :].rearrange("p (a n) -> p a n", a=FC)
    ctx.tt(t3, x3, bc_mid(B.ps, c_inv, FC, 512), ALU.mult)
    ctx.tt(o3, t3, bc_mid(B.ps, c_mean, FC, 512), ALU.subtract)


def ln_s4(ctx, B, dst, blk, router):
    ctx.dma("POOL", dst[:, blk * 8192:(blk + 1) * 8192],
            B.xln[:, :FC * 512], B.stsem)
    if not router:
        return
    for i in range(FC):
        ctx.mm(B.ps[0:16, B.rtb:B.rtb + 512], B.rw[:, i * 16:(i + 1) * 16],
               B.xln[:, i * 512:(i + 1) * 512],
               start=(i == 0), stop=(i == FC - 1),
               mark=("mm" if i == FC - 1 else None))
    ctx.act(B.lgb[:, :], B.ps[0:16, B.rtb:B.rtb + 512], AF.Copy)


def ln_s5(ctx, B, c0):
    for ts4 in range(4):
        ctx.tr(B.psb[0:128, ts4 * 16:(ts4 + 1) * 16],
               B.lgb[:, ts4 * 128:(ts4 + 1) * 128],
               B.ident[0:16, 0:16], mark="mm")
    ctx.act(B.et64[:, :], B.psb[0:128, 0:64], AF.Exp)
    ctx.act(B.wk64[:, :], B.et64[:, :], AF.Copy)
    w3 = B.wk64[:, :].rearrange("p (a n) -> p a n", a=4)
    e3 = B.eq64[:, :].rearrange("p (a n) -> p a n", a=4)
    for it in range(4):
        ctx.red(B.mxa[:, it * 4:(it + 1) * 4], w3, ALU.max)
        if it < 3:
            ctx.tt(e3, w3, ap_cols_rep(B.mxa, it * 4, 4, 16), ALU.is_equal)
            ctx.ts(B.eq64[:, :], B.eq64[:, :], -BIGNEG, ALU.mult)
            ctx.tt(B.wk64[:, :], B.wk64[:, :], B.eq64[:, :], ALU.add)
    t3 = B.et64[:, :].rearrange("p (a n) -> p a n", a=4)
    ctx.tt(e3, t3, ap_cols_rep(B.mxa, 12, 4, 16), ALU.is_ge)
    ctx.tt(B.eq64[:, :], B.et64[:, :], B.eq64[:, :], ALU.mult)
    ctx.red(B.mxa[:, 16:20], e3, ALU.add)
    ctx.ts(B.mxa[:, 16:20], B.mxa[:, 16:20], 1e-9, ALU.max)
    ctx.recip(B.mxa[:, 20:24], B.mxa[:, 16:20])
    c3 = B.cmb64[:, :].rearrange("p (a n) -> p a n", a=4)
    ctx.tt(c3, e3, ap_cols_rep(B.mxa, 20, 4, 16), ALU.mult)
    for ts4 in range(4):
        ctx.tr(B.psb[0:16, 512 + ts4 * 128:512 + (ts4 + 1) * 128],
               B.cmb64[:, ts4 * 16:(ts4 + 1) * 16],
               B.ident[:, :], mark="mm")
    ctx.act(B.comb_t[0:16, c0:c0 + 512], B.psb[0:16, 512:1024], AF.Copy)


def emit_ffn_ln(ctx, sets, d, src, n_tok, router=True, dst=None):
    if dst is None:
        dst = d.xln_t
    nblk = n_tok // 512
    ln_s1(ctx, sets[0], src, 0)
    if nblk > 1:
        ln_s1(ctx, sets[1], src, 1)
    for blk in range(nblk):
        B = sets[blk % 2]
        ln_s2(ctx, B)
        ln_s3(ctx, B)
        if blk + 2 < nblk:
            ln_s1(ctx, sets[blk % 2], src, blk + 2)
        ln_s4(ctx, B, dst, blk, router)
        if router:
            ln_s5(ctx, B, blk * 512)


def alloc_ffn_moe(ctx, es, cst, tag):
    nc = ctx.nc
    b = type("B", (), {})()
    b.__dict__.update(cst.__dict__)
    sb = nc.sbuf_tensor
    b.xg = es.enter_context(sb(tag + "_xg", [128, FC * 1024], BF16))
    b.acc = es.enter_context(sb(tag + "_acc", [128, FC * 1024], F32))
    b.big = es.enter_context(sb(tag + "_big", [128, JSH * 512], BF16))
    b.ws = es.enter_context(sb(tag + "_ws", [128, 4 * 2048], BF16))
    b.wsd = es.enter_context(sb(tag + "_wsd", [128, 2 * 22 * 128], BF16))
    b.cbc = es.enter_context(sb(tag + "_cbc", [128, 1024], BF16))
    b.sg1 = es.enter_context(sb(tag + "_sg1", [128, 4 * 512], BF16))
    b.sgb = es.enter_context(sb(tag + "_sgb", [128, 512], BF16))
    b.sgw = es.enter_context(sb(tag + "_sgw", [128, 16], BF16))
    b.tmp = es.enter_context(sb(tag + "_tmpm", [128, 1024], F32))
    b.ob = es.enter_context(sb(tag + "_ob", [128, 1024], F32))
    b.rows = es.enter_context(sb(tag + "_rowm", [1, 3 * 512], F32))
    b.ps = es.enter_context(nc.psum_tensor(tag + "_psm", [128, 8 * 512], F32))
    return b


def emit_ffn_moe(ctx, b, d, src, dst, n_tok):
    groups = []
    t0 = 0
    while t0 < n_tok:
        tg = min(1024, n_tok - t0)
        groups.append((t0, tg))
        t0 += tg
    ctx.dma("POOL", b.sgw[:, :], d.f_sgw[:, :], "wlG")

    def load_xg(gi):
        if gi >= len(groups):
            return
        g0n, tgn = groups[gi]
        for t2 in range(tgn // 512):
            ctx.dma("SP", b.xg[:, t2 * 8192:(t2 + 1) * 8192],
                    d.xln_t[:, (g0n // 512 + t2) * 8192:
                            (g0n // 512 + t2 + 1) * 8192],
                    "ldB" if t2 == 0 else "ldA")

    load_xg(0)
    for gi, (g0, tg) in enumerate(groups):
        nb = tg // 512
        for e in range(E):
            ctx.ldw(b.emat[:, e * 128:(e + 1) * 128])
            for t2 in range(nb):
                ctx.mm(b.ps[:, t2 * 512:(t2 + 1) * 512],
                       b.emat[:, e * 128:(e + 1) * 128],
                       b.comb_t[0:16, g0 + t2 * 512:g0 + (t2 + 1) * 512],
                       start=True, stop=True,
                       mark=("mm" if t2 == nb - 1 else None))
            ctx.act(b.cbc[:, 0:tg], b.ps[:, 0:tg], AF.Copy)
            for jj in range(JM):
                p = jj % 2
                wload(ctx, "POOL", b.ws, p * 2048, d.f_wg, FC, e * JM + jj,
                      "wlA" if p == 0 else "wlB")
                for i in range(FC):
                    ctx.ldw(b.ws[:, p * 2048 + i * 128: p * 2048 + (i + 1) * 128])
                    for t2 in range(nb):
                        ctx.mm(b.ps[:, (p * 2 + t2) * 512:(p * 2 + t2 + 1) * 512],
                               b.ws[:, p * 2048 + i * 128: p * 2048 + (i + 1) * 128],
                               b.xg[:, t2 * 8192 + i * 512: t2 * 8192 + (i + 1) * 512],
                               start=(i == 0), stop=(i == FC - 1),
                               mark=("mm" if (i == FC - 1 and t2 == nb - 1) else None))
                for t2 in range(nb):
                    ctx.act(b.sg1[:, (p * 2 + t2) * 512:(p * 2 + t2 + 1) * 512],
                            b.ps[:, (p * 2 + t2) * 512:(p * 2 + t2 + 1) * 512],
                            AF.Silu)
                wload(ctx, "POOL", b.ws, (2 + p) * 2048, d.f_wu, FC,
                      e * JM + jj, "wlC" if p == 0 else "wlD")
                for i in range(FC):
                    ctx.ldw(b.ws[:, (2 + p) * 2048 + i * 128:
                                 (2 + p) * 2048 + (i + 1) * 128])
                    for t2 in range(nb):
                        ctx.mm(b.ps[:, (4 + p * 2 + t2) * 512:
                                    (4 + p * 2 + t2 + 1) * 512],
                               b.ws[:, (2 + p) * 2048 + i * 128:
                                    (2 + p) * 2048 + (i + 1) * 128],
                               b.xg[:, t2 * 8192 + i * 512: t2 * 8192 + (i + 1) * 512],
                               start=(i == 0), stop=(i == FC - 1),
                               mark=("mm" if (i == FC - 1 and t2 == nb - 1) else None))
                for t2 in range(nb):
                    ctx.tt(b.tmp[:, t2 * 512:(t2 + 1) * 512],
                           b.sg1[:, (p * 2 + t2) * 512:(p * 2 + t2 + 1) * 512],
                           b.ps[:, (4 + p * 2 + t2) * 512:
                                (4 + p * 2 + t2 + 1) * 512], ALU.mult)
                    ctx.tt(b.big[:, t2 * 5632 + jj * 512: t2 * 5632 + (jj + 1) * 512],
                           b.tmp[:, t2 * 512:(t2 + 1) * 512],
                           b.cbc[:, t2 * 512:(t2 + 1) * 512], ALU.mult)
            for j2 in range(FC):
                p = j2 % 2
                wload(ctx, "POOL", b.ws, p * 2048, d.f_wd, JM, e * FC + j2,
                      "wlE" if p == 0 else "wlF")
                for k in range(JM):
                    ctx.ldw(b.ws[:, p * 2048 + k * 128: p * 2048 + (k + 1) * 128])
                    for t2 in range(nb):
                        ctx.mm(b.ps[:, (p * 2 + t2) * 512:(p * 2 + t2 + 1) * 512],
                               b.ws[:, p * 2048 + k * 128: p * 2048 + (k + 1) * 128],
                               b.big[:, t2 * 5632 + k * 512: t2 * 5632 + (k + 1) * 512],
                               start=(k == 0), stop=(k == JM - 1),
                               mark=("mm" if (k == JM - 1 and t2 == nb - 1) else None))
                for t2 in range(nb):
                    if e == 0:
                        ctx.act(b.acc[:, t2 * 8192 + j2 * 512: t2 * 8192 + (j2 + 1) * 512],
                                b.ps[:, (p * 2 + t2) * 512:(p * 2 + t2 + 1) * 512],
                                AF.Copy)
                    else:
                        ctx.tt(b.acc[:, t2 * 8192 + j2 * 512: t2 * 8192 + (j2 + 1) * 512],
                               b.acc[:, t2 * 8192 + j2 * 512: t2 * 8192 + (j2 + 1) * 512],
                               b.ps[:, (p * 2 + t2) * 512:(p * 2 + t2 + 1) * 512],
                               ALU.add)
        for t2 in range(nb):
            for i in range(FC):
                ctx.mm(b.ps[0:1, 6 * 512:6 * 512 + 512], b.sgw[:, i:i + 1],
                       b.xg[:, t2 * 8192 + i * 512: t2 * 8192 + (i + 1) * 512],
                       start=(i == 0), stop=(i == FC - 1),
                       mark=("mm" if i == FC - 1 else None))
            ctx.act(b.rows[0:1, 0:512], b.ps[0:1, 6 * 512:6 * 512 + 512],
                    AF.Exp, scale=-1.0)
            ctx.ts(b.rows[0:1, 512:1024], b.rows[0:1, 0:512], 1.0, ALU.add)
            ctx.recip(b.rows[0:1, 0:512], b.rows[0:1, 512:1024])
            ctx.mm(b.ps[:, 6 * 512:6 * 512 + 512], b.ones_row[:, :],
                   b.rows[0:1, 0:512], start=True, stop=True, mark="mm")
            ctx.act(b.sgb[:, :], b.ps[:, 6 * 512:6 * 512 + 512], AF.Copy)
            for jj in range(JSH):
                p = jj % 2
                wload(ctx, "POOL", b.ws, p * 2048, d.f_sg, FC, jj,
                      "wlA" if p == 0 else "wlB")
                for i in range(FC):
                    ctx.ldw(b.ws[:, p * 2048 + i * 128: p * 2048 + (i + 1) * 128])
                    ctx.mm(b.ps[:, p * 512:(p + 1) * 512],
                           b.ws[:, p * 2048 + i * 128: p * 2048 + (i + 1) * 128],
                           b.xg[:, t2 * 8192 + i * 512: t2 * 8192 + (i + 1) * 512],
                           start=(i == 0), stop=(i == FC - 1),
                           mark=("mm" if i == FC - 1 else None))
                ctx.act(b.sg1[:, p * 512:(p + 1) * 512],
                        b.ps[:, p * 512:(p + 1) * 512], AF.Silu)
                wload(ctx, "POOL", b.ws, (2 + p) * 2048, d.f_su, FC, jj,
                      "wlC" if p == 0 else "wlD")
                for i in range(FC):
                    ctx.ldw(b.ws[:, (2 + p) * 2048 + i * 128:
                                 (2 + p) * 2048 + (i + 1) * 128])
                    ctx.mm(b.ps[:, (2 + p) * 512:(3 + p) * 512],
                           b.ws[:, (2 + p) * 2048 + i * 128:
                                (2 + p) * 2048 + (i + 1) * 128],
                           b.xg[:, t2 * 8192 + i * 512: t2 * 8192 + (i + 1) * 512],
                           start=(i == 0), stop=(i == FC - 1),
                           mark=("mm" if i == FC - 1 else None))
                ctx.tt(b.big[:, jj * 512:(jj + 1) * 512],
                       b.sg1[:, p * 512:(p + 1) * 512],
                       b.ps[:, (2 + p) * 512:(3 + p) * 512], ALU.mult)
            if t2 == nb - 1:
                load_xg(gi + 1)
            for j2 in range(FC):
                p = j2 % 2
                for hf in range(2):
                    k0, k1 = (0, 22) if hf == 0 else (22, JSH)
                    ph = (j2 * 2 + hf) % 2
                    wload(ctx, "POOL", b.wsd, ph * 22 * 128, d.f_sd, JSH, j2,
                          "wlE" if ph == 0 else "wlF", k0=k0, k1=k1)
                    for k in range(k0, k1):
                        ctx.ldw(b.wsd[:, ph * 22 * 128 + (k - k0) * 128:
                                      ph * 22 * 128 + (k - k0 + 1) * 128])
                        ctx.mm(b.ps[:, (4 + p) * 512:(5 + p) * 512],
                               b.wsd[:, ph * 22 * 128 + (k - k0) * 128:
                                     ph * 22 * 128 + (k - k0 + 1) * 128],
                               b.big[:, k * 512:(k + 1) * 512],
                               start=(k == 0), stop=(k == JSH - 1),
                               mark=("mm" if k == JSH - 1 else None))
                ctx.tt(b.tmp[:, 0:512], b.ps[:, (4 + p) * 512:(5 + p) * 512],
                       b.sgb[:, :], ALU.mult)
                ctx.tt(b.acc[:, t2 * 8192 + j2 * 512: t2 * 8192 + (j2 + 1) * 512],
                       b.acc[:, t2 * 8192 + j2 * 512: t2 * 8192 + (j2 + 1) * 512],
                       b.tmp[:, 0:512], ALU.add)
        for j2 in range(FC):
            buf = b.ob if j2 % 2 == 0 else b.tmp
            for t2 in range(tg // 512):
                ctx.dma("ACT", buf[:, t2 * 512:(t2 + 1) * 512],
                        src[:, (g0 // 512 + t2) * 8192 + j2 * 512:
                            (g0 // 512 + t2) * 8192 + (j2 + 1) * 512],
                        "ldC" if j2 % 2 == 0 else "ldD")
                ctx.tt(buf[:, t2 * 512:(t2 + 1) * 512],
                       buf[:, t2 * 512:(t2 + 1) * 512],
                       b.acc[:, t2 * 8192 + j2 * 512:
                             t2 * 8192 + (j2 + 1) * 512], ALU.add)
                ctx.dma("SP", dst[:, (g0 // 512 + t2) * 8192 + j2 * 512:
                                  (g0 // 512 + t2) * 8192 + (j2 + 1) * 512],
                        buf[:, t2 * 512:(t2 + 1) * 512],
                        "stA" if j2 % 2 == 0 else "stB")


def alloc_ab(ctx, es, cst):
    nc = ctx.nc
    b = type("B", (), {})()
    b.__dict__.update(cst.__dict__)
    sb = nc.sbuf_tensor
    b.xblk = es.enter_context(sb("ab_xblk", [128, FC * 512], F32))
    b.xln = es.enter_context(sb("ab_xln", [128, FC * 512], BF16))
    b.q = es.enter_context(sb("ab_q", [128, FC * 512], BF16))
    b.st16 = b.q
    b.kring = es.enter_context(sb("ab_kring", [128, FC * 1024], BF16))
    b.vring = es.enter_context(sb("ab_vring", [128, 8 * H], BF16))
    b.osb = es.enter_context(sb("ab_osb", [128, NH * 512], BF16))
    b.ropec = es.enter_context(sb("ab_ropec", [128, T_MAIN], BF16))
    b.ropes = es.enter_context(sb("ab_ropes", [128, T_MAIN], BF16))
    b.pswap = es.enter_context(sb("ab_pswap", [128, 128], BF16))
    b.kdr = es.enter_context(sb("ab_kdr", [128, 512], BF16))
    b.bmask = es.enter_context(sb("ab_bmask", [128, 8 * 512], BF16))
    b.pt = es.enter_context(sb("ab_pt", [128, 2 * 512], BF16))
    b.rb = es.enter_context(sb("ab_rb", [128, 512], F32))
    b.t1 = es.enter_context(sb("ab_t1", [128, 512], F32))
    b.tmp = es.enter_context(sb("ab_tmp", [128, 512], F32))
    b.rows = es.enter_context(sb("ab_rows", [1, 3 * 512], F32))
    b.wsl = es.enter_context(sb("ab_wsl", [128, 4 * H], BF16))
    b.ob = es.enter_context(sb("ab_ob", [128, 512], F32))
    b.ps = es.enter_context(nc.psum_tensor("ab_ps", [128, 8 * 512], F32))
    return b


def emit_ab(ctx, b, d):
    ctx.dma("SP", b.ropec[:, :], d.ropec[:, :], "ldB")
    ctx.dma("SP", b.ropes[:, :], d.ropes[:, :], "ldC")
    ctx.dma("POOL", b.pswap[:, :], d.c_pswap[:, :], "wlI")
    ctx.dma("POOL", b.bmask[:, :], d.bmask[:, 8 * 512:16 * 512], "wlJ")

    def rope_apply(dst, psrc, swbank):
        ctx.act(b.kdr[:, :], psrc, AF.Copy)
        ctx.ldw(b.pswap[:, :])
        ctx.mm(b.ps[:, swbank * 512:(swbank + 1) * 512], b.pswap[:, :],
               b.kdr[:, :], start=True, stop=True, mark="mm")
        ctx.tt(b.t1[:, :], b.kdr[:, :], b.ropec[:, rc0:rc0 + 512], ALU.mult)
        ctx.tt(b.tmp[:, :], b.ps[:, swbank * 512:(swbank + 1) * 512],
               b.ropes[:, rc0:rc0 + 512], ALU.mult)
        ctx.tt(dst, b.t1[:, :], b.tmp[:, :], ALU.add)

    for pb in range(5):
        c0 = pb * 512
        rc0 = c0
        qb = pb - 1
        emit_ln(ctx, b, d.x2_t, c0, 512)
        rs = pb % 2
        for hh in range(NH):
            wload(ctx, "POOL", b.wsl, (hh % 2) * H, d.wb_k, FC, hh,
                  "wlA" if hh % 2 == 0 else "wlB")
            for i in range(FC):
                ctx.ldw(b.wsl[:, (hh % 2) * H + i * 128:(hh % 2) * H + (i + 1) * 128])
                ctx.mm(b.ps[:, 6 * 512:7 * 512],
                       b.wsl[:, (hh % 2) * H + i * 128:(hh % 2) * H + (i + 1) * 128],
                       b.xln[:, i * 512:(i + 1) * 512],
                       start=(i == 0), stop=(i == FC - 1),
                       mark=("mm" if i == FC - 1 else None))
            kd = b.kring[:, hh * 1024 + rs * 512: hh * 1024 + (rs + 1) * 512]
            rope_apply(kd, b.ps[:, 6 * 512:7 * 512], 5)
        for half in range(2):
            for i in range(FC):
                ctx.dma("POOL", b.wi_ab[:, (i % 2) * H:(i % 2) * H + 1024],
                        d.wb_v[i * 128:(i + 1) * 128,
                               half * 1024:(half + 1) * 1024],
                        "wlE" if i % 2 == 0 else "wlF")
                for ts4 in range(4):
                    xtile = b.xln[:, i * 512 + ts4 * 128:
                                  i * 512 + (ts4 + 1) * 128]
                    ctx.ldw(xtile)
                    for g2 in range(2):
                        bank = ts4 * 2 + g2
                        ctx.mm(b.ps[:, bank * 512:(bank + 1) * 512], xtile,
                               b.wi_ab[:, (i % 2) * H + g2 * 512:
                                       (i % 2) * H + (g2 + 1) * 512],
                               start=(i == 0), stop=(i == FC - 1),
                               mark=("mm" if (ts4 == 3 and g2 == 1) else None))
            for ts4 in range(4):
                for g2 in range(2):
                    bank = ts4 * 2 + g2
                    ng = half * 2 + g2
                    ctx.act(b.vring[:, (rs * 4 + ts4) * H + ng * 512:
                                    (rs * 4 + ts4) * H + (ng + 1) * 512],
                            b.ps[:, bank * 512:(bank + 1) * 512], AF.Copy)
        if pb == 0:
            continue
        for hh in range(NH):
            wload(ctx, "POOL", b.wsl, (2 + hh % 2) * H, d.wb_q, FC, hh,
                  "wlC" if hh % 2 == 0 else "wlD")
            for i in range(FC):
                ctx.ldw(b.wsl[:, (2 + hh % 2) * H + i * 128:
                              (2 + hh % 2) * H + (i + 1) * 128])
                ctx.mm(b.ps[:, 6 * 512:7 * 512],
                       b.wsl[:, (2 + hh % 2) * H + i * 128:
                             (2 + hh % 2) * H + (i + 1) * 128],
                       b.xln[:, i * 512:(i + 1) * 512],
                       start=(i == 0), stop=(i == FC - 1),
                       mark=("mm" if i == FC - 1 else None))
            qd = b.q[:, hh * 512:(hh + 1) * 512]
            rope_apply(qd, b.ps[:, 6 * 512:7 * 512], 5)
        for hh in range(NH):
            for cc in range(8):
                pbk = qb + (1 if cc >= 4 else 0)
                slot = pbk % 2
                ts = cc % 4
                ktile = b.kring[:, hh * 1024 + slot * 512 + ts * 128:
                                hh * 1024 + slot * 512 + (ts + 1) * 128]
                ctx.ldw(ktile)
                ctx.mm(b.ps[:, (cc % 2) * 512:(cc % 2 + 1) * 512],
                       ktile, b.q[:, hh * 512:(hh + 1) * 512],
                       start=True, stop=True, mark="mm")
                ctx.act(b.pt[:, (cc % 2) * 512:(cc % 2 + 1) * 512],
                        b.ps[:, (cc % 2) * 512:(cc % 2 + 1) * 512],
                        AF.Exp, scale=BQS)
                ctx.tt(b.pt[:, (cc % 2) * 512:(cc % 2 + 1) * 512],
                       b.pt[:, (cc % 2) * 512:(cc % 2 + 1) * 512],
                       b.bmask[:, cc * 512:(cc + 1) * 512], ALU.mult)
                ctx.mm(b.ps[0:1, 2 * 512:2 * 512 + 512], b.ones_col[:, :],
                       b.pt[:, (cc % 2) * 512:(cc % 2 + 1) * 512],
                       start=(cc == 0), stop=(cc == 7),
                       mark=("mm" if cc == 7 else None))
                vtile = b.vring[:, (slot * 4 + ts) * H + hh * 128:
                                (slot * 4 + ts) * H + (hh + 1) * 128]
                ctx.ldw(vtile)
                ctx.mm(b.ps[:, 3 * 512:4 * 512], vtile,
                       b.pt[:, (cc % 2) * 512:(cc % 2 + 1) * 512],
                       start=(cc == 0), stop=(cc == 7),
                       mark=("mm" if cc == 7 else None))
            ctx.recip(b.rows[0:1, 0:512], b.ps[0:1, 2 * 512:2 * 512 + 512])
            ctx.mm(b.ps[:, 4 * 512:5 * 512], b.ones_row[:, :],
                   b.rows[0:1, 0:512], start=True, stop=True, mark="mm")
            ctx.act(b.rb[:, :], b.ps[:, 4 * 512:5 * 512], AF.Copy)
            ctx.tt(b.osb[:, hh * 512:(hh + 1) * 512], b.ps[:, 3 * 512:4 * 512],
                   b.rb[:, :], ALU.mult)
        for j in range(FC):
            wload(ctx, "POOL", b.wsl, (j % 2) * H, d.wb_o, FC, j,
                  "wlG" if j % 2 == 0 else "wlH")
            for i in range(FC):
                ctx.ldw(b.wsl[:, (j % 2) * H + i * 128:(j % 2) * H + (i + 1) * 128])
                ctx.mm(b.ps[:, 6 * 512:7 * 512],
                       b.wsl[:, (j % 2) * H + i * 128:(j % 2) * H + (i + 1) * 128],
                       b.osb[:, i * 512:(i + 1) * 512],
                       start=(i == 0), stop=(i == FC - 1),
                       mark=("mm" if i == FC - 1 else None))
            obuf = b.ob if j % 2 == 0 else b.tmp
            ctx.tt(obuf[:, 0:512], b.xblk[:, j * 512:(j + 1) * 512],
                   b.ps[:, 6 * 512:7 * 512], ALU.add)
            ctx.dma("SP", d.x3_t[:, qb * 8192 + j * 512:
                                 qb * 8192 + (j + 1) * 512],
                    obuf[:, 0:512], "stA" if j % 2 == 0 else "stB")
        if pb == 1:
            ctx.dma("POOL", b.bmask[:, :], d.bmask[:, 0:8 * 512], "wlJ")


def build_program():
    nc = bass.Bass("TRN2", target_bir_lowering=False)
    d = type("D", (), {})()
    dt = nc.dram_tensor
    d.x_t = dt("x_t", [128, 8 * 8192], F32, kind="ExternalInput").ap()
    d.wa_q = dt("wa_q", [128, H * FC], F32, kind="ExternalInput").ap()
    d.wa_f = dt("wa_f", [128, H * FC], F32, kind="ExternalInput").ap()
    d.wa_i = dt("wa_i", [128, H * FC], F32, kind="ExternalInput").ap()
    d.wa_o = dt("wa_o", [128, H * FC], F32, kind="ExternalInput").ap()
    d.ga_t = dt("ga_t", [128, FC], F32, kind="ExternalInput").ap()
    d.wb_q = dt("wb_q", [128, H * FC], F32, kind="ExternalInput").ap()
    d.wb_k = dt("wb_k", [128, H * FC], F32, kind="ExternalInput").ap()
    d.wb_v = dt("wb_v", [H, H], F32, kind="ExternalInput").ap()
    d.wb_o = dt("wb_o", [128, H * FC], F32, kind="ExternalInput").ap()
    d.f_rw = dt("f_rw", [128, FC * 16], F32, kind="ExternalInput").ap()
    d.f_wg = dt("f_wg", [128, E * IM * FC], F32, kind="ExternalInput").ap()
    d.f_wu = dt("f_wu", [128, E * IM * FC], F32, kind="ExternalInput").ap()
    d.f_wd = dt("f_wd", [128, E * H * JM], F32, kind="ExternalInput").ap()
    d.f_sg = dt("f_sg", [128, ISH * FC], F32, kind="ExternalInput").ap()
    d.f_su = dt("f_su", [128, ISH * FC], F32, kind="ExternalInput").ap()
    d.f_sd = dt("f_sd", [128, H * JSH], F32, kind="ExternalInput").ap()
    d.f_sgw = dt("f_sgw", [128, FC], F32, kind="ExternalInput").ap()
    d.c_ident = dt("c_ident", [128, 128], BF16, kind="ExternalInput").ap()
    d.c_ones_col = dt("c_ones_col", [128, 1], BF16, kind="ExternalInput").ap()
    d.c_ones_row = dt("c_ones_row", [1, 128], F32, kind="ExternalInput").ap()
    d.c_maskaa = dt("c_maskaa", [128, 128], BF16, kind="ExternalInput").ap()
    d.c_emat = dt("c_emat", [16, 16 * 128], BF16, kind="ExternalInput").ap()
    d.c_pswap = dt("c_pswap", [128, 128], BF16, kind="ExternalInput").ap()
    d.c_bias = dt("c_bias", [128, 4], F32, kind="ExternalInput").ap()
    d.c_bias2 = dt("c_bias2", [128, 4], F32, kind="ExternalInput").ap()
    d.ropec = dt("ropec", [128, T_MAIN], BF16, kind="ExternalInput").ap()
    d.ropes = dt("ropes", [128, T_MAIN], BF16, kind="ExternalInput").ap()
    d.bmask = dt("bmask", [128, 16 * 512], BF16, kind="ExternalInput").ap()
    d.x1_t = dt("x1_t", [128, 5 * 8192], F32).ap()
    d.x2_t = dt("x2_t", [128, 5 * 8192], F32).ap()
    d.x3_t = dt("x3_t", [128, 4 * 8192], F32).ap()
    d.xln_t = dt("xln_t", [128, 8 * 8192], BF16).ap()
    d.osb_t = dt("osb_t", [128, 5 * 8192], BF16).ap()
    d.y_t = dt("y_t", [128, 4 * 8192], F32, kind="ExternalOutput").ap()

    with contextlib.ExitStack() as es0:
        sems = {k: es0.enter_context(nc.semaphore("s_" + k)) for k in SEMS}
        block = es0.enter_context(nc.Block())
        ctx = Ctx(nc, block, sems)

        cst = type("C", (), {})()
        cst.ident = es0.enter_context(nc.sbuf_tensor("c_ident_s", [128, 128], BF16))
        cst.ones_col = es0.enter_context(nc.sbuf_tensor("c_onec_s", [128, 1], BF16))
        cst.ones_row = es0.enter_context(nc.sbuf_tensor("c_oner_s", [1, 128], F32))
        cst.maskaa = es0.enter_context(nc.sbuf_tensor("c_maa_s", [128, 128], BF16))
        cst.emat = es0.enter_context(nc.sbuf_tensor("c_emat_s", [16, 16 * 128], BF16))
        cst.gat = es0.enter_context(nc.sbuf_tensor("c_gat_s", [128, FC], F32))
        cst.rw = es0.enter_context(nc.sbuf_tensor("c_rw_s", [128, FC * 16], BF16))
        cst.comb_t = es0.enter_context(nc.sbuf_tensor("c_combt", [16, T_MAIN], BF16))
        cst.cbias = es0.enter_context(nc.sbuf_tensor("c_cbias", [128, 4], F32))
        cst.cbias2 = es0.enter_context(nc.sbuf_tensor("c_cbias2", [128, 4], F32))
        nc.const_aps.register(F32, LN_EPS, cst.cbias[:, 0:1])
        nc.const_aps.register(F32, 1.0, cst.cbias[:, 1:2])
        nc.const_aps.register(F32, LNC, cst.cbias[:, 2:3])
        nc.const_aps.register(F32, LNC - SH, cst.cbias2[:, 0:1])
        nc.const_aps.register(F32, -SH, cst.cbias2[:, 1:2])
        nc.const_aps.register(F32, SH, cst.cbias2[:, 2:3])

        def ph_init(c):
            c.dma("SP", cst.ident[:, :], d.c_ident[:, :], "ldA")
            c.dma("SP", cst.ones_col[:, :], d.c_ones_col[:, :], "ldB")
            c.dma("SP", cst.ones_row[:, :], d.c_ones_row[:, :], "ldC")
            c.dma("SP", cst.maskaa[:, :], d.c_maskaa[:, :], "ldD")
            c.dma("SP", cst.emat[:, :], d.c_emat[:, :], "stA")
            c.dma("SP", cst.gat[:, :], d.ga_t[:, :], "stB")
            c.dma("SP", cst.cbias[:, :], d.c_bias[:, :], "stC")
            c.dma("POOL", cst.cbias2[:, :], d.c_bias2[:, :], "wlB")
            c.dma("POOL", cst.rw[:, :], d.f_rw[:, :], "wlA")

        run_phase(ctx, ph_init, "init")

        with contextlib.ExitStack() as es:
            b = alloc_ffn_ln(ctx, es, cst, "al")
            run_phase(ctx, lambda c: emit_ffn_ln(c, b, d, d.x_t, 8 * 512,
                                                 router=False), "aln")
        with contextlib.ExitStack() as es:
            b = alloc_aam(ctx, es, cst)
            run_phase(ctx, lambda c: emit_aam(c, b, d), "aam")
        with contextlib.ExitStack() as es:
            b = alloc_awo(ctx, es, cst)
            run_phase(ctx, lambda c: emit_awo(c, b, d), "awo")
        with contextlib.ExitStack() as es:
            b = alloc_ffn_ln(ctx, es, cst, "f1l")
            run_phase(ctx, lambda c: emit_ffn_ln(c, b, d, d.x1_t, T_MAIN), "f1l")
        with contextlib.ExitStack() as es:
            b = alloc_ffn_moe(ctx, es, cst, "f1m")
            run_phase(ctx, lambda c: emit_ffn_moe(c, b, d, d.x1_t, d.x2_t,
                                                  T_MAIN), "f1m")
        with contextlib.ExitStack() as es:
            b = alloc_ab(ctx, es, cst)
            b.wi_ab = b.wsl
            run_phase(ctx, lambda c: emit_ab(c, b, d), "ab")
        with contextlib.ExitStack() as es:
            b = alloc_ffn_ln(ctx, es, cst, "f2l")
            run_phase(ctx, lambda c: emit_ffn_ln(c, b, d, d.x3_t, OWN), "f2l")
        with contextlib.ExitStack() as es:
            b = alloc_ffn_moe(ctx, es, cst, "f2m")
            run_phase(ctx, lambda c: emit_ffn_moe(c, b, d, d.x3_t, d.y_t, OWN),
                      "f2m")

        def ph_fin(c):
            pass

        run_phase(ctx, ph_fin, "fin")
    return nc


def _rope_tables(positions):
    inv = 1.0 / (10000.0 ** (np.arange(0, DK, 2, dtype=np.float64) / DK))
    ang = inv[:, None] * positions[None, :].astype(np.float64)
    return np.cos(ang).astype(np.float32), np.sin(ang).astype(np.float32)


def _bmask(parity):
    out = np.zeros((16, 128, 512), dtype=np.float32)
    tk = np.arange(128)[:, None]
    tq = np.arange(512)[None, :]
    for c in range(8):
        dd = tq - tk - 128 * c + 512
        m = ((dd >= 0) & (dd < 512)).astype(np.float32)
        out[c] = m
        if parity == 1:
            out[8 + c] = m
        else:
            out[8 + c] = m if c >= 4 else 0.0
    return out.reshape(16 * 128, 512).astype(bfdt)


def _colchunk(w):
    K, D = w.shape
    return np.ascontiguousarray(
        w.reshape(K // 128, 128, D // 128, 128).transpose(1, 2, 0, 3)
        .reshape(128, -1))


def _blocked(xt):
    Hh, ncols = xt.shape
    nblk = ncols // 512
    return np.ascontiguousarray(
        xt.reshape(16, 128, nblk, 512).transpose(1, 2, 0, 3).reshape(128, -1))


def _perm_deinterleave():
    p = np.zeros(H, dtype=np.int64)
    for h in range(NH):
        base = h * DK
        p[base:base + 64] = base + 2 * np.arange(64)
        p[base + 64:base + DK] = base + 2 * np.arange(64) + 1
    return p


def make_in_maps(L0_attn_a_x, L0_attn_a_Wq, L0_attn_a_Wf, L0_attn_a_Wi,
                 L0_attn_a_Wo, L0_attn_a_g_weight, L0_attn_b_Wq, L0_attn_b_Wk,
                 L0_attn_b_Wv, L0_attn_b_Wo, L0_ffn_router_w, L0_ffn_w_gate,
                 L0_ffn_w_up, L0_ffn_w_down, L0_ffn_sh_gate, L0_ffn_sh_up,
                 L0_ffn_sh_down, L0_ffn_sh_gate_w):
    x = np.asarray(L0_attn_a_x, dtype=np.float32)
    B, S, _ = x.shape
    perm = _perm_deinterleave()

    wg_all = np.empty((H, E * IM), np.float32)
    wu_all = np.empty((H, E * IM), np.float32)
    wd_all = np.empty((IM, E * H), np.float32)
    for e in range(E):
        wg_all[:, e * IM:(e + 1) * IM] = np.asarray(L0_ffn_w_gate[e], np.float32)
        wu_all[:, e * IM:(e + 1) * IM] = np.asarray(L0_ffn_w_up[e], np.float32)
        wd_all[:, e * H:(e + 1) * H] = np.asarray(L0_ffn_w_down[e], np.float32)
    com = {
        "wa_q": _colchunk(np.asarray(L0_attn_a_Wq, dtype=np.float32)),
        "wa_f": _colchunk(np.asarray(L0_attn_a_Wf, dtype=np.float32)),
        "wa_i": _colchunk(np.asarray(L0_attn_a_Wi, dtype=np.float32)),
        "wa_o": _colchunk(np.asarray(L0_attn_a_Wo, dtype=np.float32)),
        "ga_t": np.ascontiguousarray(
            np.asarray(L0_attn_a_g_weight, dtype=np.float32).reshape(FC, 128).T),
        "wb_q": _colchunk(np.ascontiguousarray(
            np.asarray(L0_attn_b_Wq, dtype=np.float32).T[:, perm])),
        "wb_k": _colchunk(np.ascontiguousarray(
            np.asarray(L0_attn_b_Wk, dtype=np.float32).T[:, perm])),
        "wb_v": np.ascontiguousarray(np.asarray(L0_attn_b_Wv, dtype=np.float32).T),
        "wb_o": _colchunk(np.ascontiguousarray(
            np.asarray(L0_attn_b_Wo, dtype=np.float32).T)),
        "f_rw": np.ascontiguousarray(
            np.asarray(L0_ffn_router_w, dtype=np.float32).T
            .reshape(16, 128, 16).transpose(1, 0, 2).reshape(128, 256)),
        "f_wg": _colchunk(wg_all),
        "f_wu": _colchunk(wu_all),
        "f_wd": _colchunk(wd_all),
        "f_sg": _colchunk(np.ascontiguousarray(L0_ffn_sh_gate, dtype=np.float32)),
        "f_su": _colchunk(np.ascontiguousarray(L0_ffn_sh_up, dtype=np.float32)),
        "f_sd": _colchunk(np.ascontiguousarray(L0_ffn_sh_down, dtype=np.float32)),
        "f_sgw": np.ascontiguousarray(np.asarray(L0_ffn_sh_gate_w, dtype=np.float32).reshape(FC, 128).T),
        "c_ident": np.eye(128, dtype=np.float32).astype(bfdt),
        "c_ones_col": np.ones((128, 1), dtype=np.float32).astype(bfdt),
        "c_ones_row": np.ones((1, 128), dtype=np.float32),
        "c_maskaa": (SC_MASK * np.triu(np.ones((128, 128), dtype=np.float32))).astype(bfdt),
    }
    emat = np.zeros((16, 16 * 128), dtype=np.float32)
    for e in range(16):
        emat[e, e * 128:(e + 1) * 128] = 1.0
    com["c_emat"] = emat.astype(bfdt)
    psw = np.zeros((128, 128), dtype=np.float32)
    psw[(np.arange(128) + 64) % 128, np.arange(128)] = 1.0
    com["c_pswap"] = psw.astype(bfdt)
    com["c_bias"] = np.tile(np.array([[LN_EPS, 1.0, LNC, 0.0]], np.float32),
                            (128, 1))
    com["c_bias2"] = np.tile(np.array([[LNC - SH, -SH, SH, 0.0]], np.float32),
                             (128, 1))

    in_maps = []
    for c in range(8):
        bb, odd = c // 2, c % 2
        seq = x[bb]
        if odd:
            chain = seq
        else:
            chain = np.concatenate([np.zeros((2048, H), np.float32), seq[0:2048]],
                                   axis=0)
        m = dict(com)
        m["x_t"] = _blocked(np.ascontiguousarray(chain.T))
        start = 2048 * odd
        pos = np.arange(MAIN0, NT) - (0 if odd else 2048)
        pos = np.maximum(pos, 0)
        rc, rs = _rope_tables(pos)
        m["ropec"] = np.concatenate([rc, rc], axis=0).astype(bfdt)
        m["ropes"] = np.concatenate([-rs, rs], axis=0).astype(bfdt)
        m["bmask"] = np.ascontiguousarray(
            _bmask(odd).reshape(16, 128, 512).transpose(1, 0, 2)
            .reshape(128, 16 * 512))
        in_maps.append(m)
    return in_maps, (B, S)


def kernel(**inputs):
    in_maps, (B, S) = make_in_maps(**inputs)
    nc = build_program()
    res = run_bass_kernel_spmd(nc, in_maps, core_ids=list(range(8)))
    out = np.empty((B, S, H), dtype=np.float32)
    for c in range(8):
        bb, odd = c // 2, c % 2
        y = res.results[c]["y_t"].reshape(128, 4, 16, 512)
        out[bb, odd * 2048:(odd + 1) * 2048, :] = (
            y.transpose(1, 3, 2, 0).reshape(2048, 2048))
    return out


# version 08
# speedup vs baseline: 1.0407x; 1.0407x over previous
import contextlib
import math

import numpy as np
import ml_dtypes

import concourse.bass as bass
import concourse.mybir as mybir
from concourse.bass_utils import run_bass_kernel_spmd

F32 = mybir.dt.float32
BF16 = mybir.dt.bfloat16
AF = mybir.ActivationFunctionType
ALU = mybir.AluOpType
AX = mybir.AxisListType
bfdt = ml_dtypes.bfloat16

H = 2048
FC = 16
NH = 16
DK = 128
CH = 128
NCH = 4
NT = 4096
MAIN0 = 1536
T_MAIN = 2560
OWN0 = 2048
OWN = 2048
E = 16
IM = 1408
JM = 11
ISH = 5632
JSH = 44
LN_EPS = 1e-5
AA_EPS = 1e-5
LNC = math.log(DK ** -0.5)
BQS = 1.0 / math.sqrt(DK)
BIGNEG = 1.0e30
SH = 52 * math.log(2.0)
SC_MASK = 2.0 ** 104

ROLES = ["SP", "POOL", "PE", "ACT", "DVE"]
SEMS = (["ld" + c for c in "ABCD"] + ["st" + c for c in "ABC"] +
        ["wl" + c for c in "ABCDEFGHIJ"] + ["mm", "ev", "dv", "bar"])


def _region(ap):
    h = ap.tensor if hasattr(ap, "tensor") else ap
    if not hasattr(ap, "ap"):
        shp = h.shape
        return (h.name, 0, 128, 0, int(np.prod(shp)))
    name = h.name
    shp = list(h.shape)
    off = ap.offset
    dims = [list(x) for x in ap.ap]
    memtype = getattr(h, "__class__", None).__name__
    if memtype in ("SBTensorHandle", "PSumTensorHandle") and len(shp) == 2:
        rowlen = shp[1]
        p0 = off // rowlen
        c0 = off % rowlen
        pspan = 0
        cspan = 0
        for (st, cnt) in dims:
            if cnt <= 1:
                continue
            if st != 0 and st % rowlen == 0:
                pspan += (st // rowlen) * (cnt - 1)
            else:
                cspan += abs(st) * (cnt - 1)
        c1 = c0 + cspan + 1
        if memtype == "PSumTensorHandle":
            bank = 512 if h.dtype == mybir.dt.float32 else 1024
            c0 = (c0 // bank) * bank
            c1 = ((c1 + bank - 1) // bank) * bank
            return (name, 0, 128, c0, c1)
        return (name, p0, p0 + pspan + 1, c0, c1)
    span = 0
    for (st, cnt) in dims:
        if cnt > 1:
            span += abs(st) * (cnt - 1)
    return (name, 0, 1 << 30, off, off + span + 1)


def _overlap(a, b):
    return (a[0] == b[0] and a[1] < b[2] and b[1] < a[2]
            and a[3] < b[4] and b[3] < a[4])


class Ctx:
    ENG_SEM = {"PE": "mm", "ACT": "ev", "DVE": "dv"}

    def __init__(self, nc, block, sems):
        self.nc = nc
        self.block = block
        self.sem = sems
        self.cnt = {k: 0 for k in SEMS}
        self.role = None
        self.eng = None
        self.hiw = {}
        self.twr = {}
        self.trd = {}
        self.pe_pending = []
        self.dma_owner = {}

    def t_clear(self):
        self.twr.clear()
        self.trd.clear()
        self.pe_pending = []

    def _dep_read(self, ap, deps):
        r = _region(ap)
        for w in self.twr.get(r[0], ()):
            if _overlap(r, (r[0], w[0], w[1], w[2], w[3])):
                if w[5] is None:
                    raise RuntimeError(
                        f"read of {r[0]} depends on unmarked PE write")
                deps.append((w[4], w[5]))

    def _note_read(self, ap, sem, cntv):
        r = _region(ap)
        ent = [r[1], r[2], r[3], r[4], sem, cntv]
        self.trd.setdefault(r[0], []).append(ent)
        return ent

    def _dep_write(self, ap, deps, is_pe):
        r = _region(ap)
        ws = self.twr.get(r[0], ())
        for w in ws:
            if _overlap(r, (r[0], w[0], w[1], w[2], w[3])):
                if is_pe and w[6]:
                    continue
                if w[5] is None:
                    raise RuntimeError(
                        f"write of {r[0]} overlaps unmarked PE write")
                deps.append((w[4], w[5]))
        for rd in self.trd.get(r[0], ()):
            if _overlap(r, (r[0], rd[0], rd[1], rd[2], rd[3])):
                if rd[5] is None:
                    raise RuntimeError(
                        f"write of {r[0]} overlaps PE read not yet covered "
                        f"by a PE mark")
                deps.append((rd[4], rd[5]))

    def _note_write(self, ap, sem, cntv, is_pe):
        r = _region(ap)
        lst = self.twr.setdefault(r[0], [])
        lst[:] = [w for w in lst
                  if not (r[1] <= w[0] and w[1] <= r[2]
                          and r[3] <= w[2] and w[3] <= r[4])]
        ent = [r[1], r[2], r[3], r[4], sem, cntv, is_pe]
        lst.append(ent)
        rl = self.trd.get(r[0])
        if rl:
            rl[:] = [x for x in rl
                     if not (r[1] <= x[0] and x[1] <= r[2]
                             and r[3] <= x[2] and x[3] <= r[4])]
        return ent

    def _emit_op(self, engine, reads, writes, waits, mark, emit_fn,
                 is_dma=False):
        own = self.ENG_SEM.get(engine)
        deps = list(waits)
        for ap in reads:
            if ap is not None and not isinstance(ap, (int, float)):
                self._dep_read(ap, deps)
        for ap in writes:
            self._dep_write(ap, deps, engine == "PE")
        if is_dma:
            sem = mark
            prev = self.dma_owner.setdefault(sem, engine)
            assert prev == engine, f"dma sem {sem} used from {prev} and {engine}"
            self.cnt[sem] += 16
            cntv = self.cnt[sem]
        elif engine == "PE":
            if mark:
                self.cnt["mm"] += 1
                cntv = self.cnt["mm"]
            else:
                cntv = None
            sem = "mm"
        else:
            sem = own
            self.cnt[sem] += 1
            cntv = self.cnt[sem]
        for ap in reads:
            if ap is not None and not isinstance(ap, (int, float)):
                e = self._note_read(ap, sem,
                                    None if (engine == "PE" and not is_dma)
                                    else cntv)
                if engine == "PE" and not is_dma:
                    self.pe_pending.append(e)
        ents = [self._note_write(ap, sem, cntv, engine == "PE" and not is_dma)
                for ap in writes]
        if engine == "PE" and not is_dma:
            if cntv is None:
                self.pe_pending.extend(ents)
            else:
                for e in self.pe_pending:
                    if e[5] is None:
                        e[5] = cntv
                self.pe_pending = []
                for e in ents:
                    e[5] = cntv
        if self.role == engine:
            self.w(*[(s, v) for (s, v) in deps if v])
            ins = emit_fn()
            if is_dma:
                ins.then_inc(self.sem[sem], 16)
            elif engine == "PE":
                if cntv is not None:
                    ins.then_inc(self.sem[sem], 1)
            else:
                ins.then_inc(self.sem[sem], 1)
        return cntv if cntv is not None else self.cnt.get(sem, 0)

    def w(self, *waits):
        for (s, v) in waits:
            if v <= 0:
                continue
            if self.hiw.get(s, -1) >= v:
                continue
            self.eng.wait_ge(self.sem[s], v)
            self.hiw[s] = v

    def now(self):
        return [("mm", self.cnt["mm"]), ("ev", self.cnt["ev"]),
                ("dv", self.cnt["dv"])]

    def dma(self, q, out, in_, sem, waits=()):
        return self._emit_op(q, [in_], [out], waits, sem,
                             lambda: self.eng.dma_start(out=out, in_=in_),
                             is_dma=True)

    def ldw(self, wt, waits=()):
        deps = list(waits)
        self._dep_read(wt, deps)
        self.pe_pending.append(self._note_read(wt, "mm", None))
        if self.role == "PE":
            self.w(*[(s, v) for (s, v) in deps if v])
            self.eng.ldweights(wt)

    def mm(self, out, lhsT, rhs, start, stop, mark=None, waits=()):
        return self._emit_op(
            "PE", [lhsT, rhs], [out], waits, mark,
            lambda: self.eng.matmul(out, lhsT, rhs, start=start, stop=stop))

    def tr(self, out, in_, ident, mark=None, waits=()):
        return self._emit_op(
            "PE", [in_, ident], [out], waits, mark,
            lambda: self.eng.transpose(out, in_, ident))

    def act(self, out, in_, func=AF.Copy, bias=None, scale=None, mark=None,
            waits=()):
        kw = {}
        if bias is not None:
            kw["bias"] = bias
        if scale is not None:
            kw["scale"] = scale
        return self._emit_op(
            "ACT", [in_, bias if hasattr(bias, "tensor") else None], [out],
            waits, mark, lambda: self.eng.activation(out, in_, func, **kw))

    def amul(self, out, in_, mul, mark=None, waits=()):
        return self._emit_op(
            "ACT", [in_, mul if hasattr(mul, "tensor") else None], [out],
            waits, mark, lambda: self.eng.mul(out, in_, mul))

    def aadd(self, out, in_, add, mark=None, waits=()):
        return self._emit_op(
            "ACT", [in_, add if hasattr(add, "tensor") else None], [out],
            waits, mark, lambda: self.eng.add(out, in_, add))

    def tt(self, out, a, b, op, mark=None, waits=()):
        return self._emit_op(
            "DVE", [a, b], [out], waits, mark,
            lambda: self.eng.tensor_tensor(out, a, b, op))

    def ts(self, out, in0, s1, op0, mark=None, waits=()):
        return self._emit_op(
            "DVE", [in0, s1 if hasattr(s1, "tensor") else None], [out],
            waits, mark,
            lambda: self.eng.tensor_scalar(out, in0, s1, None, op0))

    def red(self, out, in_, op, mark=None, waits=()):
        return self._emit_op(
            "DVE", [in_], [out], waits, mark,
            lambda: self.eng.tensor_reduce(out, in_, AX.X, op))

    def scan(self, out, d0, d1, init, op0, op1, mark=None, waits=()):
        return self._emit_op(
            "DVE", [d0, d1], [out], waits, mark,
            lambda: self.eng.tensor_tensor_scan(out, d0, d1, init, op0, op1))

    def recip(self, out, in_, mark=None, waits=()):
        return self._emit_op(
            "DVE", [in_], [out], waits, mark,
            lambda: self.eng.reciprocal(out, in_))

    def sel(self, out, mask, on_true, on_false, mark=None, waits=()):
        return self._emit_op(
            "DVE", [mask, on_true, on_false], [out], waits, mark,
            lambda: self.eng.select(out, mask, on_true, on_false, add_drain=True))

    def vmemset(self, ap, val, mark=None, waits=()):
        return self._emit_op(
            "DVE", [], [ap], waits, mark,
            lambda: self.eng.memset(ap, val))

    def c(self, sem):
        return self.cnt[sem]


def run_phase(ctx, fn, name=""):
    start = dict(ctx.cnt)
    final = None
    for role in ROLES:
        ctx.cnt = dict(start)
        ctx.hiw = {}
        ctx.t_clear()
        ctx.dma_owner = {}
        ctx.role = role

        def body(eng):
            ctx.eng = eng
            fn(ctx)
            own = {"PE": ["mm"], "ACT": ["ev"], "DVE": ["dv"],
                   "SP": [], "POOL": []}[ctx.role]
            own = own + [s for (s, r) in sorted(ctx.dma_owner.items())
                         if r == ctx.role]
            for s in own:
                if ctx.cnt[s]:
                    eng.wait_ge(ctx.sem[s], ctx.cnt[s])
            tgt = ctx.cnt["bar"] + 5
            eng.nop().then_inc(ctx.sem["bar"], 1)
            eng.wait_ge(ctx.sem["bar"], tgt)
            ctx.cnt["bar"] = tgt

        getattr(ctx.block, {"SP": "sync", "POOL": "gpsimd", "PE": "tensor",
                            "ACT": "scalar", "DVE": "vector"}[role])(body)
        if final is None:
            final = dict(ctx.cnt)
        else:
            assert final == ctx.cnt, f"role divergence in {name}"
    ctx.cnt = final


def hdl(t):
    return t.tensor if hasattr(t, "tensor") else t


def ap_cols_rep(sb, col0, ncols, rep):
    h = hdl(sb)
    return bass.AP(h, col0, [[h.shape[1], 128], [1, ncols], [0, rep]])


def ap_stride(sb, off, step, count):
    h = hdl(sb)
    return bass.AP(h, off, [[h.shape[1], 128], [step, count]])


def wload(ctx, q, dst_sb, dst_free0, src, Ka, j, sem, k0=0, k1=None):
    if k1 is None:
        k1 = Ka
    s0 = j * Ka * 128 + k0 * 128
    return ctx.dma(q, dst_sb[:, dst_free0 + 0:dst_free0 + (k1 - k0) * 128],
                   src[:, s0:s0 + (k1 - k0) * 128], sem)


def emit_ln(ctx, b, xsrc, c0, ncols):
    ps, rows = b.ps, b.rows
    s0, s1, c_inv, c_mean = getattr(b, "lnb", (0, 512, 1024, 1536))
    ldsem = getattr(b, "ldsem", "ldA")
    n = ncols
    blk = c0 // 512
    ctx.dma("SP", b.xblk[:, 0:FC * n], xsrc[:, blk * 8192:blk * 8192 + FC * n],
            ldsem)
    ctx.act(b.st16[:, :FC * n], b.xblk[:, :FC * n], AF.Square)
    for a in range(FC):
        ctx.mm(ps[0:1, s1:s1 + n], b.ones_col[:, :],
               b.st16[:, a * n:(a + 1) * n],
               start=(a == 0), stop=(a == FC - 1),
               mark=("mm" if a == FC - 1 else None))
    ctx.act(b.st16[:, :FC * n], b.xblk[:, :FC * n], AF.Copy)
    for a in range(FC):
        ctx.mm(ps[0:1, s0:s0 + n], b.ones_col[:, :], b.st16[:, a * n:(a + 1) * n],
               start=(a == 0), stop=(a == FC - 1),
               mark=("mm" if a == FC - 1 else None))
    inv_h = 1.0 / H
    ctx.amul(rows[0:1, 0:n], ps[0:1, s0:s0 + n], inv_h)
    ctx.amul(rows[0:1, n:2 * n], ps[0:1, s1:s1 + n], inv_h)
    ctx.tt(rows[0:1, 2 * n:3 * n], rows[0:1, 0:n], rows[0:1, 0:n], ALU.mult)
    ctx.tt(rows[0:1, n:2 * n], rows[0:1, n:2 * n], rows[0:1, 2 * n:3 * n],
           ALU.subtract)
    ctx.act(rows[0:1, 2 * n:3 * n], rows[0:1, n:2 * n], AF.Ln, bias=LN_EPS)
    ctx.act(rows[0:1, n:2 * n], rows[0:1, 2 * n:3 * n], AF.Exp, scale=-0.5)
    ctx.tt(rows[0:1, 2 * n:3 * n], rows[0:1, 0:n], rows[0:1, n:2 * n],
           ALU.mult)
    ctx.mm(ps[:, c_inv:c_inv + n], b.ones_row[:, :], rows[0:1, n:2 * n],
           start=True, stop=True, mark="mm")
    ctx.mm(ps[:, c_mean:c_mean + n], b.ones_row[:, :], rows[0:1, 2 * n:3 * n],
           start=True, stop=True, mark="mm")
    dvv = 0
    for a in range(FC):
        ctx.tt(b.tmp[:, 0:n], b.xblk[:, a * n:(a + 1) * n],
               ps[:, c_inv:c_inv + n], ALU.mult)
        dvv = ctx.tt(b.xln[:, a * n:(a + 1) * n], b.tmp[:, 0:n],
                     ps[:, c_mean:c_mean + n], ALU.subtract)
    return dvv


def alloc_aam(ctx, es, cst):
    nc = ctx.nc
    sb = nc.sbuf_tensor
    b = type("B", (), {})()
    b.__dict__.update(cst.__dict__)
    b.xlnb = [es.enter_context(sb(f"aam_xln{i}", [128, FC * 512], BF16))
              for i in range(2)]
    b.wqf = es.enter_context(sb("aam_wqf", [128, 4 * 3 * H], BF16))
    b.Hst = es.enter_context(sb("aam_H", [128, NH * 128], F32))
    b.fs = [es.enter_context(sb(f"aam_fs{i}", [128, 1024], F32))
            for i in range(2)]
    b.qs = [es.enter_context(sb(f"aam_qs{i}", [128, 1024], F32))
            for i in range(2)]
    b.vfm = [es.enter_context(sb(f"aam_vfm{i}", [128, 1024], BF16))
             for i in range(2)]
    for nm in ["tmp", "tm2", "F1", "ks", "gs", "As"]:
        setattr(b, nm, es.enter_context(sb("aam_" + nm, [128, 1024], F32)))
    for nm in ["qt", "kt", "kh", "khT", "sm", "vtok"]:
        setattr(b, nm, es.enter_context(sb("aam_" + nm, [128, 1024], BF16)))
    b.Hb = es.enter_context(sb("aam_Hb", [128, 256], BF16))
    b.cm = es.enter_context(sb("aam_cm", [128, 32], F32))
    b.osbp = [es.enter_context(sb(f"aam_osbp{i}", [128, 512], BF16))
              for i in range(2)]
    b.ps = es.enter_context(nc.psum_tensor("aam_ps", [128, 7 * 512], F32))
    b.psb = es.enter_context(nc.psum_tensor("aam_psb", [128, 1024], BF16))
    return b


def aam_A(ctx, b, hh, hp, is_main, bs):
    wslot = (hp % 2) * (6 * H)
    hoff = hh * 2048
    ho5 = hh * 512

    def proj(woff, dst_ev, func):
        pbk = b.pb()
        for i in range(FC):
            wt = b.wqf[:, wslot + woff + i * 128:
                       wslot + woff + (i + 1) * 128]
            ctx.ldw(wt)
            ctx.mm(b.ps[:, pbk * 512:(pbk + 1) * 512], wt,
                   b.xlnb[bs][:, i * 512:(i + 1) * 512],
                   start=(i == 0), stop=(i == FC - 1),
                   mark=("mm" if i == FC - 1 else None))
        ctx.act(dst_ev, b.ps[:, pbk * 512:(pbk + 1) * 512], func)

    if is_main:
        proj(hoff, b.qs[bs][:, ho5:ho5 + 512], AF.Silu)
    proj(4096 + hoff, b.fs[bs][:, ho5:ho5 + 512], AF.Copy)
    proj(8192 + hoff, b.vfm[bs][:, ho5:ho5 + 512], AF.Copy)


def aam_B(ctx, b, is_main, bs):
    fs, qs = b.fs[bs], b.qs[bs]
    ctx.act(b.tmp[:, :], fs[:, :], AF.Abs)
    ctx.act(b.tmp[:, :], b.tmp[:, :], AF.Exp, scale=-1.0)
    ctx.act(b.tm2[:, :], b.tmp[:, :], AF.Ln, bias=1.0)
    ctx.ts(b.gs[:, :], fs[:, :], 0.0, ALU.min)
    ctx.tt(b.gs[:, :], b.gs[:, :], b.tm2[:, :], ALU.subtract)
    for a in range(8):
        ctx.scan(b.As[:, a * CH:(a + 1) * CH],
                 b.gs[:, a * CH:(a + 1) * CH],
                 b.gs[:, a * CH:(a + 1) * CH], 0.0, ALU.add, ALU.bypass)
    for hh in range(2):
        co = hh * 16
        ACm = b.cm[:, co:co + 4]
        mcol = b.cm[:, co + 4:co + 8]
        ctx.act(ACm, ap_stride(b.As, hh * 512 + CH - 1, CH, NCH), AF.Copy)
        ctx.amul(mcol, ACm, 0.5)
        ctx.act(b.cm[:, co + 8:co + 12], ACm, AF.Exp)
        ctx.act(b.cm[:, co + 12:co + 16], mcol, AF.Exp, bias=SH)
        t3 = b.tmp[:, hh * 512:(hh + 1) * 512].rearrange(
            "p (c n) -> p c n", c=NCH)
        a3 = b.As[:, hh * 512:(hh + 1) * 512].rearrange(
            "p (c n) -> p c n", c=NCH)
        ctx.tt(t3, a3, ap_cols_rep(b.cm, co + 4, NCH, CH), ALU.subtract)
    if is_main:
        ctx.act(b.F1[:, :], b.tmp[:, :], AF.Exp, bias=LNC - SH)
        ctx.tt(b.qt[:, :], qs[:, :], b.F1[:, :], ALU.mult)
    ctx.act(b.F1[:, :], b.tmp[:, :], AF.Exp, scale=-1.0, bias=-SH)
    ctx.act(b.tm2[:, :], b.gs[:, :], AF.Exp)
    ctx.act(b.ks[:, :], b.tm2[:, :], AF.Copy, scale=-1.0, bias=1.0)
    ctx.tt(b.kt[:, :], b.ks[:, :], b.F1[:, :], ALU.mult)
    for hh in range(2):
        co = hh * 16
        k3 = b.kh[:, hh * 512:(hh + 1) * 512].rearrange(
            "p (c n) -> p c n", c=NCH)
        kt3 = b.kt[:, hh * 512:(hh + 1) * 512].rearrange(
            "p (c n) -> p c n", c=NCH)
        ctx.tt(k3, kt3, ap_cols_rep(b.cm, co + 12, NCH, CH), ALU.mult)


def aam_C(ctx, b, d, hp, blk, is_main, bs):
    for hh in range(2):
        for cc in range(NCH):
            sl = (hh * 4 + cc) * 128
            ctx.tr(b.psb[:, sl:sl + 128],
                   b.vfm[bs][:, sl:sl + 128], b.ident[:, :], mark="mm")
            ctx.act(b.vtok[:, sl:sl + 128], b.psb[:, sl:sl + 128], AF.Copy)
    for hh in range(2):
        for cc in range(NCH):
            sl = (hh * 4 + cc) * 128
            ctx.tr(b.psb[:, sl:sl + 128],
                   b.kh[:, sl:sl + 128], b.ident[:, :], mark="mm")
            ctx.act(b.khT[:, sl:sl + 128], b.psb[:, sl:sl + 128], AF.Copy)
    for cc in range(NCH):
        for hh in range(2):
            h = hp * 2 + hh
            o = hh * 512 + cc * 128
            co = hh * 16
            Hh = b.Hst[:, h * 128:(h + 1) * 128]
            vt = b.vtok[:, o:o + 128]
            ao = (2 + hh) * 512
            if is_main:
                ctx.amul(b.Hb[:, hh * 128:(hh + 1) * 128], Hh,
                         b.cm[:, co + 12 + cc:co + 13 + cc])
                ctx.ldw(b.Hb[:, hh * 128:(hh + 1) * 128])
                ctx.mm(b.ps[:, ao:ao + 128], b.Hb[:, hh * 128:(hh + 1) * 128],
                       b.qt[:, o:o + 128], start=True, stop=False)
                ctx.ldw(b.kt[:, o:o + 128])
                ctx.mm(b.ps[:, 4 * 512:4 * 512 + 128], b.kt[:, o:o + 128],
                       b.qt[:, o:o + 128], start=True, stop=True, mark="mm")
                ctx.tt(b.sm[:, o:o + 128], b.ps[:, 4 * 512:4 * 512 + 128],
                       b.maskaa[:, :], ALU.mult)
                ctx.ldw(vt)
                ctx.mm(b.ps[:, ao:ao + 128], vt, b.sm[:, o:o + 128],
                       start=False, stop=True, mark="mm")
                ctx.act(b.osbp[hh][:, cc * 128:(cc + 1) * 128],
                        b.ps[:, ao:ao + 128], AF.Copy)
            ctx.ldw(b.khT[:, o:o + 128])
            ctx.mm(b.ps[:, hh * 512:hh * 512 + 128], b.khT[:, o:o + 128],
                   vt, start=True, stop=True, mark="mm")
            ctx.amul(Hh, Hh, b.cm[:, co + 8 + cc:co + 9 + cc])
            ctx.tt(Hh, Hh, b.ps[:, hh * 512:hh * 512 + 128], ALU.add)
    if is_main:
        for hh in range(2):
            h = hp * 2 + hh
            ctx.dma("SP", d.osb_t[:, (blk - 3) * 8192 + h * 512:
                                  (blk - 3) * 8192 + (h + 1) * 512],
                    b.osbp[hh][:, :], "stA" if hh == 0 else "stB")


def emit_aam(ctx, b, d):
    ctx.vmemset(b.Hst[:, :], 0.0)
    pbc = [0]

    def pb():
        pbc[0] ^= 1
        return 5 + pbc[0]
    b.pb = pb

    def load_w(hp):
        wslot = (hp % 2) * (6 * H)
        for hh in range(2):
            h = hp * 2 + hh
            wload(ctx, "POOL", b.wqf, wslot + hh * 2048, d.wa_q, FC, h,
                  "wlA" if hp % 2 == 0 else "wlB")
            wload(ctx, "POOL", b.wqf, wslot + 4096 + hh * 2048, d.wa_f, FC, h,
                  "wlC" if hp % 2 == 0 else "wlD")
            wload(ctx, "POOL", b.wqf, wslot + 8192 + hh * 2048, d.wa_i, FC, h,
                  "wlE" if hp % 2 == 0 else "wlF")

    ctx.dma("SP", b.xlnb[0][:, :], d.xln_t[:, 0:8192], "ldA")
    for hp in range(NH // 2):
        load_w(hp)
        for blk in range(8):
            is_main = blk >= 3
            bs = blk % 2
            for hh in range(2):
                aam_A(ctx, b, hh, hp, is_main, bs)
            if blk + 1 < 8:
                nxt = blk + 1
                ctx.dma("SP", b.xlnb[nxt % 2][:, :],
                        d.xln_t[:, nxt * 8192:(nxt + 1) * 8192],
                        "ldA" if nxt % 2 == 0 else "ldB")
            elif hp + 1 < NH // 2:
                ctx.dma("SP", b.xlnb[0][:, :], d.xln_t[:, 0:8192], "ldA")
            if blk > 0:
                aam_C(ctx, b, d, hp, blk - 1, blk - 1 >= 3, (blk - 1) % 2)
            aam_B(ctx, b, is_main, bs)
        aam_C(ctx, b, d, hp, 7, True, 1)


def alloc_awo(ctx, es, cst):
    nc = ctx.nc
    sb = nc.sbuf_tensor
    b = type("B", (), {})()
    b.__dict__.update(cst.__dict__)
    b.osb = [es.enter_context(sb(f"awo_osb{i}", [128, FC * 512], BF16))
             for i in range(2)]
    b.st16 = es.enter_context(sb("awo_st16", [128, FC * 512], BF16))
    b.wo = es.enter_context(sb("awo_wo", [128, 2 * H], BF16))
    b.xr = [es.enter_context(sb(f"awo_xr{i}", [128, 512], F32))
            for i in range(2)]
    b.ob = [es.enter_context(sb(f"awo_ob{i}", [128, 512], F32))
            for i in range(2)]
    b.rows = es.enter_context(sb("awo_rows", [1, 2 * 512], F32))
    b.ps = es.enter_context(nc.psum_tensor("awo_ps", [128, 7 * 512], F32))
    return b


def emit_awo(ctx, b, d):
    ctx.dma("SP", b.osb[0][:, :], d.osb_t[:, 0:8192], "ldA")
    for blk in range(5):
        par = blk % 2
        if blk + 1 < 5:
            ctx.dma("SP", b.osb[(blk + 1) % 2][:, :],
                    d.osb_t[:, (blk + 1) * 8192:(blk + 2) * 8192],
                    "ldA" if (blk + 1) % 2 == 0 else "ldB")
        osb = b.osb[par]
        ctx.act(b.st16[:, :], osb[:, :], AF.Square)
        for a in range(FC):
            ctx.mm(b.ps[0:1, 0:512], b.ones_col[:, :],
                   b.st16[:, a * 512:(a + 1) * 512],
                   start=(a == 0), stop=(a == FC - 1),
                   mark=("mm" if a == FC - 1 else None))
        ctx.amul(b.rows[0:1, 0:512], b.ps[0:1, 0:512], 1.0 / H)
        ctx.act(b.rows[0:1, 512:1024], b.rows[0:1, 0:512], AF.Ln, bias=AA_EPS)
        ctx.act(b.rows[0:1, 0:512], b.rows[0:1, 512:1024], AF.Exp, scale=-0.5)
        ctx.mm(b.ps[:, 4 * 512:5 * 512], b.ones_row[:, :],
               b.rows[0:1, 0:512], start=True, stop=True, mark="mm")
        o3 = osb[:, :].rearrange("p (a n) -> p a n", a=FC)
        ctx.tt(o3, o3, bc_mid(b.ps, 4 * 512, FC, 512), ALU.mult)
        ctx.tt(o3, o3, ap_cols_rep(b.gat, 0, FC, 512), ALU.mult)
        for j in range(FC):
            jp = j % 2
            wload(ctx, "POOL", b.wo, jp * H, d.wa_o, FC, j,
                  "wlG" if jp == 0 else "wlH")
            pbk = 5 + jp
            for i in range(FC):
                wt = b.wo[:, jp * H + i * 128: jp * H + (i + 1) * 128]
                ctx.ldw(wt)
                ctx.mm(b.ps[:, pbk * 512:(pbk + 1) * 512], wt,
                       osb[:, i * 512:(i + 1) * 512],
                       start=(i == 0), stop=(i == FC - 1),
                       mark=("mm" if i == FC - 1 else None))
            ctx.dma("ACT", b.xr[jp][:, :],
                    d.x_t[:, (blk + 3) * 8192 + j * 512:
                          (blk + 3) * 8192 + (j + 1) * 512],
                    "ldC" if jp == 0 else "ldD")
            ctx.tt(b.ob[jp][:, :], b.xr[jp][:, :],
                   b.ps[:, pbk * 512:(pbk + 1) * 512], ALU.add)
            ctx.dma("SP", d.x1_t[:, blk * 8192 + j * 512:
                                 blk * 8192 + (j + 1) * 512],
                    b.ob[jp][:, :], "stA" if jp == 0 else "stB")


def ap3_m(cm):
    h = hdl(cm)
    return bass.AP(h, NCH, [[h.shape[1], 128], [1, NCH], [0, CH]])


def bc_mid(t, off, rep, n):
    h = hdl(t)
    return bass.AP(h, off, [[h.shape[1], 128], [0, rep], [1, n]])


def alloc_ffn_ln(ctx, es, cst, tag):
    nc = ctx.nc
    sb = nc.sbuf_tensor
    com = type("C", (), {})()
    com.ps = es.enter_context(nc.psum_tensor(tag + "_ps", [128, 7 * 512], F32))
    com.psb = es.enter_context(nc.psum_tensor(tag + "_psb", [128, 1024], BF16))
    com.tmpall = es.enter_context(sb(tag + "_tmpa", [128, FC * 512], F32))
    sets = []
    for par in range(2):
        b = type("B", (), {})()
        b.__dict__.update(cst.__dict__)
        t = tag + str(par)
        b.xblk = es.enter_context(sb(t + "_xblk", [128, FC * 512], F32))
        b.st16 = es.enter_context(sb(t + "_st16", [128, FC * 512], BF16))
        b.xln = es.enter_context(sb(t + "_xln", [128, FC * 512], BF16))
        b.rows = es.enter_context(sb(t + "_rows", [1, 3 * 512], F32))
        b.lgb = es.enter_context(sb(t + "_lgb", [16, 512], BF16))
        b.et64 = es.enter_context(sb(t + "_et64", [128, 64], F32))
        b.wk64 = es.enter_context(sb(t + "_wk64", [128, 64], F32))
        b.eq64 = es.enter_context(sb(t + "_eq64", [128, 64], F32))
        b.mxa = es.enter_context(sb(t + "_mxa", [128, 24], F32))
        b.cmb64 = es.enter_context(sb(t + "_cmb64", [128, 64], BF16))
        b.ps = com.ps
        b.psb = com.psb
        b.tmpall = com.tmpall
        b.lnb = (0, 512, 4 * 512, 5 * 512) if par == 0 else \
                (1024, 1536, 4 * 512, 5 * 512)
        b.ldsem = "ldA" if par == 0 else "ldB"
        b.stsem = "wlH" if par == 0 else "wlI"
        b.rtb = 6 * 512
        sets.append(b)
    return sets


def ln_s1(ctx, B, src, blk):
    s0, s1 = B.lnb[0], B.lnb[1]
    n = 512
    ctx.dma("SP", B.xblk[:, 0:FC * n],
            src[:, blk * 8192:blk * 8192 + FC * n], B.ldsem)
    ctx.act(B.st16[:, :FC * n], B.xblk[:, :FC * n], AF.Square)
    for a in range(FC):
        ctx.mm(B.ps[0:1, s1:s1 + n], B.ones_col[:, :],
               B.st16[:, a * n:(a + 1) * n],
               start=(a == 0), stop=(a == FC - 1),
               mark=("mm" if a == FC - 1 else None))
    ctx.act(B.st16[:, :FC * n], B.xblk[:, :FC * n], AF.Copy)
    for a in range(FC):
        ctx.mm(B.ps[0:1, s0:s0 + n], B.ones_col[:, :],
               B.st16[:, a * n:(a + 1) * n],
               start=(a == 0), stop=(a == FC - 1),
               mark=("mm" if a == FC - 1 else None))


def ln_s2(ctx, B):
    s0, s1, c_inv, c_mean = B.lnb
    rows = B.rows
    n = 512
    ctx.amul(rows[0:1, 0:n], B.ps[0:1, s0:s0 + n], 1.0 / H)
    ctx.amul(rows[0:1, n:2 * n], B.ps[0:1, s1:s1 + n], 1.0 / H)
    ctx.tt(rows[0:1, 2 * n:3 * n], rows[0:1, 0:n], rows[0:1, 0:n], ALU.mult)
    ctx.tt(rows[0:1, n:2 * n], rows[0:1, n:2 * n], rows[0:1, 2 * n:3 * n],
           ALU.subtract)
    ctx.act(rows[0:1, 2 * n:3 * n], rows[0:1, n:2 * n], AF.Ln, bias=LN_EPS)
    ctx.act(rows[0:1, n:2 * n], rows[0:1, 2 * n:3 * n], AF.Exp, scale=-0.5)
    ctx.tt(rows[0:1, 2 * n:3 * n], rows[0:1, 0:n], rows[0:1, n:2 * n],
           ALU.mult)
    ctx.mm(B.ps[:, c_inv:c_inv + n], B.ones_row[:, :], rows[0:1, n:2 * n],
           start=True, stop=True, mark="mm")
    ctx.mm(B.ps[:, c_mean:c_mean + n], B.ones_row[:, :],
           rows[0:1, 2 * n:3 * n], start=True, stop=True, mark="mm")


def ln_s3(ctx, B):
    c_inv, c_mean = B.lnb[2], B.lnb[3]
    x3 = B.xblk[:, :].rearrange("p (a n) -> p a n", a=FC)
    t3 = B.tmpall[:, :].rearrange("p (a n) -> p a n", a=FC)
    o3 = B.xln[:, :].rearrange("p (a n) -> p a n", a=FC)
    ctx.tt(t3, x3, bc_mid(B.ps, c_inv, FC, 512), ALU.mult)
    ctx.tt(o3, t3, bc_mid(B.ps, c_mean, FC, 512), ALU.subtract)


def ln_s4(ctx, B, dst, blk, router):
    ctx.dma("POOL", dst[:, blk * 8192:(blk + 1) * 8192],
            B.xln[:, :FC * 512], B.stsem)
    if not router:
        return
    for i in range(FC):
        ctx.mm(B.ps[0:16, B.rtb:B.rtb + 512], B.rw[:, i * 16:(i + 1) * 16],
               B.xln[:, i * 512:(i + 1) * 512],
               start=(i == 0), stop=(i == FC - 1),
               mark=("mm" if i == FC - 1 else None))
    ctx.act(B.lgb[:, :], B.ps[0:16, B.rtb:B.rtb + 512], AF.Copy)


def ln_s5(ctx, B, c0):
    for ts4 in range(4):
        ctx.tr(B.psb[0:128, ts4 * 16:(ts4 + 1) * 16],
               B.lgb[:, ts4 * 128:(ts4 + 1) * 128],
               B.ident[0:16, 0:16], mark="mm")
    ctx.act(B.et64[:, :], B.psb[0:128, 0:64], AF.Exp)
    ctx.act(B.wk64[:, :], B.et64[:, :], AF.Copy)
    w3 = B.wk64[:, :].rearrange("p (a n) -> p a n", a=4)
    e3 = B.eq64[:, :].rearrange("p (a n) -> p a n", a=4)
    for it in range(4):
        ctx.red(B.mxa[:, it * 4:(it + 1) * 4], w3, ALU.max)
        if it < 3:
            ctx.tt(e3, w3, ap_cols_rep(B.mxa, it * 4, 4, 16), ALU.is_equal)
            ctx.ts(B.eq64[:, :], B.eq64[:, :], -BIGNEG, ALU.mult)
            ctx.tt(B.wk64[:, :], B.wk64[:, :], B.eq64[:, :], ALU.add)
    t3 = B.et64[:, :].rearrange("p (a n) -> p a n", a=4)
    ctx.tt(e3, t3, ap_cols_rep(B.mxa, 12, 4, 16), ALU.is_ge)
    ctx.tt(B.eq64[:, :], B.et64[:, :], B.eq64[:, :], ALU.mult)
    ctx.red(B.mxa[:, 16:20], e3, ALU.add)
    ctx.ts(B.mxa[:, 16:20], B.mxa[:, 16:20], 1e-9, ALU.max)
    ctx.recip(B.mxa[:, 20:24], B.mxa[:, 16:20])
    c3 = B.cmb64[:, :].rearrange("p (a n) -> p a n", a=4)
    ctx.tt(c3, e3, ap_cols_rep(B.mxa, 20, 4, 16), ALU.mult)
    for ts4 in range(4):
        ctx.tr(B.psb[0:16, 512 + ts4 * 128:512 + (ts4 + 1) * 128],
               B.cmb64[:, ts4 * 16:(ts4 + 1) * 16],
               B.ident[:, :], mark="mm")
    ctx.act(B.comb_t[0:16, c0:c0 + 512], B.psb[0:16, 512:1024], AF.Copy)


def emit_ffn_ln(ctx, sets, d, src, n_tok, router=True, dst=None):
    if dst is None:
        dst = d.xln_t
    nblk = n_tok // 512
    ln_s1(ctx, sets[0], src, 0)
    if nblk > 1:
        ln_s1(ctx, sets[1], src, 1)
    for blk in range(nblk):
        B = sets[blk % 2]
        ln_s2(ctx, B)
        ln_s3(ctx, B)
        if blk + 2 < nblk:
            ln_s1(ctx, sets[blk % 2], src, blk + 2)
        ln_s4(ctx, B, dst, blk, router)
        if router:
            ln_s5(ctx, B, blk * 512)


def alloc_ffn_moe(ctx, es, cst, tag):
    nc = ctx.nc
    b = type("B", (), {})()
    b.__dict__.update(cst.__dict__)
    sb = nc.sbuf_tensor
    b.xg = es.enter_context(sb(tag + "_xg", [128, FC * 1024], BF16))
    b.acc = es.enter_context(sb(tag + "_acc", [128, FC * 1024], F32))
    b.big = es.enter_context(sb(tag + "_big", [128, JSH * 512], BF16))
    b.ws = es.enter_context(sb(tag + "_ws", [128, 4 * 2048], BF16))
    b.wsd = es.enter_context(sb(tag + "_wsd", [128, 2 * 22 * 128], BF16))
    b.cbc = es.enter_context(sb(tag + "_cbc", [128, 1024], BF16))
    b.sg1 = es.enter_context(sb(tag + "_sg1", [128, 4 * 512], BF16))
    b.sgb = es.enter_context(sb(tag + "_sgb", [128, 512], BF16))
    b.sgw = es.enter_context(sb(tag + "_sgw", [128, 16], BF16))
    b.tmp = es.enter_context(sb(tag + "_tmpm", [128, 1024], F32))
    b.ob = es.enter_context(sb(tag + "_ob", [128, 1024], F32))
    b.rows = es.enter_context(sb(tag + "_rowm", [1, 3 * 512], F32))
    b.ps = es.enter_context(nc.psum_tensor(tag + "_psm", [128, 8 * 512], F32))
    return b


def emit_ffn_moe(ctx, b, d, src, dst, n_tok):
    groups = []
    t0 = 0
    while t0 < n_tok:
        tg = min(1024, n_tok - t0)
        groups.append((t0, tg))
        t0 += tg
    ctx.dma("POOL", b.sgw[:, :], d.f_sgw[:, :], "wlG")

    def load_xg(gi):
        if gi >= len(groups):
            return
        g0n, tgn = groups[gi]
        for t2 in range(tgn // 512):
            ctx.dma("SP", b.xg[:, t2 * 8192:(t2 + 1) * 8192],
                    d.xln_t[:, (g0n // 512 + t2) * 8192:
                            (g0n // 512 + t2 + 1) * 8192],
                    "ldB" if t2 == 0 else "ldA")

    load_xg(0)
    for gi, (g0, tg) in enumerate(groups):
        nb = tg // 512
        for e in range(E):
            ctx.ldw(b.emat[:, e * 128:(e + 1) * 128])
            for t2 in range(nb):
                ctx.mm(b.ps[:, t2 * 512:(t2 + 1) * 512],
                       b.emat[:, e * 128:(e + 1) * 128],
                       b.comb_t[0:16, g0 + t2 * 512:g0 + (t2 + 1) * 512],
                       start=True, stop=True,
                       mark=("mm" if t2 == nb - 1 else None))
            ctx.act(b.cbc[:, 0:tg], b.ps[:, 0:tg], AF.Copy)
            for jj in range(JM):
                p = jj % 2
                wload(ctx, "POOL", b.ws, p * 2048, d.f_wg, FC, e * JM + jj,
                      "wlA" if p == 0 else "wlB")
                for i in range(FC):
                    ctx.ldw(b.ws[:, p * 2048 + i * 128: p * 2048 + (i + 1) * 128])
                    for t2 in range(nb):
                        ctx.mm(b.ps[:, (p * 2 + t2) * 512:(p * 2 + t2 + 1) * 512],
                               b.ws[:, p * 2048 + i * 128: p * 2048 + (i + 1) * 128],
                               b.xg[:, t2 * 8192 + i * 512: t2 * 8192 + (i + 1) * 512],
                               start=(i == 0), stop=(i == FC - 1),
                               mark=("mm" if (i == FC - 1 and t2 == nb - 1) else None))
                for t2 in range(nb):
                    ctx.act(b.sg1[:, (p * 2 + t2) * 512:(p * 2 + t2 + 1) * 512],
                            b.ps[:, (p * 2 + t2) * 512:(p * 2 + t2 + 1) * 512],
                            AF.Silu)
                wload(ctx, "POOL", b.ws, (2 + p) * 2048, d.f_wu, FC,
                      e * JM + jj, "wlC" if p == 0 else "wlD")
                for i in range(FC):
                    ctx.ldw(b.ws[:, (2 + p) * 2048 + i * 128:
                                 (2 + p) * 2048 + (i + 1) * 128])
                    for t2 in range(nb):
                        ctx.mm(b.ps[:, (4 + p * 2 + t2) * 512:
                                    (4 + p * 2 + t2 + 1) * 512],
                               b.ws[:, (2 + p) * 2048 + i * 128:
                                    (2 + p) * 2048 + (i + 1) * 128],
                               b.xg[:, t2 * 8192 + i * 512: t2 * 8192 + (i + 1) * 512],
                               start=(i == 0), stop=(i == FC - 1),
                               mark=("mm" if (i == FC - 1 and t2 == nb - 1) else None))
                for t2 in range(nb):
                    ctx.tt(b.tmp[:, t2 * 512:(t2 + 1) * 512],
                           b.sg1[:, (p * 2 + t2) * 512:(p * 2 + t2 + 1) * 512],
                           b.ps[:, (4 + p * 2 + t2) * 512:
                                (4 + p * 2 + t2 + 1) * 512], ALU.mult)
                    ctx.tt(b.big[:, t2 * 5632 + jj * 512: t2 * 5632 + (jj + 1) * 512],
                           b.tmp[:, t2 * 512:(t2 + 1) * 512],
                           b.cbc[:, t2 * 512:(t2 + 1) * 512], ALU.mult)
            for j2 in range(FC):
                p = j2 % 2
                wload(ctx, "POOL", b.ws, p * 2048, d.f_wd, JM, e * FC + j2,
                      "wlE" if p == 0 else "wlF")
                for k in range(JM):
                    ctx.ldw(b.ws[:, p * 2048 + k * 128: p * 2048 + (k + 1) * 128])
                    for t2 in range(nb):
                        ctx.mm(b.ps[:, (p * 2 + t2) * 512:(p * 2 + t2 + 1) * 512],
                               b.ws[:, p * 2048 + k * 128: p * 2048 + (k + 1) * 128],
                               b.big[:, t2 * 5632 + k * 512: t2 * 5632 + (k + 1) * 512],
                               start=(k == 0), stop=(k == JM - 1),
                               mark=("mm" if (k == JM - 1 and t2 == nb - 1) else None))
                for t2 in range(nb):
                    if e == 0:
                        ctx.act(b.acc[:, t2 * 8192 + j2 * 512: t2 * 8192 + (j2 + 1) * 512],
                                b.ps[:, (p * 2 + t2) * 512:(p * 2 + t2 + 1) * 512],
                                AF.Copy)
                    else:
                        ctx.tt(b.acc[:, t2 * 8192 + j2 * 512: t2 * 8192 + (j2 + 1) * 512],
                               b.acc[:, t2 * 8192 + j2 * 512: t2 * 8192 + (j2 + 1) * 512],
                               b.ps[:, (p * 2 + t2) * 512:(p * 2 + t2 + 1) * 512],
                               ALU.add)
        for t2 in range(nb):
            for i in range(FC):
                ctx.mm(b.ps[0:1, 6 * 512:6 * 512 + 512], b.sgw[:, i:i + 1],
                       b.xg[:, t2 * 8192 + i * 512: t2 * 8192 + (i + 1) * 512],
                       start=(i == 0), stop=(i == FC - 1),
                       mark=("mm" if i == FC - 1 else None))
            ctx.act(b.rows[0:1, 0:512], b.ps[0:1, 6 * 512:6 * 512 + 512],
                    AF.Exp, scale=-1.0)
            ctx.ts(b.rows[0:1, 512:1024], b.rows[0:1, 0:512], 1.0, ALU.add)
            ctx.recip(b.rows[0:1, 0:512], b.rows[0:1, 512:1024])
            ctx.mm(b.ps[:, 6 * 512:6 * 512 + 512], b.ones_row[:, :],
                   b.rows[0:1, 0:512], start=True, stop=True, mark="mm")
            ctx.act(b.sgb[:, :], b.ps[:, 6 * 512:6 * 512 + 512], AF.Copy)
            for jj in range(JSH):
                p = jj % 2
                wload(ctx, "POOL", b.ws, p * 2048, d.f_sg, FC, jj,
                      "wlA" if p == 0 else "wlB")
                for i in range(FC):
                    ctx.ldw(b.ws[:, p * 2048 + i * 128: p * 2048 + (i + 1) * 128])
                    ctx.mm(b.ps[:, p * 512:(p + 1) * 512],
                           b.ws[:, p * 2048 + i * 128: p * 2048 + (i + 1) * 128],
                           b.xg[:, t2 * 8192 + i * 512: t2 * 8192 + (i + 1) * 512],
                           start=(i == 0), stop=(i == FC - 1),
                           mark=("mm" if i == FC - 1 else None))
                ctx.act(b.sg1[:, p * 512:(p + 1) * 512],
                        b.ps[:, p * 512:(p + 1) * 512], AF.Silu)
                wload(ctx, "POOL", b.ws, (2 + p) * 2048, d.f_su, FC, jj,
                      "wlC" if p == 0 else "wlD")
                for i in range(FC):
                    ctx.ldw(b.ws[:, (2 + p) * 2048 + i * 128:
                                 (2 + p) * 2048 + (i + 1) * 128])
                    ctx.mm(b.ps[:, (2 + p) * 512:(3 + p) * 512],
                           b.ws[:, (2 + p) * 2048 + i * 128:
                                (2 + p) * 2048 + (i + 1) * 128],
                           b.xg[:, t2 * 8192 + i * 512: t2 * 8192 + (i + 1) * 512],
                           start=(i == 0), stop=(i == FC - 1),
                           mark=("mm" if i == FC - 1 else None))
                ctx.tt(b.big[:, jj * 512:(jj + 1) * 512],
                       b.sg1[:, p * 512:(p + 1) * 512],
                       b.ps[:, (2 + p) * 512:(3 + p) * 512], ALU.mult)
            if t2 == nb - 1:
                load_xg(gi + 1)
            for j2 in range(FC):
                p = j2 % 2
                for hf in range(2):
                    k0, k1 = (0, 22) if hf == 0 else (22, JSH)
                    ph = (j2 * 2 + hf) % 2
                    wload(ctx, "POOL", b.wsd, ph * 22 * 128, d.f_sd, JSH, j2,
                          "wlE" if ph == 0 else "wlF", k0=k0, k1=k1)
                    for k in range(k0, k1):
                        ctx.ldw(b.wsd[:, ph * 22 * 128 + (k - k0) * 128:
                                      ph * 22 * 128 + (k - k0 + 1) * 128])
                        ctx.mm(b.ps[:, (4 + p) * 512:(5 + p) * 512],
                               b.wsd[:, ph * 22 * 128 + (k - k0) * 128:
                                     ph * 22 * 128 + (k - k0 + 1) * 128],
                               b.big[:, k * 512:(k + 1) * 512],
                               start=(k == 0), stop=(k == JSH - 1),
                               mark=("mm" if k == JSH - 1 else None))
                ctx.tt(b.tmp[:, 0:512], b.ps[:, (4 + p) * 512:(5 + p) * 512],
                       b.sgb[:, :], ALU.mult)
                ctx.tt(b.acc[:, t2 * 8192 + j2 * 512: t2 * 8192 + (j2 + 1) * 512],
                       b.acc[:, t2 * 8192 + j2 * 512: t2 * 8192 + (j2 + 1) * 512],
                       b.tmp[:, 0:512], ALU.add)
        for j2 in range(FC):
            buf = b.ob if j2 % 2 == 0 else b.tmp
            for t2 in range(tg // 512):
                ctx.dma("ACT", buf[:, t2 * 512:(t2 + 1) * 512],
                        src[:, (g0 // 512 + t2) * 8192 + j2 * 512:
                            (g0 // 512 + t2) * 8192 + (j2 + 1) * 512],
                        "ldC" if j2 % 2 == 0 else "ldD")
                ctx.tt(buf[:, t2 * 512:(t2 + 1) * 512],
                       buf[:, t2 * 512:(t2 + 1) * 512],
                       b.acc[:, t2 * 8192 + j2 * 512:
                             t2 * 8192 + (j2 + 1) * 512], ALU.add)
                ctx.dma("SP", dst[:, (g0 // 512 + t2) * 8192 + j2 * 512:
                                  (g0 // 512 + t2) * 8192 + (j2 + 1) * 512],
                        buf[:, t2 * 512:(t2 + 1) * 512],
                        "stA" if j2 % 2 == 0 else "stB")


def alloc_ab(ctx, es, cst):
    nc = ctx.nc
    b = type("B", (), {})()
    b.__dict__.update(cst.__dict__)
    sb = nc.sbuf_tensor
    b.xblk = es.enter_context(sb("ab_xblk", [128, FC * 512], F32))
    b.xln = es.enter_context(sb("ab_xln", [128, FC * 512], BF16))
    b.q = es.enter_context(sb("ab_q", [128, FC * 512], BF16))
    b.st16 = b.q
    b.kring = es.enter_context(sb("ab_kring", [128, FC * 1024], BF16))
    b.vring = es.enter_context(sb("ab_vring", [128, 8 * H], BF16))
    b.osb = es.enter_context(sb("ab_osb", [128, NH * 512], BF16))
    b.ropec = es.enter_context(sb("ab_ropec", [128, T_MAIN], BF16))
    b.ropes = es.enter_context(sb("ab_ropes", [128, T_MAIN], BF16))
    b.pswap = es.enter_context(sb("ab_pswap", [128, 128], BF16))
    b.kdr = es.enter_context(sb("ab_kdr", [128, 512], BF16))
    b.bmask = es.enter_context(sb("ab_bmask", [128, 8 * 512], BF16))
    b.pt = es.enter_context(sb("ab_pt", [128, 2 * 512], BF16))
    b.rb = es.enter_context(sb("ab_rb", [128, 512], F32))
    b.t1 = es.enter_context(sb("ab_t1", [128, 512], F32))
    b.tmp = es.enter_context(sb("ab_tmp", [128, 512], F32))
    b.rows = es.enter_context(sb("ab_rows", [1, 3 * 512], F32))
    b.wsl = es.enter_context(sb("ab_wsl", [128, 4 * H], BF16))
    b.ob = es.enter_context(sb("ab_ob", [128, 512], F32))
    b.ps = es.enter_context(nc.psum_tensor("ab_ps", [128, 8 * 512], F32))
    return b


def emit_ab(ctx, b, d):
    ctx.dma("SP", b.ropec[:, :], d.ropec[:, :], "ldB")
    ctx.dma("SP", b.ropes[:, :], d.ropes[:, :], "ldC")
    ctx.dma("POOL", b.pswap[:, :], d.c_pswap[:, :], "wlI")
    ctx.dma("POOL", b.bmask[:, :], d.bmask[:, 8 * 512:16 * 512], "wlJ")

    def rope_apply(dst, psrc, swbank):
        ctx.act(b.kdr[:, :], psrc, AF.Copy)
        ctx.ldw(b.pswap[:, :])
        ctx.mm(b.ps[:, swbank * 512:(swbank + 1) * 512], b.pswap[:, :],
               b.kdr[:, :], start=True, stop=True, mark="mm")
        ctx.tt(b.t1[:, :], b.kdr[:, :], b.ropec[:, rc0:rc0 + 512], ALU.mult)
        ctx.tt(b.tmp[:, :], b.ps[:, swbank * 512:(swbank + 1) * 512],
               b.ropes[:, rc0:rc0 + 512], ALU.mult)
        ctx.tt(dst, b.t1[:, :], b.tmp[:, :], ALU.add)

    for pb in range(5):
        c0 = pb * 512
        rc0 = c0
        qb = pb - 1
        emit_ln(ctx, b, d.x2_t, c0, 512)
        rs = pb % 2
        for hh in range(NH):
            wload(ctx, "POOL", b.wsl, (hh % 2) * H, d.wb_k, FC, hh,
                  "wlA" if hh % 2 == 0 else "wlB")
            for i in range(FC):
                ctx.ldw(b.wsl[:, (hh % 2) * H + i * 128:(hh % 2) * H + (i + 1) * 128])
                ctx.mm(b.ps[:, 6 * 512:7 * 512],
                       b.wsl[:, (hh % 2) * H + i * 128:(hh % 2) * H + (i + 1) * 128],
                       b.xln[:, i * 512:(i + 1) * 512],
                       start=(i == 0), stop=(i == FC - 1),
                       mark=("mm" if i == FC - 1 else None))
            kd = b.kring[:, hh * 1024 + rs * 512: hh * 1024 + (rs + 1) * 512]
            rope_apply(kd, b.ps[:, 6 * 512:7 * 512], 5)
        for half in range(2):
            for i in range(FC):
                ctx.dma("POOL", b.wi_ab[:, (i % 2) * H:(i % 2) * H + 1024],
                        d.wb_v[i * 128:(i + 1) * 128,
                               half * 1024:(half + 1) * 1024],
                        "wlE" if i % 2 == 0 else "wlF")
                for ts4 in range(4):
                    xtile = b.xln[:, i * 512 + ts4 * 128:
                                  i * 512 + (ts4 + 1) * 128]
                    ctx.ldw(xtile)
                    for g2 in range(2):
                        bank = ts4 * 2 + g2
                        ctx.mm(b.ps[:, bank * 512:(bank + 1) * 512], xtile,
                               b.wi_ab[:, (i % 2) * H + g2 * 512:
                                       (i % 2) * H + (g2 + 1) * 512],
                               start=(i == 0), stop=(i == FC - 1),
                               mark=("mm" if (ts4 == 3 and g2 == 1) else None))
            for ts4 in range(4):
                for g2 in range(2):
                    bank = ts4 * 2 + g2
                    ng = half * 2 + g2
                    ctx.act(b.vring[:, (rs * 4 + ts4) * H + ng * 512:
                                    (rs * 4 + ts4) * H + (ng + 1) * 512],
                            b.ps[:, bank * 512:(bank + 1) * 512], AF.Copy)
        if pb == 0:
            continue
        for hh in range(NH):
            wload(ctx, "POOL", b.wsl, (2 + hh % 2) * H, d.wb_q, FC, hh,
                  "wlC" if hh % 2 == 0 else "wlD")
            for i in range(FC):
                ctx.ldw(b.wsl[:, (2 + hh % 2) * H + i * 128:
                              (2 + hh % 2) * H + (i + 1) * 128])
                ctx.mm(b.ps[:, 6 * 512:7 * 512],
                       b.wsl[:, (2 + hh % 2) * H + i * 128:
                             (2 + hh % 2) * H + (i + 1) * 128],
                       b.xln[:, i * 512:(i + 1) * 512],
                       start=(i == 0), stop=(i == FC - 1),
                       mark=("mm" if i == FC - 1 else None))
            qd = b.q[:, hh * 512:(hh + 1) * 512]
            rope_apply(qd, b.ps[:, 6 * 512:7 * 512], 5)
        for hh in range(NH):
            for cc in range(8):
                pbk = qb + (1 if cc >= 4 else 0)
                slot = pbk % 2
                ts = cc % 4
                ktile = b.kring[:, hh * 1024 + slot * 512 + ts * 128:
                                hh * 1024 + slot * 512 + (ts + 1) * 128]
                ctx.ldw(ktile)
                ctx.mm(b.ps[:, (cc % 2) * 512:(cc % 2 + 1) * 512],
                       ktile, b.q[:, hh * 512:(hh + 1) * 512],
                       start=True, stop=True, mark="mm")
                ctx.act(b.pt[:, (cc % 2) * 512:(cc % 2 + 1) * 512],
                        b.ps[:, (cc % 2) * 512:(cc % 2 + 1) * 512],
                        AF.Exp, scale=BQS)
                ctx.tt(b.pt[:, (cc % 2) * 512:(cc % 2 + 1) * 512],
                       b.pt[:, (cc % 2) * 512:(cc % 2 + 1) * 512],
                       b.bmask[:, cc * 512:(cc + 1) * 512], ALU.mult)
                ctx.mm(b.ps[0:1, 2 * 512:2 * 512 + 512], b.ones_col[:, :],
                       b.pt[:, (cc % 2) * 512:(cc % 2 + 1) * 512],
                       start=(cc == 0), stop=(cc == 7),
                       mark=("mm" if cc == 7 else None))
                vtile = b.vring[:, (slot * 4 + ts) * H + hh * 128:
                                (slot * 4 + ts) * H + (hh + 1) * 128]
                ctx.ldw(vtile)
                ctx.mm(b.ps[:, 3 * 512:4 * 512], vtile,
                       b.pt[:, (cc % 2) * 512:(cc % 2 + 1) * 512],
                       start=(cc == 0), stop=(cc == 7),
                       mark=("mm" if cc == 7 else None))
            ctx.recip(b.rows[0:1, 0:512], b.ps[0:1, 2 * 512:2 * 512 + 512])
            ctx.mm(b.ps[:, 4 * 512:5 * 512], b.ones_row[:, :],
                   b.rows[0:1, 0:512], start=True, stop=True, mark="mm")
            ctx.act(b.rb[:, :], b.ps[:, 4 * 512:5 * 512], AF.Copy)
            ctx.tt(b.osb[:, hh * 512:(hh + 1) * 512], b.ps[:, 3 * 512:4 * 512],
                   b.rb[:, :], ALU.mult)
        for j in range(FC):
            wload(ctx, "POOL", b.wsl, (j % 2) * H, d.wb_o, FC, j,
                  "wlG" if j % 2 == 0 else "wlH")
            for i in range(FC):
                ctx.ldw(b.wsl[:, (j % 2) * H + i * 128:(j % 2) * H + (i + 1) * 128])
                ctx.mm(b.ps[:, 6 * 512:7 * 512],
                       b.wsl[:, (j % 2) * H + i * 128:(j % 2) * H + (i + 1) * 128],
                       b.osb[:, i * 512:(i + 1) * 512],
                       start=(i == 0), stop=(i == FC - 1),
                       mark=("mm" if i == FC - 1 else None))
            obuf = b.ob if j % 2 == 0 else b.tmp
            ctx.tt(obuf[:, 0:512], b.xblk[:, j * 512:(j + 1) * 512],
                   b.ps[:, 6 * 512:7 * 512], ALU.add)
            ctx.dma("SP", d.x3_t[:, qb * 8192 + j * 512:
                                 qb * 8192 + (j + 1) * 512],
                    obuf[:, 0:512], "stA" if j % 2 == 0 else "stB")
        if pb == 1:
            ctx.dma("POOL", b.bmask[:, :], d.bmask[:, 0:8 * 512], "wlJ")


def build_program():
    nc = bass.Bass("TRN2", target_bir_lowering=False)
    d = type("D", (), {})()
    dt = nc.dram_tensor
    d.x_t = dt("x_t", [128, 8 * 8192], F32, kind="ExternalInput").ap()
    d.wa_q = dt("wa_q", [128, H * FC], F32, kind="ExternalInput").ap()
    d.wa_f = dt("wa_f", [128, H * FC], F32, kind="ExternalInput").ap()
    d.wa_i = dt("wa_i", [128, H * FC], F32, kind="ExternalInput").ap()
    d.wa_o = dt("wa_o", [128, H * FC], F32, kind="ExternalInput").ap()
    d.ga_t = dt("ga_t", [128, FC], F32, kind="ExternalInput").ap()
    d.wb_q = dt("wb_q", [128, H * FC], F32, kind="ExternalInput").ap()
    d.wb_k = dt("wb_k", [128, H * FC], F32, kind="ExternalInput").ap()
    d.wb_v = dt("wb_v", [H, H], F32, kind="ExternalInput").ap()
    d.wb_o = dt("wb_o", [128, H * FC], F32, kind="ExternalInput").ap()
    d.f_rw = dt("f_rw", [128, FC * 16], F32, kind="ExternalInput").ap()
    d.f_wg = dt("f_wg", [128, E * IM * FC], F32, kind="ExternalInput").ap()
    d.f_wu = dt("f_wu", [128, E * IM * FC], F32, kind="ExternalInput").ap()
    d.f_wd = dt("f_wd", [128, E * H * JM], F32, kind="ExternalInput").ap()
    d.f_sg = dt("f_sg", [128, ISH * FC], F32, kind="ExternalInput").ap()
    d.f_su = dt("f_su", [128, ISH * FC], F32, kind="ExternalInput").ap()
    d.f_sd = dt("f_sd", [128, H * JSH], F32, kind="ExternalInput").ap()
    d.f_sgw = dt("f_sgw", [128, FC], F32, kind="ExternalInput").ap()
    d.c_ident = dt("c_ident", [128, 128], BF16, kind="ExternalInput").ap()
    d.c_ones_col = dt("c_ones_col", [128, 1], BF16, kind="ExternalInput").ap()
    d.c_ones_row = dt("c_ones_row", [1, 128], F32, kind="ExternalInput").ap()
    d.c_maskaa = dt("c_maskaa", [128, 128], BF16, kind="ExternalInput").ap()
    d.c_emat = dt("c_emat", [16, 16 * 128], BF16, kind="ExternalInput").ap()
    d.c_pswap = dt("c_pswap", [128, 128], BF16, kind="ExternalInput").ap()
    d.c_bias = dt("c_bias", [128, 4], F32, kind="ExternalInput").ap()
    d.c_bias2 = dt("c_bias2", [128, 4], F32, kind="ExternalInput").ap()
    d.ropec = dt("ropec", [128, T_MAIN], BF16, kind="ExternalInput").ap()
    d.ropes = dt("ropes", [128, T_MAIN], BF16, kind="ExternalInput").ap()
    d.bmask = dt("bmask", [128, 16 * 512], BF16, kind="ExternalInput").ap()
    d.x1_t = dt("x1_t", [128, 5 * 8192], F32).ap()
    d.x2_t = dt("x2_t", [128, 5 * 8192], F32).ap()
    d.x3_t = dt("x3_t", [128, 4 * 8192], F32).ap()
    d.xln_t = dt("xln_t", [128, 8 * 8192], BF16).ap()
    d.osb_t = dt("osb_t", [128, 5 * 8192], BF16).ap()
    d.y_t = dt("y_t", [128, 4 * 8192], F32, kind="ExternalOutput").ap()

    with contextlib.ExitStack() as es0:
        sems = {k: es0.enter_context(nc.semaphore("s_" + k)) for k in SEMS}
        block = es0.enter_context(nc.Block())
        ctx = Ctx(nc, block, sems)

        cst = type("C", (), {})()
        cst.ident = es0.enter_context(nc.sbuf_tensor("c_ident_s", [128, 128], BF16))
        cst.ones_col = es0.enter_context(nc.sbuf_tensor("c_onec_s", [128, 1], BF16))
        cst.ones_row = es0.enter_context(nc.sbuf_tensor("c_oner_s", [1, 128], F32))
        cst.maskaa = es0.enter_context(nc.sbuf_tensor("c_maa_s", [128, 128], BF16))
        cst.emat = es0.enter_context(nc.sbuf_tensor("c_emat_s", [16, 16 * 128], BF16))
        cst.gat = es0.enter_context(nc.sbuf_tensor("c_gat_s", [128, FC], F32))
        cst.rw = es0.enter_context(nc.sbuf_tensor("c_rw_s", [128, FC * 16], BF16))
        cst.comb_t = es0.enter_context(nc.sbuf_tensor("c_combt", [16, T_MAIN], BF16))
        cst.cbias = es0.enter_context(nc.sbuf_tensor("c_cbias", [128, 4], F32))
        cst.cbias2 = es0.enter_context(nc.sbuf_tensor("c_cbias2", [128, 4], F32))
        nc.const_aps.register(F32, LN_EPS, cst.cbias[:, 0:1])
        nc.const_aps.register(F32, 1.0, cst.cbias[:, 1:2])
        nc.const_aps.register(F32, LNC, cst.cbias[:, 2:3])
        nc.const_aps.register(F32, LNC - SH, cst.cbias2[:, 0:1])
        nc.const_aps.register(F32, -SH, cst.cbias2[:, 1:2])
        nc.const_aps.register(F32, SH, cst.cbias2[:, 2:3])

        def ph_init(c):
            c.dma("SP", cst.ident[:, :], d.c_ident[:, :], "ldA")
            c.dma("SP", cst.ones_col[:, :], d.c_ones_col[:, :], "ldB")
            c.dma("SP", cst.ones_row[:, :], d.c_ones_row[:, :], "ldC")
            c.dma("SP", cst.maskaa[:, :], d.c_maskaa[:, :], "ldD")
            c.dma("SP", cst.emat[:, :], d.c_emat[:, :], "stA")
            c.dma("SP", cst.gat[:, :], d.ga_t[:, :], "stB")
            c.dma("SP", cst.cbias[:, :], d.c_bias[:, :], "stC")
            c.dma("POOL", cst.cbias2[:, :], d.c_bias2[:, :], "wlB")
            c.dma("POOL", cst.rw[:, :], d.f_rw[:, :], "wlA")

        run_phase(ctx, ph_init, "init")

        with contextlib.ExitStack() as es:
            b = alloc_ffn_ln(ctx, es, cst, "al")
            run_phase(ctx, lambda c: emit_ffn_ln(c, b, d, d.x_t, 8 * 512,
                                                 router=False), "aln")
        with contextlib.ExitStack() as es:
            b = alloc_aam(ctx, es, cst)
            run_phase(ctx, lambda c: emit_aam(c, b, d), "aam")
        with contextlib.ExitStack() as es:
            b = alloc_awo(ctx, es, cst)
            run_phase(ctx, lambda c: emit_awo(c, b, d), "awo")
        with contextlib.ExitStack() as es:
            b = alloc_ffn_ln(ctx, es, cst, "f1l")
            run_phase(ctx, lambda c: emit_ffn_ln(c, b, d, d.x1_t, T_MAIN), "f1l")
        with contextlib.ExitStack() as es:
            b = alloc_ffn_moe(ctx, es, cst, "f1m")
            run_phase(ctx, lambda c: emit_ffn_moe(c, b, d, d.x1_t, d.x2_t,
                                                  T_MAIN), "f1m")
        with contextlib.ExitStack() as es:
            b = alloc_ab(ctx, es, cst)
            b.wi_ab = b.wsl
            run_phase(ctx, lambda c: emit_ab(c, b, d), "ab")
        with contextlib.ExitStack() as es:
            b = alloc_ffn_ln(ctx, es, cst, "f2l")
            run_phase(ctx, lambda c: emit_ffn_ln(c, b, d, d.x3_t, OWN), "f2l")
        with contextlib.ExitStack() as es:
            b = alloc_ffn_moe(ctx, es, cst, "f2m")
            run_phase(ctx, lambda c: emit_ffn_moe(c, b, d, d.x3_t, d.y_t, OWN),
                      "f2m")

        def ph_fin(c):
            pass

        run_phase(ctx, ph_fin, "fin")
    return nc


def _rope_tables(positions):
    inv = 1.0 / (10000.0 ** (np.arange(0, DK, 2, dtype=np.float64) / DK))
    ang = inv[:, None] * positions[None, :].astype(np.float64)
    return np.cos(ang).astype(np.float32), np.sin(ang).astype(np.float32)


def _bmask(parity):
    out = np.zeros((16, 128, 512), dtype=np.float32)
    tk = np.arange(128)[:, None]
    tq = np.arange(512)[None, :]
    for c in range(8):
        dd = tq - tk - 128 * c + 512
        m = ((dd >= 0) & (dd < 512)).astype(np.float32)
        out[c] = m
        if parity == 1:
            out[8 + c] = m
        else:
            out[8 + c] = m if c >= 4 else 0.0
    return out.reshape(16 * 128, 512).astype(bfdt)


def _colchunk(w):
    K, D = w.shape
    return np.ascontiguousarray(
        w.reshape(K // 128, 128, D // 128, 128).transpose(1, 2, 0, 3)
        .reshape(128, -1))


def _blocked(xt):
    Hh, ncols = xt.shape
    nblk = ncols // 512
    return np.ascontiguousarray(
        xt.reshape(16, 128, nblk, 512).transpose(1, 2, 0, 3).reshape(128, -1))


def _perm_deinterleave():
    p = np.zeros(H, dtype=np.int64)
    for h in range(NH):
        base = h * DK
        p[base:base + 64] = base + 2 * np.arange(64)
        p[base + 64:base + DK] = base + 2 * np.arange(64) + 1
    return p


def make_in_maps(L0_attn_a_x, L0_attn_a_Wq, L0_attn_a_Wf, L0_attn_a_Wi,
                 L0_attn_a_Wo, L0_attn_a_g_weight, L0_attn_b_Wq, L0_attn_b_Wk,
                 L0_attn_b_Wv, L0_attn_b_Wo, L0_ffn_router_w, L0_ffn_w_gate,
                 L0_ffn_w_up, L0_ffn_w_down, L0_ffn_sh_gate, L0_ffn_sh_up,
                 L0_ffn_sh_down, L0_ffn_sh_gate_w):
    x = np.asarray(L0_attn_a_x, dtype=np.float32)
    B, S, _ = x.shape
    perm = _perm_deinterleave()

    wg_all = np.empty((H, E * IM), np.float32)
    wu_all = np.empty((H, E * IM), np.float32)
    wd_all = np.empty((IM, E * H), np.float32)
    for e in range(E):
        wg_all[:, e * IM:(e + 1) * IM] = np.asarray(L0_ffn_w_gate[e], np.float32)
        wu_all[:, e * IM:(e + 1) * IM] = np.asarray(L0_ffn_w_up[e], np.float32)
        wd_all[:, e * H:(e + 1) * H] = np.asarray(L0_ffn_w_down[e], np.float32)
    com = {
        "wa_q": _colchunk(np.asarray(L0_attn_a_Wq, dtype=np.float32)),
        "wa_f": _colchunk(np.asarray(L0_attn_a_Wf, dtype=np.float32)),
        "wa_i": _colchunk(np.asarray(L0_attn_a_Wi, dtype=np.float32)),
        "wa_o": _colchunk(np.asarray(L0_attn_a_Wo, dtype=np.float32)),
        "ga_t": np.ascontiguousarray(
            np.asarray(L0_attn_a_g_weight, dtype=np.float32).reshape(FC, 128).T),
        "wb_q": _colchunk(np.ascontiguousarray(
            np.asarray(L0_attn_b_Wq, dtype=np.float32).T[:, perm])),
        "wb_k": _colchunk(np.ascontiguousarray(
            np.asarray(L0_attn_b_Wk, dtype=np.float32).T[:, perm])),
        "wb_v": np.ascontiguousarray(np.asarray(L0_attn_b_Wv, dtype=np.float32).T),
        "wb_o": _colchunk(np.ascontiguousarray(
            np.asarray(L0_attn_b_Wo, dtype=np.float32).T)),
        "f_rw": np.ascontiguousarray(
            np.asarray(L0_ffn_router_w, dtype=np.float32).T
            .reshape(16, 128, 16).transpose(1, 0, 2).reshape(128, 256)),
        "f_wg": _colchunk(wg_all),
        "f_wu": _colchunk(wu_all),
        "f_wd": _colchunk(wd_all),
        "f_sg": _colchunk(np.ascontiguousarray(L0_ffn_sh_gate, dtype=np.float32)),
        "f_su": _colchunk(np.ascontiguousarray(L0_ffn_sh_up, dtype=np.float32)),
        "f_sd": _colchunk(np.ascontiguousarray(L0_ffn_sh_down, dtype=np.float32)),
        "f_sgw": np.ascontiguousarray(np.asarray(L0_ffn_sh_gate_w, dtype=np.float32).reshape(FC, 128).T),
        "c_ident": np.eye(128, dtype=np.float32).astype(bfdt),
        "c_ones_col": np.ones((128, 1), dtype=np.float32).astype(bfdt),
        "c_ones_row": np.ones((1, 128), dtype=np.float32),
        "c_maskaa": (SC_MASK * np.triu(np.ones((128, 128), dtype=np.float32))).astype(bfdt),
    }
    emat = np.zeros((16, 16 * 128), dtype=np.float32)
    for e in range(16):
        emat[e, e * 128:(e + 1) * 128] = 1.0
    com["c_emat"] = emat.astype(bfdt)
    psw = np.zeros((128, 128), dtype=np.float32)
    psw[(np.arange(128) + 64) % 128, np.arange(128)] = 1.0
    com["c_pswap"] = psw.astype(bfdt)
    com["c_bias"] = np.tile(np.array([[LN_EPS, 1.0, LNC, 0.0]], np.float32),
                            (128, 1))
    com["c_bias2"] = np.tile(np.array([[LNC - SH, -SH, SH, 0.0]], np.float32),
                             (128, 1))

    in_maps = []
    for c in range(8):
        bb, odd = c // 2, c % 2
        seq = x[bb]
        if odd:
            chain = seq
        else:
            chain = np.concatenate([np.zeros((2048, H), np.float32), seq[0:2048]],
                                   axis=0)
        m = dict(com)
        m["x_t"] = _blocked(np.ascontiguousarray(chain.T))
        start = 2048 * odd
        pos = np.arange(MAIN0, NT) - (0 if odd else 2048)
        pos = np.maximum(pos, 0)
        rc, rs = _rope_tables(pos)
        m["ropec"] = np.concatenate([rc, rc], axis=0).astype(bfdt)
        m["ropes"] = np.concatenate([-rs, rs], axis=0).astype(bfdt)
        m["bmask"] = np.ascontiguousarray(
            _bmask(odd).reshape(16, 128, 512).transpose(1, 0, 2)
            .reshape(128, 16 * 512))
        in_maps.append(m)
    return in_maps, (B, S)


def kernel(**inputs):
    in_maps, (B, S) = make_in_maps(**inputs)
    nc = build_program()
    res = run_bass_kernel_spmd(nc, in_maps, core_ids=list(range(8)))
    out = np.empty((B, S, H), dtype=np.float32)
    for c in range(8):
        bb, odd = c // 2, c % 2
        y = res.results[c]["y_t"].reshape(128, 4, 16, 512)
        out[bb, odd * 2048:(odd + 1) * 2048, :] = (
            y.transpose(1, 3, 2, 0).reshape(2048, 2048))
    return out


# version 09
# speedup vs baseline: 1.0660x; 1.0243x over previous
import contextlib
import math

import numpy as np
import ml_dtypes

import concourse.bass as bass
import concourse.mybir as mybir
from concourse.bass_utils import run_bass_kernel_spmd

F32 = mybir.dt.float32
BF16 = mybir.dt.bfloat16
AF = mybir.ActivationFunctionType
ALU = mybir.AluOpType
AX = mybir.AxisListType
bfdt = ml_dtypes.bfloat16

H = 2048
FC = 16
NH = 16
DK = 128
CH = 128
NCH = 4
NT = 4096
MAIN0 = 1536
T_MAIN = 2560
OWN0 = 2048
OWN = 2048
E = 16
IM = 1408
JM = 11
ISH = 5632
JSH = 44
LN_EPS = 1e-5
AA_EPS = 1e-5
LNC = math.log(DK ** -0.5)
BQS = 1.0 / math.sqrt(DK)
BIGNEG = 1.0e30
SH = 52 * math.log(2.0)
SC_MASK = 2.0 ** 104

ROLES = ["SP", "POOL", "PE", "ACT", "DVE"]
SEMS = (["ld" + c for c in "ABCD"] + ["st" + c for c in "ABC"] +
        ["wl" + c for c in "ABCDEFGHIJ"] + ["mm", "ev", "dv", "bar"])


def _region(ap):
    h = ap.tensor if hasattr(ap, "tensor") else ap
    if not hasattr(ap, "ap"):
        shp = h.shape
        return (h.name, 0, 128, 0, int(np.prod(shp)))
    name = h.name
    shp = list(h.shape)
    off = ap.offset
    dims = [list(x) for x in ap.ap]
    memtype = getattr(h, "__class__", None).__name__
    if memtype in ("SBTensorHandle", "PSumTensorHandle") and len(shp) == 2:
        rowlen = shp[1]
        p0 = off // rowlen
        c0 = off % rowlen
        pspan = 0
        cspan = 0
        for (st, cnt) in dims:
            if cnt <= 1:
                continue
            if st != 0 and st % rowlen == 0:
                pspan += (st // rowlen) * (cnt - 1)
            else:
                cspan += abs(st) * (cnt - 1)
        c1 = c0 + cspan + 1
        if memtype == "PSumTensorHandle":
            bank = 512 if h.dtype == mybir.dt.float32 else 1024
            c0 = (c0 // bank) * bank
            c1 = ((c1 + bank - 1) // bank) * bank
            return (name, 0, 128, c0, c1)
        return (name, p0, p0 + pspan + 1, c0, c1)
    span = 0
    for (st, cnt) in dims:
        if cnt > 1:
            span += abs(st) * (cnt - 1)
    return (name, 0, 1 << 30, off, off + span + 1)


def _overlap(a, b):
    return (a[0] == b[0] and a[1] < b[2] and b[1] < a[2]
            and a[3] < b[4] and b[3] < a[4])


class Ctx:
    ENG_SEM = {"PE": "mm", "ACT": "ev", "DVE": "dv"}

    def __init__(self, nc, block, sems):
        self.nc = nc
        self.block = block
        self.sem = sems
        self.cnt = {k: 0 for k in SEMS}
        self.role = None
        self.eng = None
        self.hiw = {}
        self.twr = {}
        self.trd = {}
        self.pe_pending = []
        self.dma_owner = {}

    def t_clear(self):
        self.twr.clear()
        self.trd.clear()
        self.pe_pending = []

    def _dep_read(self, ap, deps):
        r = _region(ap)
        for w in self.twr.get(r[0], ()):
            if _overlap(r, (r[0], w[0], w[1], w[2], w[3])):
                if w[5] is None:
                    raise RuntimeError(
                        f"read of {r[0]} depends on unmarked PE write")
                deps.append((w[4], w[5]))

    def _note_read(self, ap, sem, cntv):
        r = _region(ap)
        ent = [r[1], r[2], r[3], r[4], sem, cntv]
        self.trd.setdefault(r[0], []).append(ent)
        return ent

    def _dep_write(self, ap, deps, is_pe):
        r = _region(ap)
        ws = self.twr.get(r[0], ())
        for w in ws:
            if _overlap(r, (r[0], w[0], w[1], w[2], w[3])):
                if is_pe and w[6]:
                    continue
                if w[5] is None:
                    raise RuntimeError(
                        f"write of {r[0]} overlaps unmarked PE write")
                deps.append((w[4], w[5]))
        for rd in self.trd.get(r[0], ()):
            if _overlap(r, (r[0], rd[0], rd[1], rd[2], rd[3])):
                if rd[5] is None:
                    raise RuntimeError(
                        f"write of {r[0]} overlaps PE read not yet covered "
                        f"by a PE mark")
                deps.append((rd[4], rd[5]))

    def _note_write(self, ap, sem, cntv, is_pe):
        r = _region(ap)
        lst = self.twr.setdefault(r[0], [])
        lst[:] = [w for w in lst
                  if not (r[1] <= w[0] and w[1] <= r[2]
                          and r[3] <= w[2] and w[3] <= r[4])]
        ent = [r[1], r[2], r[3], r[4], sem, cntv, is_pe]
        lst.append(ent)
        rl = self.trd.get(r[0])
        if rl:
            rl[:] = [x for x in rl
                     if not (r[1] <= x[0] and x[1] <= r[2]
                             and r[3] <= x[2] and x[3] <= r[4])]
        return ent

    def _emit_op(self, engine, reads, writes, waits, mark, emit_fn,
                 is_dma=False):
        own = self.ENG_SEM.get(engine)
        deps = list(waits)
        for ap in reads:
            if ap is not None and not isinstance(ap, (int, float)):
                self._dep_read(ap, deps)
        for ap in writes:
            self._dep_write(ap, deps, engine == "PE")
        if is_dma:
            sem = mark
            prev = self.dma_owner.setdefault(sem, engine)
            assert prev == engine, f"dma sem {sem} used from {prev} and {engine}"
            self.cnt[sem] += 16
            cntv = self.cnt[sem]
        elif engine == "PE":
            if mark:
                self.cnt["mm"] += 1
                cntv = self.cnt["mm"]
            else:
                cntv = None
            sem = "mm"
        else:
            sem = own
            self.cnt[sem] += 1
            cntv = self.cnt[sem]
        for ap in reads:
            if ap is not None and not isinstance(ap, (int, float)):
                e = self._note_read(ap, sem,
                                    None if (engine == "PE" and not is_dma)
                                    else cntv)
                if engine == "PE" and not is_dma:
                    self.pe_pending.append(e)
        ents = [self._note_write(ap, sem, cntv, engine == "PE" and not is_dma)
                for ap in writes]
        if engine == "PE" and not is_dma:
            if cntv is None:
                self.pe_pending.extend(ents)
            else:
                for e in self.pe_pending:
                    if e[5] is None:
                        e[5] = cntv
                self.pe_pending = []
                for e in ents:
                    e[5] = cntv
        if self.role == engine:
            self.w(*[(s, v) for (s, v) in deps if v])
            ins = emit_fn()
            if is_dma:
                ins.then_inc(self.sem[sem], 16)
            elif engine == "PE":
                if cntv is not None:
                    ins.then_inc(self.sem[sem], 1)
            else:
                ins.then_inc(self.sem[sem], 1)
        return cntv if cntv is not None else self.cnt.get(sem, 0)

    def w(self, *waits):
        for (s, v) in waits:
            if v <= 0:
                continue
            if self.hiw.get(s, -1) >= v:
                continue
            self.eng.wait_ge(self.sem[s], v)
            self.hiw[s] = v

    def now(self):
        return [("mm", self.cnt["mm"]), ("ev", self.cnt["ev"]),
                ("dv", self.cnt["dv"])]

    def dma(self, q, out, in_, sem, waits=()):
        return self._emit_op(q, [in_], [out], waits, sem,
                             lambda: self.eng.dma_start(out=out, in_=in_),
                             is_dma=True)

    def ldw(self, wt, waits=()):
        deps = list(waits)
        self._dep_read(wt, deps)
        self.pe_pending.append(self._note_read(wt, "mm", None))
        if self.role == "PE":
            self.w(*[(s, v) for (s, v) in deps if v])
            self.eng.ldweights(wt)

    def mm(self, out, lhsT, rhs, start, stop, mark=None, waits=()):
        return self._emit_op(
            "PE", [lhsT, rhs], [out], waits, mark,
            lambda: self.eng.matmul(out, lhsT, rhs, start=start, stop=stop))

    def tr(self, out, in_, ident, mark=None, waits=()):
        return self._emit_op(
            "PE", [in_, ident], [out], waits, mark,
            lambda: self.eng.transpose(out, in_, ident))

    def act(self, out, in_, func=AF.Copy, bias=None, scale=None, mark=None,
            waits=()):
        kw = {}
        if bias is not None:
            kw["bias"] = bias
        if scale is not None:
            kw["scale"] = scale
        return self._emit_op(
            "ACT", [in_, bias if hasattr(bias, "tensor") else None], [out],
            waits, mark, lambda: self.eng.activation(out, in_, func, **kw))

    def amul(self, out, in_, mul, mark=None, waits=()):
        return self._emit_op(
            "ACT", [in_, mul if hasattr(mul, "tensor") else None], [out],
            waits, mark, lambda: self.eng.mul(out, in_, mul))

    def aadd(self, out, in_, add, mark=None, waits=()):
        return self._emit_op(
            "ACT", [in_, add if hasattr(add, "tensor") else None], [out],
            waits, mark, lambda: self.eng.add(out, in_, add))

    def tt(self, out, a, b, op, mark=None, waits=()):
        return self._emit_op(
            "DVE", [a, b], [out], waits, mark,
            lambda: self.eng.tensor_tensor(out, a, b, op))

    def ts(self, out, in0, s1, op0, mark=None, waits=()):
        return self._emit_op(
            "DVE", [in0, s1 if hasattr(s1, "tensor") else None], [out],
            waits, mark,
            lambda: self.eng.tensor_scalar(out, in0, s1, None, op0))

    def red(self, out, in_, op, mark=None, waits=()):
        return self._emit_op(
            "DVE", [in_], [out], waits, mark,
            lambda: self.eng.tensor_reduce(out, in_, AX.X, op))

    def scan(self, out, d0, d1, init, op0, op1, mark=None, waits=()):
        return self._emit_op(
            "DVE", [d0, d1], [out], waits, mark,
            lambda: self.eng.tensor_tensor_scan(out, d0, d1, init, op0, op1))

    def recip(self, out, in_, mark=None, waits=()):
        return self._emit_op(
            "DVE", [in_], [out], waits, mark,
            lambda: self.eng.reciprocal(out, in_))

    def sel(self, out, mask, on_true, on_false, mark=None, waits=()):
        return self._emit_op(
            "DVE", [mask, on_true, on_false], [out], waits, mark,
            lambda: self.eng.select(out, mask, on_true, on_false, add_drain=True))

    def vmemset(self, ap, val, mark=None, waits=()):
        return self._emit_op(
            "DVE", [], [ap], waits, mark,
            lambda: self.eng.memset(ap, val))

    def c(self, sem):
        return self.cnt[sem]


def run_phase(ctx, fn, name=""):
    start = dict(ctx.cnt)
    final = None
    for role in ROLES:
        ctx.cnt = dict(start)
        ctx.hiw = {}
        ctx.t_clear()
        ctx.dma_owner = {}
        ctx.role = role

        def body(eng):
            ctx.eng = eng
            fn(ctx)
            own = {"PE": ["mm"], "ACT": ["ev"], "DVE": ["dv"],
                   "SP": [], "POOL": []}[ctx.role]
            own = own + [s for (s, r) in sorted(ctx.dma_owner.items())
                         if r == ctx.role]
            for s in own:
                if ctx.cnt[s]:
                    eng.wait_ge(ctx.sem[s], ctx.cnt[s])
            tgt = ctx.cnt["bar"] + 5
            eng.nop().then_inc(ctx.sem["bar"], 1)
            eng.wait_ge(ctx.sem["bar"], tgt)
            ctx.cnt["bar"] = tgt

        getattr(ctx.block, {"SP": "sync", "POOL": "gpsimd", "PE": "tensor",
                            "ACT": "scalar", "DVE": "vector"}[role])(body)
        if final is None:
            final = dict(ctx.cnt)
        else:
            assert final == ctx.cnt, f"role divergence in {name}"
    ctx.cnt = final


def hdl(t):
    return t.tensor if hasattr(t, "tensor") else t


def ap_cols_rep(sb, col0, ncols, rep):
    h = hdl(sb)
    return bass.AP(h, col0, [[h.shape[1], 128], [1, ncols], [0, rep]])


def ap_stride(sb, off, step, count):
    h = hdl(sb)
    return bass.AP(h, off, [[h.shape[1], 128], [step, count]])


def wload(ctx, q, dst_sb, dst_free0, src, Ka, j, sem, k0=0, k1=None):
    if k1 is None:
        k1 = Ka
    s0 = j * Ka * 128 + k0 * 128
    return ctx.dma(q, dst_sb[:, dst_free0 + 0:dst_free0 + (k1 - k0) * 128],
                   src[:, s0:s0 + (k1 - k0) * 128], sem)


def emit_ln(ctx, b, xsrc, c0, ncols):
    ps, rows = b.ps, b.rows
    s0, s1, c_inv, c_mean = getattr(b, "lnb", (0, 512, 1024, 1536))
    ldsem = getattr(b, "ldsem", "ldA")
    n = ncols
    blk = c0 // 512
    ctx.dma("SP", b.xblk[:, 0:FC * n], xsrc[:, blk * 8192:blk * 8192 + FC * n],
            ldsem)
    ctx.act(b.st16[:, :FC * n], b.xblk[:, :FC * n], AF.Square)
    for a in range(FC):
        ctx.mm(ps[0:1, s1:s1 + n], b.ones_col[:, :],
               b.st16[:, a * n:(a + 1) * n],
               start=(a == 0), stop=(a == FC - 1),
               mark=("mm" if a == FC - 1 else None))
    ctx.act(b.st16[:, :FC * n], b.xblk[:, :FC * n], AF.Copy)
    for a in range(FC):
        ctx.mm(ps[0:1, s0:s0 + n], b.ones_col[:, :], b.st16[:, a * n:(a + 1) * n],
               start=(a == 0), stop=(a == FC - 1),
               mark=("mm" if a == FC - 1 else None))
    inv_h = 1.0 / H
    ctx.amul(rows[0:1, 0:n], ps[0:1, s0:s0 + n], inv_h)
    ctx.amul(rows[0:1, n:2 * n], ps[0:1, s1:s1 + n], inv_h)
    ctx.tt(rows[0:1, 2 * n:3 * n], rows[0:1, 0:n], rows[0:1, 0:n], ALU.mult)
    ctx.tt(rows[0:1, n:2 * n], rows[0:1, n:2 * n], rows[0:1, 2 * n:3 * n],
           ALU.subtract)
    ctx.act(rows[0:1, 2 * n:3 * n], rows[0:1, n:2 * n], AF.Ln, bias=LN_EPS)
    ctx.act(rows[0:1, n:2 * n], rows[0:1, 2 * n:3 * n], AF.Exp, scale=-0.5)
    ctx.tt(rows[0:1, 2 * n:3 * n], rows[0:1, 0:n], rows[0:1, n:2 * n],
           ALU.mult)
    ctx.mm(ps[:, c_inv:c_inv + n], b.ones_row[:, :], rows[0:1, n:2 * n],
           start=True, stop=True, mark="mm")
    ctx.mm(ps[:, c_mean:c_mean + n], b.ones_row[:, :], rows[0:1, 2 * n:3 * n],
           start=True, stop=True, mark="mm")
    dvv = 0
    for a in range(FC):
        ctx.tt(b.tmp[:, 0:n], b.xblk[:, a * n:(a + 1) * n],
               ps[:, c_inv:c_inv + n], ALU.mult)
        dvv = ctx.tt(b.xln[:, a * n:(a + 1) * n], b.tmp[:, 0:n],
                     ps[:, c_mean:c_mean + n], ALU.subtract)
    return dvv


def alloc_aam(ctx, es, cst):
    nc = ctx.nc
    sb = nc.sbuf_tensor
    b = type("B", (), {})()
    b.__dict__.update(cst.__dict__)
    b.xlnb = [es.enter_context(sb(f"aam_xln{i}", [128, FC * 512], BF16))
              for i in range(2)]
    b.wqf = es.enter_context(sb("aam_wqf", [128, 4 * 3 * H], BF16))
    b.Hst = es.enter_context(sb("aam_H", [128, NH * 128], F32))
    b.fs = [es.enter_context(sb(f"aam_fs{i}", [128, 1024], F32))
            for i in range(2)]
    b.qs = [es.enter_context(sb(f"aam_qs{i}", [128, 1024], F32))
            for i in range(2)]
    b.vfm = [es.enter_context(sb(f"aam_vfm{i}", [128, 1024], BF16))
             for i in range(2)]
    for nm in ["tmp", "tm2", "F1", "ks", "gs", "As"]:
        setattr(b, nm, es.enter_context(sb("aam_" + nm, [128, 1024], F32)))
    for nm in ["qt", "kt", "kh", "khT", "sm", "vtok"]:
        setattr(b, nm, es.enter_context(sb("aam_" + nm, [128, 1024], BF16)))
    b.Hb = es.enter_context(sb("aam_Hb", [128, 256], BF16))
    b.cm = es.enter_context(sb("aam_cm", [128, 32], F32))
    b.osbp = [es.enter_context(sb(f"aam_osbp{i}", [128, 512], BF16))
              for i in range(2)]
    b.ps = es.enter_context(nc.psum_tensor("aam_ps", [128, 7 * 512], F32))
    b.psb = es.enter_context(nc.psum_tensor("aam_psb", [128, 1024], BF16))
    return b


def aam_A(ctx, b, hh, hp, is_main, bs):
    wslot = (hp % 2) * (6 * H)
    hoff = hh * 2048
    ho5 = hh * 512

    def proj(woff, dst_ev, func):
        pbk = b.pb()
        for i in range(FC):
            wt = b.wqf[:, wslot + woff + i * 128:
                       wslot + woff + (i + 1) * 128]
            ctx.ldw(wt)
            ctx.mm(b.ps[:, pbk * 512:(pbk + 1) * 512], wt,
                   b.xlnb[bs][:, i * 512:(i + 1) * 512],
                   start=(i == 0), stop=(i == FC - 1),
                   mark=("mm" if i == FC - 1 else None))
        ctx.act(dst_ev, b.ps[:, pbk * 512:(pbk + 1) * 512], func)

    if is_main:
        proj(hoff, b.qs[bs][:, ho5:ho5 + 512], AF.Silu)
    proj(4096 + hoff, b.fs[bs][:, ho5:ho5 + 512], AF.Copy)
    proj(8192 + hoff, b.vfm[bs][:, ho5:ho5 + 512], AF.Copy)


def aam_B(ctx, b, is_main, bs):
    fs, qs = b.fs[bs], b.qs[bs]
    ctx.act(b.tmp[:, :], fs[:, :], AF.Abs)
    ctx.act(b.tmp[:, :], b.tmp[:, :], AF.Exp, scale=-1.0)
    ctx.act(b.tm2[:, :], b.tmp[:, :], AF.Ln, bias=1.0)
    ctx.ts(b.gs[:, :], fs[:, :], 0.0, ALU.min)
    ctx.tt(b.gs[:, :], b.gs[:, :], b.tm2[:, :], ALU.subtract)
    for a in range(8):
        ctx.scan(b.As[:, a * CH:(a + 1) * CH],
                 b.gs[:, a * CH:(a + 1) * CH],
                 b.gs[:, a * CH:(a + 1) * CH], 0.0, ALU.add, ALU.bypass)
    for hh in range(2):
        co = hh * 16
        ACm = b.cm[:, co:co + 4]
        mcol = b.cm[:, co + 4:co + 8]
        ctx.act(ACm, ap_stride(b.As, hh * 512 + CH - 1, CH, NCH), AF.Copy)
        ctx.amul(mcol, ACm, 0.5)
        ctx.act(b.cm[:, co + 8:co + 12], ACm, AF.Exp)
        ctx.act(b.cm[:, co + 12:co + 16], mcol, AF.Exp, bias=SH)
        t3 = b.tmp[:, hh * 512:(hh + 1) * 512].rearrange(
            "p (c n) -> p c n", c=NCH)
        a3 = b.As[:, hh * 512:(hh + 1) * 512].rearrange(
            "p (c n) -> p c n", c=NCH)
        ctx.tt(t3, a3, ap_cols_rep(b.cm, co + 4, NCH, CH), ALU.subtract)
    if is_main:
        ctx.act(b.F1[:, :], b.tmp[:, :], AF.Exp, bias=LNC - SH)
        ctx.tt(b.qt[:, :], qs[:, :], b.F1[:, :], ALU.mult)
    ctx.act(b.F1[:, :], b.tmp[:, :], AF.Exp, scale=-1.0, bias=-SH)
    ctx.act(b.tm2[:, :], b.gs[:, :], AF.Exp)
    ctx.act(b.ks[:, :], b.tm2[:, :], AF.Copy, scale=-1.0, bias=1.0)
    ctx.tt(b.kt[:, :], b.ks[:, :], b.F1[:, :], ALU.mult)
    for hh in range(2):
        co = hh * 16
        k3 = b.kh[:, hh * 512:(hh + 1) * 512].rearrange(
            "p (c n) -> p c n", c=NCH)
        kt3 = b.kt[:, hh * 512:(hh + 1) * 512].rearrange(
            "p (c n) -> p c n", c=NCH)
        ctx.tt(k3, kt3, ap_cols_rep(b.cm, co + 12, NCH, CH), ALU.mult)


def aam_C(ctx, b, d, hp, blk, is_main, bs):
    for hh in range(2):
        for cc in range(NCH):
            sl = (hh * 4 + cc) * 128
            ctx.tr(b.psb[:, sl:sl + 128],
                   b.vfm[bs][:, sl:sl + 128], b.ident[:, :], mark="mm")
            ctx.act(b.vtok[:, sl:sl + 128], b.psb[:, sl:sl + 128], AF.Copy)
    for hh in range(2):
        for cc in range(NCH):
            sl = (hh * 4 + cc) * 128
            ctx.tr(b.psb[:, sl:sl + 128],
                   b.kh[:, sl:sl + 128], b.ident[:, :], mark="mm")
            ctx.act(b.khT[:, sl:sl + 128], b.psb[:, sl:sl + 128], AF.Copy)
    for cc in range(NCH):
        for hh in range(2):
            h = hp * 2 + hh
            o = hh * 512 + cc * 128
            co = hh * 16
            Hh = b.Hst[:, h * 128:(h + 1) * 128]
            vt = b.vtok[:, o:o + 128]
            ao = (2 + hh) * 512
            if is_main:
                ctx.amul(b.Hb[:, hh * 128:(hh + 1) * 128], Hh,
                         b.cm[:, co + 12 + cc:co + 13 + cc])
                ctx.ldw(b.Hb[:, hh * 128:(hh + 1) * 128])
                ctx.mm(b.ps[:, ao:ao + 128], b.Hb[:, hh * 128:(hh + 1) * 128],
                       b.qt[:, o:o + 128], start=True, stop=False)
                ctx.ldw(b.kt[:, o:o + 128])
                ctx.mm(b.ps[:, 4 * 512:4 * 512 + 128], b.kt[:, o:o + 128],
                       b.qt[:, o:o + 128], start=True, stop=True, mark="mm")
                ctx.tt(b.sm[:, o:o + 128], b.ps[:, 4 * 512:4 * 512 + 128],
                       b.maskaa[:, :], ALU.mult)
                ctx.ldw(vt)
                ctx.mm(b.ps[:, ao:ao + 128], vt, b.sm[:, o:o + 128],
                       start=False, stop=True, mark="mm")
                ctx.act(b.osbp[hh][:, cc * 128:(cc + 1) * 128],
                        b.ps[:, ao:ao + 128], AF.Copy)
            ctx.ldw(b.khT[:, o:o + 128])
            ctx.mm(b.ps[:, hh * 512:hh * 512 + 128], b.khT[:, o:o + 128],
                   vt, start=True, stop=True, mark="mm")
            ctx.amul(Hh, Hh, b.cm[:, co + 8 + cc:co + 9 + cc])
            ctx.tt(Hh, Hh, b.ps[:, hh * 512:hh * 512 + 128], ALU.add)
    if is_main:
        for hh in range(2):
            h = hp * 2 + hh
            ctx.dma("SP", d.osb_t[:, (blk - 3) * 8192 + h * 512:
                                  (blk - 3) * 8192 + (h + 1) * 512],
                    b.osbp[hh][:, :], "stA" if hh == 0 else "stB")


def emit_aam(ctx, b, d):
    ctx.vmemset(b.Hst[:, :], 0.0)
    pbc = [0]

    def pb():
        pbc[0] ^= 1
        return 5 + pbc[0]
    b.pb = pb

    def load_w(hp):
        wslot = (hp % 2) * (6 * H)
        for hh in range(2):
            h = hp * 2 + hh
            wload(ctx, "POOL", b.wqf, wslot + hh * 2048, d.wa_q, FC, h,
                  "wlA" if hp % 2 == 0 else "wlB")
            wload(ctx, "POOL", b.wqf, wslot + 4096 + hh * 2048, d.wa_f, FC, h,
                  "wlC" if hp % 2 == 0 else "wlD")
            wload(ctx, "POOL", b.wqf, wslot + 8192 + hh * 2048, d.wa_i, FC, h,
                  "wlE" if hp % 2 == 0 else "wlF")

    ctx.dma("SP", b.xlnb[0][:, :], d.xln_t[:, 0:8192], "ldA")
    slots = [(hp, blk) for hp in range(NH // 2) for blk in range(8)]
    prev = None
    for si, (hp, blk) in enumerate(slots):
        if blk == 0:
            load_w(hp)
        is_main = blk >= 3
        bs = blk % 2
        for hh in range(2):
            aam_A(ctx, b, hh, hp, is_main, bs)
        if si + 1 < len(slots):
            nxt = slots[si + 1][1]
            ctx.dma("SP", b.xlnb[nxt % 2][:, :],
                    d.xln_t[:, nxt * 8192:(nxt + 1) * 8192],
                    "ldA" if nxt % 2 == 0 else "ldB")
        if prev is not None:
            aam_C(ctx, b, d, prev[0], prev[1], prev[1] >= 3, prev[1] % 2)
        aam_B(ctx, b, is_main, bs)
        prev = (hp, blk)
    aam_C(ctx, b, d, prev[0], prev[1], True, prev[1] % 2)


def alloc_awo(ctx, es, cst):
    nc = ctx.nc
    sb = nc.sbuf_tensor
    b = type("B", (), {})()
    b.__dict__.update(cst.__dict__)
    b.osb = [es.enter_context(sb(f"awo_osb{i}", [128, FC * 512], BF16))
             for i in range(2)]
    b.st16 = es.enter_context(sb("awo_st16", [128, FC * 512], BF16))
    b.wo = es.enter_context(sb("awo_wo", [128, 2 * H], BF16))
    b.xr = [es.enter_context(sb(f"awo_xr{i}", [128, 512], F32))
            for i in range(2)]
    b.ob = [es.enter_context(sb(f"awo_ob{i}", [128, 512], F32))
            for i in range(2)]
    b.rows = es.enter_context(sb("awo_rows", [1, 2 * 512], F32))
    b.ps = es.enter_context(nc.psum_tensor("awo_ps", [128, 7 * 512], F32))
    return b


def emit_awo(ctx, b, d):
    ctx.dma("SP", b.osb[0][:, :], d.osb_t[:, 0:8192], "ldA")
    for blk in range(5):
        par = blk % 2
        if blk + 1 < 5:
            ctx.dma("SP", b.osb[(blk + 1) % 2][:, :],
                    d.osb_t[:, (blk + 1) * 8192:(blk + 2) * 8192],
                    "ldA" if (blk + 1) % 2 == 0 else "ldB")
        osb = b.osb[par]
        ctx.act(b.st16[:, :], osb[:, :], AF.Square)
        for a in range(FC):
            ctx.mm(b.ps[0:1, 0:512], b.ones_col[:, :],
                   b.st16[:, a * 512:(a + 1) * 512],
                   start=(a == 0), stop=(a == FC - 1),
                   mark=("mm" if a == FC - 1 else None))
        ctx.amul(b.rows[0:1, 0:512], b.ps[0:1, 0:512], 1.0 / H)
        ctx.act(b.rows[0:1, 512:1024], b.rows[0:1, 0:512], AF.Ln, bias=AA_EPS)
        ctx.act(b.rows[0:1, 0:512], b.rows[0:1, 512:1024], AF.Exp, scale=-0.5)
        ctx.mm(b.ps[:, 4 * 512:5 * 512], b.ones_row[:, :],
               b.rows[0:1, 0:512], start=True, stop=True, mark="mm")
        o3 = osb[:, :].rearrange("p (a n) -> p a n", a=FC)
        ctx.tt(o3, o3, bc_mid(b.ps, 4 * 512, FC, 512), ALU.mult)
        ctx.tt(o3, o3, ap_cols_rep(b.gat, 0, FC, 512), ALU.mult)
        for j in range(FC):
            jp = j % 2
            wload(ctx, "POOL", b.wo, jp * H, d.wa_o, FC, j,
                  "wlG" if jp == 0 else "wlH")
            pbk = 5 + jp
            for i in range(FC):
                wt = b.wo[:, jp * H + i * 128: jp * H + (i + 1) * 128]
                ctx.ldw(wt)
                ctx.mm(b.ps[:, pbk * 512:(pbk + 1) * 512], wt,
                       osb[:, i * 512:(i + 1) * 512],
                       start=(i == 0), stop=(i == FC - 1),
                       mark=("mm" if i == FC - 1 else None))
            ctx.dma("ACT", b.xr[jp][:, :],
                    d.x_t[:, (blk + 3) * 8192 + j * 512:
                          (blk + 3) * 8192 + (j + 1) * 512],
                    "ldC" if jp == 0 else "ldD")
            ctx.tt(b.ob[jp][:, :], b.xr[jp][:, :],
                   b.ps[:, pbk * 512:(pbk + 1) * 512], ALU.add)
            ctx.dma("SP", d.x1_t[:, blk * 8192 + j * 512:
                                 blk * 8192 + (j + 1) * 512],
                    b.ob[jp][:, :], "stA" if jp == 0 else "stB")


def ap3_m(cm):
    h = hdl(cm)
    return bass.AP(h, NCH, [[h.shape[1], 128], [1, NCH], [0, CH]])


def bc_mid(t, off, rep, n):
    h = hdl(t)
    return bass.AP(h, off, [[h.shape[1], 128], [0, rep], [1, n]])


def alloc_ffn_ln(ctx, es, cst, tag):
    nc = ctx.nc
    sb = nc.sbuf_tensor
    com = type("C", (), {})()
    com.ps = es.enter_context(nc.psum_tensor(tag + "_ps", [128, 7 * 512], F32))
    com.psb = es.enter_context(nc.psum_tensor(tag + "_psb", [128, 1024], BF16))
    com.tmpall = es.enter_context(sb(tag + "_tmpa", [128, FC * 512], F32))
    sets = []
    for par in range(2):
        b = type("B", (), {})()
        b.__dict__.update(cst.__dict__)
        t = tag + str(par)
        b.xblk = es.enter_context(sb(t + "_xblk", [128, FC * 512], F32))
        b.st16 = es.enter_context(sb(t + "_st16", [128, FC * 512], BF16))
        b.xln = es.enter_context(sb(t + "_xln", [128, FC * 512], BF16))
        b.rows = es.enter_context(sb(t + "_rows", [1, 3 * 512], F32))
        b.lgb = es.enter_context(sb(t + "_lgb", [16, 512], BF16))
        b.et64 = es.enter_context(sb(t + "_et64", [128, 64], F32))
        b.wk64 = es.enter_context(sb(t + "_wk64", [128, 64], F32))
        b.eq64 = es.enter_context(sb(t + "_eq64", [128, 64], F32))
        b.mxa = es.enter_context(sb(t + "_mxa", [128, 24], F32))
        b.cmb64 = es.enter_context(sb(t + "_cmb64", [128, 64], BF16))
        b.ps = com.ps
        b.psb = com.psb
        b.tmpall = com.tmpall
        b.lnb = (0, 512, 4 * 512, 5 * 512) if par == 0 else \
                (1024, 1536, 4 * 512, 5 * 512)
        b.ldsem = "ldA" if par == 0 else "ldB"
        b.stsem = "wlH" if par == 0 else "wlI"
        b.rtb = 6 * 512
        sets.append(b)
    return sets


def ln_s1(ctx, B, src, blk):
    s0, s1 = B.lnb[0], B.lnb[1]
    n = 512
    ctx.dma("SP", B.xblk[:, 0:FC * n],
            src[:, blk * 8192:blk * 8192 + FC * n], B.ldsem)
    ctx.act(B.st16[:, :FC * n], B.xblk[:, :FC * n], AF.Square)
    for a in range(FC):
        ctx.mm(B.ps[0:1, s1:s1 + n], B.ones_col[:, :],
               B.st16[:, a * n:(a + 1) * n],
               start=(a == 0), stop=(a == FC - 1),
               mark=("mm" if a == FC - 1 else None))
    ctx.act(B.st16[:, :FC * n], B.xblk[:, :FC * n], AF.Copy)
    for a in range(FC):
        ctx.mm(B.ps[0:1, s0:s0 + n], B.ones_col[:, :],
               B.st16[:, a * n:(a + 1) * n],
               start=(a == 0), stop=(a == FC - 1),
               mark=("mm" if a == FC - 1 else None))


def ln_s2(ctx, B):
    s0, s1, c_inv, c_mean = B.lnb
    rows = B.rows
    n = 512
    ctx.amul(rows[0:1, 0:n], B.ps[0:1, s0:s0 + n], 1.0 / H)
    ctx.amul(rows[0:1, n:2 * n], B.ps[0:1, s1:s1 + n], 1.0 / H)
    ctx.tt(rows[0:1, 2 * n:3 * n], rows[0:1, 0:n], rows[0:1, 0:n], ALU.mult)
    ctx.tt(rows[0:1, n:2 * n], rows[0:1, n:2 * n], rows[0:1, 2 * n:3 * n],
           ALU.subtract)
    ctx.act(rows[0:1, 2 * n:3 * n], rows[0:1, n:2 * n], AF.Ln, bias=LN_EPS)
    ctx.act(rows[0:1, n:2 * n], rows[0:1, 2 * n:3 * n], AF.Exp, scale=-0.5)
    ctx.tt(rows[0:1, 2 * n:3 * n], rows[0:1, 0:n], rows[0:1, n:2 * n],
           ALU.mult)
    ctx.mm(B.ps[:, c_inv:c_inv + n], B.ones_row[:, :], rows[0:1, n:2 * n],
           start=True, stop=True, mark="mm")
    ctx.mm(B.ps[:, c_mean:c_mean + n], B.ones_row[:, :],
           rows[0:1, 2 * n:3 * n], start=True, stop=True, mark="mm")


def ln_s3(ctx, B):
    c_inv, c_mean = B.lnb[2], B.lnb[3]
    x3 = B.xblk[:, :].rearrange("p (a n) -> p a n", a=FC)
    t3 = B.tmpall[:, :].rearrange("p (a n) -> p a n", a=FC)
    o3 = B.xln[:, :].rearrange("p (a n) -> p a n", a=FC)
    ctx.tt(t3, x3, bc_mid(B.ps, c_inv, FC, 512), ALU.mult)
    ctx.tt(o3, t3, bc_mid(B.ps, c_mean, FC, 512), ALU.subtract)


def ln_s4(ctx, B, dst, blk, router):
    ctx.dma("POOL", dst[:, blk * 8192:(blk + 1) * 8192],
            B.xln[:, :FC * 512], B.stsem)
    if not router:
        return
    for i in range(FC):
        ctx.mm(B.ps[0:16, B.rtb:B.rtb + 512], B.rw[:, i * 16:(i + 1) * 16],
               B.xln[:, i * 512:(i + 1) * 512],
               start=(i == 0), stop=(i == FC - 1),
               mark=("mm" if i == FC - 1 else None))
    ctx.act(B.lgb[:, :], B.ps[0:16, B.rtb:B.rtb + 512], AF.Copy)


def ln_s5(ctx, B, c0):
    for ts4 in range(4):
        ctx.tr(B.psb[0:128, ts4 * 16:(ts4 + 1) * 16],
               B.lgb[:, ts4 * 128:(ts4 + 1) * 128],
               B.ident[0:16, 0:16], mark="mm")
    ctx.act(B.et64[:, :], B.psb[0:128, 0:64], AF.Exp)
    ctx.act(B.wk64[:, :], B.et64[:, :], AF.Copy)
    w3 = B.wk64[:, :].rearrange("p (a n) -> p a n", a=4)
    e3 = B.eq64[:, :].rearrange("p (a n) -> p a n", a=4)
    for it in range(4):
        ctx.red(B.mxa[:, it * 4:(it + 1) * 4], w3, ALU.max)
        if it < 3:
            ctx.tt(e3, w3, ap_cols_rep(B.mxa, it * 4, 4, 16), ALU.is_equal)
            ctx.ts(B.eq64[:, :], B.eq64[:, :], -BIGNEG, ALU.mult)
            ctx.tt(B.wk64[:, :], B.wk64[:, :], B.eq64[:, :], ALU.add)
    t3 = B.et64[:, :].rearrange("p (a n) -> p a n", a=4)
    ctx.tt(e3, t3, ap_cols_rep(B.mxa, 12, 4, 16), ALU.is_ge)
    ctx.tt(B.eq64[:, :], B.et64[:, :], B.eq64[:, :], ALU.mult)
    ctx.red(B.mxa[:, 16:20], e3, ALU.add)
    ctx.ts(B.mxa[:, 16:20], B.mxa[:, 16:20], 1e-9, ALU.max)
    ctx.recip(B.mxa[:, 20:24], B.mxa[:, 16:20])
    c3 = B.cmb64[:, :].rearrange("p (a n) -> p a n", a=4)
    ctx.tt(c3, e3, ap_cols_rep(B.mxa, 20, 4, 16), ALU.mult)
    for ts4 in range(4):
        ctx.tr(B.psb[0:16, 512 + ts4 * 128:512 + (ts4 + 1) * 128],
               B.cmb64[:, ts4 * 16:(ts4 + 1) * 16],
               B.ident[:, :], mark="mm")
    ctx.act(B.comb_t[0:16, c0:c0 + 512], B.psb[0:16, 512:1024], AF.Copy)


def emit_ffn_ln(ctx, sets, d, src, n_tok, router=True, dst=None):
    if dst is None:
        dst = d.xln_t
    nblk = n_tok // 512
    ln_s1(ctx, sets[0], src, 0)
    if nblk > 1:
        ln_s1(ctx, sets[1], src, 1)
    for blk in range(nblk):
        B = sets[blk % 2]
        ln_s2(ctx, B)
        ln_s3(ctx, B)
        if blk + 2 < nblk:
            ln_s1(ctx, sets[blk % 2], src, blk + 2)
        ln_s4(ctx, B, dst, blk, router)
        if router:
            ln_s5(ctx, B, blk * 512)


def alloc_ffn_moe(ctx, es, cst, tag):
    nc = ctx.nc
    b = type("B", (), {})()
    b.__dict__.update(cst.__dict__)
    sb = nc.sbuf_tensor
    b.xg = es.enter_context(sb(tag + "_xg", [128, FC * 1024], BF16))
    b.acc = es.enter_context(sb(tag + "_acc", [128, FC * 1024], F32))
    b.big = es.enter_context(sb(tag + "_big", [128, JSH * 512], BF16))
    b.ws = es.enter_context(sb(tag + "_ws", [128, 4 * 2048], BF16))
    b.wsd = es.enter_context(sb(tag + "_wsd", [128, 2 * 22 * 128], BF16))
    b.cbc = es.enter_context(sb(tag + "_cbc", [128, 1024], BF16))
    b.sg1 = es.enter_context(sb(tag + "_sg1", [128, 4 * 512], BF16))
    b.sgb = es.enter_context(sb(tag + "_sgb", [128, 512], BF16))
    b.sgw = es.enter_context(sb(tag + "_sgw", [128, 16], BF16))
    b.tmp = es.enter_context(sb(tag + "_tmpm", [128, 1024], F32))
    b.ob = es.enter_context(sb(tag + "_ob", [128, 1024], F32))
    b.rows = es.enter_context(sb(tag + "_rowm", [1, 3 * 512], F32))
    b.ps = es.enter_context(nc.psum_tensor(tag + "_psm", [128, 8 * 512], F32))
    return b


def emit_ffn_moe(ctx, b, d, src, dst, n_tok):
    groups = []
    t0 = 0
    while t0 < n_tok:
        tg = min(1024, n_tok - t0)
        groups.append((t0, tg))
        t0 += tg
    ctx.dma("POOL", b.sgw[:, :], d.f_sgw[:, :], "wlG")

    def load_xg(gi):
        if gi >= len(groups):
            return
        g0n, tgn = groups[gi]
        for t2 in range(tgn // 512):
            ctx.dma("SP", b.xg[:, t2 * 8192:(t2 + 1) * 8192],
                    d.xln_t[:, (g0n // 512 + t2) * 8192:
                            (g0n // 512 + t2 + 1) * 8192],
                    "ldB" if t2 == 0 else "ldA")

    load_xg(0)
    for gi, (g0, tg) in enumerate(groups):
        nb = tg // 512
        for e in range(E):
            ctx.ldw(b.emat[:, e * 128:(e + 1) * 128])
            for t2 in range(nb):
                ctx.mm(b.ps[:, t2 * 512:(t2 + 1) * 512],
                       b.emat[:, e * 128:(e + 1) * 128],
                       b.comb_t[0:16, g0 + t2 * 512:g0 + (t2 + 1) * 512],
                       start=True, stop=True,
                       mark=("mm" if t2 == nb - 1 else None))
            ctx.act(b.cbc[:, 0:tg], b.ps[:, 0:tg], AF.Copy)
            for jj in range(JM):
                p = jj % 2
                wload(ctx, "POOL", b.ws, p * 2048, d.f_wg, FC, e * JM + jj,
                      "wlA" if p == 0 else "wlB")
                for i in range(FC):
                    ctx.ldw(b.ws[:, p * 2048 + i * 128: p * 2048 + (i + 1) * 128])
                    for t2 in range(nb):
                        ctx.mm(b.ps[:, (p * 2 + t2) * 512:(p * 2 + t2 + 1) * 512],
                               b.ws[:, p * 2048 + i * 128: p * 2048 + (i + 1) * 128],
                               b.xg[:, t2 * 8192 + i * 512: t2 * 8192 + (i + 1) * 512],
                               start=(i == 0), stop=(i == FC - 1),
                               mark=("mm" if (i == FC - 1 and t2 == nb - 1) else None))
                for t2 in range(nb):
                    ctx.act(b.sg1[:, (p * 2 + t2) * 512:(p * 2 + t2 + 1) * 512],
                            b.ps[:, (p * 2 + t2) * 512:(p * 2 + t2 + 1) * 512],
                            AF.Silu)
                wload(ctx, "POOL", b.ws, (2 + p) * 2048, d.f_wu, FC,
                      e * JM + jj, "wlC" if p == 0 else "wlD")
                for i in range(FC):
                    ctx.ldw(b.ws[:, (2 + p) * 2048 + i * 128:
                                 (2 + p) * 2048 + (i + 1) * 128])
                    for t2 in range(nb):
                        ctx.mm(b.ps[:, (4 + p * 2 + t2) * 512:
                                    (4 + p * 2 + t2 + 1) * 512],
                               b.ws[:, (2 + p) * 2048 + i * 128:
                                    (2 + p) * 2048 + (i + 1) * 128],
                               b.xg[:, t2 * 8192 + i * 512: t2 * 8192 + (i + 1) * 512],
                               start=(i == 0), stop=(i == FC - 1),
                               mark=("mm" if (i == FC - 1 and t2 == nb - 1) else None))
                for t2 in range(nb):
                    ctx.tt(b.tmp[:, t2 * 512:(t2 + 1) * 512],
                           b.sg1[:, (p * 2 + t2) * 512:(p * 2 + t2 + 1) * 512],
                           b.ps[:, (4 + p * 2 + t2) * 512:
                                (4 + p * 2 + t2 + 1) * 512], ALU.mult)
                    ctx.tt(b.big[:, t2 * 5632 + jj * 512: t2 * 5632 + (jj + 1) * 512],
                           b.tmp[:, t2 * 512:(t2 + 1) * 512],
                           b.cbc[:, t2 * 512:(t2 + 1) * 512], ALU.mult)
            for j2 in range(FC):
                p = j2 % 2
                wload(ctx, "POOL", b.ws, p * 2048, d.f_wd, JM, e * FC + j2,
                      "wlE" if p == 0 else "wlF")
                for k in range(JM):
                    ctx.ldw(b.ws[:, p * 2048 + k * 128: p * 2048 + (k + 1) * 128])
                    for t2 in range(nb):
                        ctx.mm(b.ps[:, (p * 2 + t2) * 512:(p * 2 + t2 + 1) * 512],
                               b.ws[:, p * 2048 + k * 128: p * 2048 + (k + 1) * 128],
                               b.big[:, t2 * 5632 + k * 512: t2 * 5632 + (k + 1) * 512],
                               start=(k == 0), stop=(k == JM - 1),
                               mark=("mm" if (k == JM - 1 and t2 == nb - 1) else None))
                for t2 in range(nb):
                    if e == 0:
                        ctx.act(b.acc[:, t2 * 8192 + j2 * 512: t2 * 8192 + (j2 + 1) * 512],
                                b.ps[:, (p * 2 + t2) * 512:(p * 2 + t2 + 1) * 512],
                                AF.Copy)
                    else:
                        ctx.tt(b.acc[:, t2 * 8192 + j2 * 512: t2 * 8192 + (j2 + 1) * 512],
                               b.acc[:, t2 * 8192 + j2 * 512: t2 * 8192 + (j2 + 1) * 512],
                               b.ps[:, (p * 2 + t2) * 512:(p * 2 + t2 + 1) * 512],
                               ALU.add)
        for t2 in range(nb):
            for i in range(FC):
                ctx.mm(b.ps[0:1, 6 * 512:6 * 512 + 512], b.sgw[:, i:i + 1],
                       b.xg[:, t2 * 8192 + i * 512: t2 * 8192 + (i + 1) * 512],
                       start=(i == 0), stop=(i == FC - 1),
                       mark=("mm" if i == FC - 1 else None))
            ctx.act(b.rows[0:1, 0:512], b.ps[0:1, 6 * 512:6 * 512 + 512],
                    AF.Exp, scale=-1.0)
            ctx.ts(b.rows[0:1, 512:1024], b.rows[0:1, 0:512], 1.0, ALU.add)
            ctx.recip(b.rows[0:1, 0:512], b.rows[0:1, 512:1024])
            ctx.mm(b.ps[:, 6 * 512:6 * 512 + 512], b.ones_row[:, :],
                   b.rows[0:1, 0:512], start=True, stop=True, mark="mm")
            ctx.act(b.sgb[:, :], b.ps[:, 6 * 512:6 * 512 + 512], AF.Copy)
            for jj in range(JSH):
                p = jj % 2
                wload(ctx, "POOL", b.ws, p * 2048, d.f_sg, FC, jj,
                      "wlA" if p == 0 else "wlB")
                for i in range(FC):
                    ctx.ldw(b.ws[:, p * 2048 + i * 128: p * 2048 + (i + 1) * 128])
                    ctx.mm(b.ps[:, p * 512:(p + 1) * 512],
                           b.ws[:, p * 2048 + i * 128: p * 2048 + (i + 1) * 128],
                           b.xg[:, t2 * 8192 + i * 512: t2 * 8192 + (i + 1) * 512],
                           start=(i == 0), stop=(i == FC - 1),
                           mark=("mm" if i == FC - 1 else None))
                ctx.act(b.sg1[:, p * 512:(p + 1) * 512],
                        b.ps[:, p * 512:(p + 1) * 512], AF.Silu)
                wload(ctx, "POOL", b.ws, (2 + p) * 2048, d.f_su, FC, jj,
                      "wlC" if p == 0 else "wlD")
                for i in range(FC):
                    ctx.ldw(b.ws[:, (2 + p) * 2048 + i * 128:
                                 (2 + p) * 2048 + (i + 1) * 128])
                    ctx.mm(b.ps[:, (2 + p) * 512:(3 + p) * 512],
                           b.ws[:, (2 + p) * 2048 + i * 128:
                                (2 + p) * 2048 + (i + 1) * 128],
                           b.xg[:, t2 * 8192 + i * 512: t2 * 8192 + (i + 1) * 512],
                           start=(i == 0), stop=(i == FC - 1),
                           mark=("mm" if i == FC - 1 else None))
                ctx.tt(b.big[:, jj * 512:(jj + 1) * 512],
                       b.sg1[:, p * 512:(p + 1) * 512],
                       b.ps[:, (2 + p) * 512:(3 + p) * 512], ALU.mult)
            if t2 == nb - 1:
                load_xg(gi + 1)
            for j2 in range(FC):
                p = j2 % 2
                for hf in range(2):
                    k0, k1 = (0, 22) if hf == 0 else (22, JSH)
                    ph = (j2 * 2 + hf) % 2
                    wload(ctx, "POOL", b.wsd, ph * 22 * 128, d.f_sd, JSH, j2,
                          "wlE" if ph == 0 else "wlF", k0=k0, k1=k1)
                    for k in range(k0, k1):
                        ctx.ldw(b.wsd[:, ph * 22 * 128 + (k - k0) * 128:
                                      ph * 22 * 128 + (k - k0 + 1) * 128])
                        ctx.mm(b.ps[:, (4 + p) * 512:(5 + p) * 512],
                               b.wsd[:, ph * 22 * 128 + (k - k0) * 128:
                                     ph * 22 * 128 + (k - k0 + 1) * 128],
                               b.big[:, k * 512:(k + 1) * 512],
                               start=(k == 0), stop=(k == JSH - 1),
                               mark=("mm" if k == JSH - 1 else None))
                ctx.tt(b.tmp[:, 0:512], b.ps[:, (4 + p) * 512:(5 + p) * 512],
                       b.sgb[:, :], ALU.mult)
                ctx.tt(b.acc[:, t2 * 8192 + j2 * 512: t2 * 8192 + (j2 + 1) * 512],
                       b.acc[:, t2 * 8192 + j2 * 512: t2 * 8192 + (j2 + 1) * 512],
                       b.tmp[:, 0:512], ALU.add)
        for j2 in range(FC):
            buf = b.ob if j2 % 2 == 0 else b.tmp
            for t2 in range(tg // 512):
                ctx.dma("ACT", buf[:, t2 * 512:(t2 + 1) * 512],
                        src[:, (g0 // 512 + t2) * 8192 + j2 * 512:
                            (g0 // 512 + t2) * 8192 + (j2 + 1) * 512],
                        "ldC" if j2 % 2 == 0 else "ldD")
                ctx.tt(buf[:, t2 * 512:(t2 + 1) * 512],
                       buf[:, t2 * 512:(t2 + 1) * 512],
                       b.acc[:, t2 * 8192 + j2 * 512:
                             t2 * 8192 + (j2 + 1) * 512], ALU.add)
                ctx.dma("SP", dst[:, (g0 // 512 + t2) * 8192 + j2 * 512:
                                  (g0 // 512 + t2) * 8192 + (j2 + 1) * 512],
                        buf[:, t2 * 512:(t2 + 1) * 512],
                        "stA" if j2 % 2 == 0 else "stB")


def alloc_ab(ctx, es, cst):
    nc = ctx.nc
    b = type("B", (), {})()
    b.__dict__.update(cst.__dict__)
    sb = nc.sbuf_tensor
    b.xlnb = [es.enter_context(sb(f"ab_xln{i}", [128, FC * 512], BF16))
              for i in range(2)]
    b.q = es.enter_context(sb("ab_q", [128, FC * 512], BF16))
    b.kring = es.enter_context(sb("ab_kring", [128, FC * 1024], BF16))
    b.vring = es.enter_context(sb("ab_vring", [128, 8 * H], BF16))
    b.osb = es.enter_context(sb("ab_osb", [128, NH * 512], BF16))
    b.ropec = es.enter_context(sb("ab_ropec", [128, T_MAIN], BF16))
    b.ropes = es.enter_context(sb("ab_ropes", [128, T_MAIN], BF16))
    b.pswap = es.enter_context(sb("ab_pswap", [128, 128], BF16))
    b.kdr = es.enter_context(sb("ab_kdr", [128, 512], BF16))
    b.bmask = es.enter_context(sb("ab_bmask", [128, 8 * 512], BF16))
    b.pt = es.enter_context(sb("ab_pt", [128, 8 * 512], BF16))
    b.rb = es.enter_context(sb("ab_rb", [128, 512], F32))
    b.t1 = es.enter_context(sb("ab_t1", [128, 512], F32))
    b.tmp = es.enter_context(sb("ab_tmp", [128, 512], F32))
    b.rows = es.enter_context(sb("ab_rows", [1, 3 * 512], F32))
    b.wsl = es.enter_context(sb("ab_wsl", [128, 4 * H], BF16))
    b.xr = [es.enter_context(sb(f"ab_xr{i}", [128, 512], F32))
            for i in range(2)]
    b.ob = [es.enter_context(sb(f"ab_ob{i}", [128, 512], F32))
            for i in range(2)]
    b.ps = es.enter_context(nc.psum_tensor("ab_ps", [128, 8 * 512], F32))
    return b


def emit_ab(ctx, b, d):
    ctx.dma("SP", b.ropec[:, :], d.ropec[:, :], "stC")
    ctx.dma("SP", b.ropes[:, :], d.ropes[:, :], "stC")
    ctx.dma("POOL", b.pswap[:, :], d.c_pswap[:, :], "wlI")
    ctx.dma("POOL", b.bmask[:, :], d.bmask[:, 8 * 512:16 * 512], "wlJ")
    ctx.dma("SP", b.xlnb[0][:, :], d.xln_t[:, 0:8192], "ldA")

    def rope_apply(dst, psrc):
        ctx.act(b.kdr[:, :], psrc, AF.Copy)
        ctx.ldw(b.pswap[:, :])
        ctx.mm(b.ps[:, 7 * 512:8 * 512], b.pswap[:, :],
               b.kdr[:, :], start=True, stop=True, mark="mm")
        ctx.tt(b.t1[:, :], b.kdr[:, :], b.ropec[:, rc0:rc0 + 512], ALU.mult)
        ctx.tt(b.tmp[:, :], b.ps[:, 7 * 512:8 * 512],
               b.ropes[:, rc0:rc0 + 512], ALU.mult)
        ctx.tt(dst, b.t1[:, :], b.tmp[:, :], ALU.add)

    for pb in range(5):
        c0 = pb * 512
        rc0 = c0
        qb = pb - 1
        xln = b.xlnb[pb % 2]
        rs = pb % 2
        for hh in range(NH):
            wload(ctx, "POOL", b.wsl, (hh % 2) * H, d.wb_k, FC, hh,
                  "wlA" if hh % 2 == 0 else "wlB")
            for i in range(FC):
                ctx.ldw(b.wsl[:, (hh % 2) * H + i * 128:(hh % 2) * H + (i + 1) * 128])
                ctx.mm(b.ps[:, 6 * 512:7 * 512],
                       b.wsl[:, (hh % 2) * H + i * 128:(hh % 2) * H + (i + 1) * 128],
                       xln[:, i * 512:(i + 1) * 512],
                       start=(i == 0), stop=(i == FC - 1),
                       mark=("mm" if i == FC - 1 else None))
            kd = b.kring[:, hh * 1024 + rs * 512: hh * 1024 + (rs + 1) * 512]
            rope_apply(kd, b.ps[:, 6 * 512:7 * 512])
        if pb + 1 < 5:
            ctx.dma("SP", b.xlnb[(pb + 1) % 2][:, :],
                    d.xln_t[:, (pb + 1) * 8192:(pb + 2) * 8192],
                    "ldA" if (pb + 1) % 2 == 0 else "ldB")
        for half in range(2):
            for i in range(FC):
                ctx.dma("POOL", b.wi_ab[:, (i % 2) * H:(i % 2) * H + 1024],
                        d.wb_v[i * 128:(i + 1) * 128,
                               half * 1024:(half + 1) * 1024],
                        "wlE" if i % 2 == 0 else "wlF")
                for ts4 in range(4):
                    xtile = xln[:, i * 512 + ts4 * 128:
                                i * 512 + (ts4 + 1) * 128]
                    ctx.ldw(xtile)
                    for g2 in range(2):
                        bank = ts4 * 2 + g2
                        ctx.mm(b.ps[:, bank * 512:(bank + 1) * 512], xtile,
                               b.wi_ab[:, (i % 2) * H + g2 * 512:
                                       (i % 2) * H + (g2 + 1) * 512],
                               start=(i == 0), stop=(i == FC - 1),
                               mark=("mm" if (ts4 == 3 and g2 == 1) else None))
            for ts4 in range(4):
                for g2 in range(2):
                    bank = ts4 * 2 + g2
                    ng = half * 2 + g2
                    ctx.act(b.vring[:, (rs * 4 + ts4) * H + ng * 512:
                                    (rs * 4 + ts4) * H + (ng + 1) * 512],
                            b.ps[:, bank * 512:(bank + 1) * 512], AF.Copy)
        if pb == 0:
            continue
        for hh in range(NH):
            wload(ctx, "POOL", b.wsl, (2 + hh % 2) * H, d.wb_q, FC, hh,
                  "wlC" if hh % 2 == 0 else "wlD")
            for i in range(FC):
                ctx.ldw(b.wsl[:, (2 + hh % 2) * H + i * 128:
                              (2 + hh % 2) * H + (i + 1) * 128])
                ctx.mm(b.ps[:, 6 * 512:7 * 512],
                       b.wsl[:, (2 + hh % 2) * H + i * 128:
                             (2 + hh % 2) * H + (i + 1) * 128],
                       xln[:, i * 512:(i + 1) * 512],
                       start=(i == 0), stop=(i == FC - 1),
                       mark=("mm" if i == FC - 1 else None))
            qd = b.q[:, hh * 512:(hh + 1) * 512]
            rope_apply(qd, b.ps[:, 6 * 512:7 * 512])
        for hh in range(NH):
            for p2 in range(2):
                for q4 in range(4):
                    cc = p2 * 4 + q4
                    pbk = qb + (1 if cc >= 4 else 0)
                    slot = pbk % 2
                    ts = cc % 4
                    ktile = b.kring[:, hh * 1024 + slot * 512 + ts * 128:
                                    hh * 1024 + slot * 512 + (ts + 1) * 128]
                    ctx.ldw(ktile)
                    ctx.mm(b.ps[:, q4 * 512:(q4 + 1) * 512],
                           ktile, b.q[:, hh * 512:(hh + 1) * 512],
                           start=True, stop=True,
                           mark=("mm" if q4 == 3 else None))
                ctx.act(b.pt[:, p2 * 2048:(p2 + 1) * 2048],
                        b.ps[:, 0:2048], AF.Exp, scale=BQS)
                ctx.tt(b.pt[:, p2 * 2048:(p2 + 1) * 2048],
                       b.pt[:, p2 * 2048:(p2 + 1) * 2048],
                       b.bmask[:, p2 * 2048:(p2 + 1) * 2048], ALU.mult)
            for cc in range(8):
                pbk = qb + (1 if cc >= 4 else 0)
                slot = pbk % 2
                ts = cc % 4
                ctx.mm(b.ps[0:1, 4 * 512:4 * 512 + 512], b.ones_col[:, :],
                       b.pt[:, cc * 512:(cc + 1) * 512],
                       start=(cc == 0), stop=(cc == 7),
                       mark=("mm" if cc == 7 else None))
                vtile = b.vring[:, (slot * 4 + ts) * H + hh * 128:
                                (slot * 4 + ts) * H + (hh + 1) * 128]
                ctx.ldw(vtile)
                ctx.mm(b.ps[:, 5 * 512:6 * 512], vtile,
                       b.pt[:, cc * 512:(cc + 1) * 512],
                       start=(cc == 0), stop=(cc == 7),
                       mark=("mm" if cc == 7 else None))
            ctx.recip(b.rows[0:1, 0:512], b.ps[0:1, 4 * 512:4 * 512 + 512])
            ctx.mm(b.ps[:, 4 * 512:5 * 512], b.ones_row[:, :],
                   b.rows[0:1, 0:512], start=True, stop=True, mark="mm")
            ctx.act(b.rb[:, :], b.ps[:, 4 * 512:5 * 512], AF.Copy)
            ctx.tt(b.osb[:, hh * 512:(hh + 1) * 512], b.ps[:, 5 * 512:6 * 512],
                   b.rb[:, :], ALU.mult)
        for j in range(FC):
            jp = j % 2
            wload(ctx, "POOL", b.wsl, jp * H, d.wb_o, FC, j,
                  "wlG" if jp == 0 else "wlH")
            for i in range(FC):
                ctx.ldw(b.wsl[:, jp * H + i * 128:jp * H + (i + 1) * 128])
                ctx.mm(b.ps[:, 6 * 512:7 * 512],
                       b.wsl[:, jp * H + i * 128:jp * H + (i + 1) * 128],
                       b.osb[:, i * 512:(i + 1) * 512],
                       start=(i == 0), stop=(i == FC - 1),
                       mark=("mm" if i == FC - 1 else None))
            ctx.dma("ACT", b.xr[jp][:, :],
                    d.x2_t[:, pb * 8192 + j * 512:pb * 8192 + (j + 1) * 512],
                    "ldC" if jp == 0 else "ldD")
            ctx.tt(b.ob[jp][:, :], b.xr[jp][:, :],
                   b.ps[:, 6 * 512:7 * 512], ALU.add)
            ctx.dma("SP", d.x3_t[:, qb * 8192 + j * 512:
                                 qb * 8192 + (j + 1) * 512],
                    b.ob[jp][:, :], "stA" if jp == 0 else "stB")
        if pb == 1:
            ctx.dma("POOL", b.bmask[:, :], d.bmask[:, 0:8 * 512], "wlJ")


def build_program():
    nc = bass.Bass("TRN2", target_bir_lowering=False)
    d = type("D", (), {})()
    dt = nc.dram_tensor
    d.x_t = dt("x_t", [128, 8 * 8192], F32, kind="ExternalInput").ap()
    d.wa_q = dt("wa_q", [128, H * FC], F32, kind="ExternalInput").ap()
    d.wa_f = dt("wa_f", [128, H * FC], F32, kind="ExternalInput").ap()
    d.wa_i = dt("wa_i", [128, H * FC], F32, kind="ExternalInput").ap()
    d.wa_o = dt("wa_o", [128, H * FC], F32, kind="ExternalInput").ap()
    d.ga_t = dt("ga_t", [128, FC], F32, kind="ExternalInput").ap()
    d.wb_q = dt("wb_q", [128, H * FC], F32, kind="ExternalInput").ap()
    d.wb_k = dt("wb_k", [128, H * FC], F32, kind="ExternalInput").ap()
    d.wb_v = dt("wb_v", [H, H], F32, kind="ExternalInput").ap()
    d.wb_o = dt("wb_o", [128, H * FC], F32, kind="ExternalInput").ap()
    d.f_rw = dt("f_rw", [128, FC * 16], F32, kind="ExternalInput").ap()
    d.f_wg = dt("f_wg", [128, E * IM * FC], F32, kind="ExternalInput").ap()
    d.f_wu = dt("f_wu", [128, E * IM * FC], F32, kind="ExternalInput").ap()
    d.f_wd = dt("f_wd", [128, E * H * JM], F32, kind="ExternalInput").ap()
    d.f_sg = dt("f_sg", [128, ISH * FC], F32, kind="ExternalInput").ap()
    d.f_su = dt("f_su", [128, ISH * FC], F32, kind="ExternalInput").ap()
    d.f_sd = dt("f_sd", [128, H * JSH], F32, kind="ExternalInput").ap()
    d.f_sgw = dt("f_sgw", [128, FC], F32, kind="ExternalInput").ap()
    d.c_ident = dt("c_ident", [128, 128], BF16, kind="ExternalInput").ap()
    d.c_ones_col = dt("c_ones_col", [128, 1], BF16, kind="ExternalInput").ap()
    d.c_ones_row = dt("c_ones_row", [1, 128], F32, kind="ExternalInput").ap()
    d.c_maskaa = dt("c_maskaa", [128, 128], BF16, kind="ExternalInput").ap()
    d.c_emat = dt("c_emat", [16, 16 * 128], BF16, kind="ExternalInput").ap()
    d.c_pswap = dt("c_pswap", [128, 128], BF16, kind="ExternalInput").ap()
    d.c_bias = dt("c_bias", [128, 4], F32, kind="ExternalInput").ap()
    d.c_bias2 = dt("c_bias2", [128, 4], F32, kind="ExternalInput").ap()
    d.ropec = dt("ropec", [128, T_MAIN], BF16, kind="ExternalInput").ap()
    d.ropes = dt("ropes", [128, T_MAIN], BF16, kind="ExternalInput").ap()
    d.bmask = dt("bmask", [128, 16 * 512], BF16, kind="ExternalInput").ap()
    d.x1_t = dt("x1_t", [128, 5 * 8192], F32).ap()
    d.x2_t = dt("x2_t", [128, 5 * 8192], F32).ap()
    d.x3_t = dt("x3_t", [128, 4 * 8192], F32).ap()
    d.xln_t = dt("xln_t", [128, 8 * 8192], BF16).ap()
    d.osb_t = dt("osb_t", [128, 5 * 8192], BF16).ap()
    d.y_t = dt("y_t", [128, 4 * 8192], F32, kind="ExternalOutput").ap()

    with contextlib.ExitStack() as es0:
        sems = {k: es0.enter_context(nc.semaphore("s_" + k)) for k in SEMS}
        block = es0.enter_context(nc.Block())
        ctx = Ctx(nc, block, sems)

        cst = type("C", (), {})()
        cst.ident = es0.enter_context(nc.sbuf_tensor("c_ident_s", [128, 128], BF16))
        cst.ones_col = es0.enter_context(nc.sbuf_tensor("c_onec_s", [128, 1], BF16))
        cst.ones_row = es0.enter_context(nc.sbuf_tensor("c_oner_s", [1, 128], F32))
        cst.maskaa = es0.enter_context(nc.sbuf_tensor("c_maa_s", [128, 128], BF16))
        cst.emat = es0.enter_context(nc.sbuf_tensor("c_emat_s", [16, 16 * 128], BF16))
        cst.gat = es0.enter_context(nc.sbuf_tensor("c_gat_s", [128, FC], F32))
        cst.rw = es0.enter_context(nc.sbuf_tensor("c_rw_s", [128, FC * 16], BF16))
        cst.comb_t = es0.enter_context(nc.sbuf_tensor("c_combt", [16, T_MAIN], BF16))
        cst.cbias = es0.enter_context(nc.sbuf_tensor("c_cbias", [128, 4], F32))
        cst.cbias2 = es0.enter_context(nc.sbuf_tensor("c_cbias2", [128, 4], F32))
        nc.const_aps.register(F32, LN_EPS, cst.cbias[:, 0:1])
        nc.const_aps.register(F32, 1.0, cst.cbias[:, 1:2])
        nc.const_aps.register(F32, LNC, cst.cbias[:, 2:3])
        nc.const_aps.register(F32, LNC - SH, cst.cbias2[:, 0:1])
        nc.const_aps.register(F32, -SH, cst.cbias2[:, 1:2])
        nc.const_aps.register(F32, SH, cst.cbias2[:, 2:3])

        def ph_init(c):
            c.dma("SP", cst.ident[:, :], d.c_ident[:, :], "ldA")
            c.dma("SP", cst.ones_col[:, :], d.c_ones_col[:, :], "ldB")
            c.dma("SP", cst.ones_row[:, :], d.c_ones_row[:, :], "ldC")
            c.dma("SP", cst.maskaa[:, :], d.c_maskaa[:, :], "ldD")
            c.dma("SP", cst.emat[:, :], d.c_emat[:, :], "stA")
            c.dma("SP", cst.gat[:, :], d.ga_t[:, :], "stB")
            c.dma("SP", cst.cbias[:, :], d.c_bias[:, :], "stC")
            c.dma("POOL", cst.cbias2[:, :], d.c_bias2[:, :], "wlB")
            c.dma("POOL", cst.rw[:, :], d.f_rw[:, :], "wlA")

        run_phase(ctx, ph_init, "init")

        with contextlib.ExitStack() as es:
            b = alloc_ffn_ln(ctx, es, cst, "al")
            run_phase(ctx, lambda c: emit_ffn_ln(c, b, d, d.x_t, 8 * 512,
                                                 router=False), "aln")
        with contextlib.ExitStack() as es:
            b = alloc_aam(ctx, es, cst)
            run_phase(ctx, lambda c: emit_aam(c, b, d), "aam")
        with contextlib.ExitStack() as es:
            b = alloc_awo(ctx, es, cst)
            run_phase(ctx, lambda c: emit_awo(c, b, d), "awo")
        with contextlib.ExitStack() as es:
            b = alloc_ffn_ln(ctx, es, cst, "f1l")
            run_phase(ctx, lambda c: emit_ffn_ln(c, b, d, d.x1_t, T_MAIN), "f1l")
        with contextlib.ExitStack() as es:
            b = alloc_ffn_moe(ctx, es, cst, "f1m")
            run_phase(ctx, lambda c: emit_ffn_moe(c, b, d, d.x1_t, d.x2_t,
                                                  T_MAIN), "f1m")
        with contextlib.ExitStack() as es:
            b = alloc_ffn_ln(ctx, es, cst, "bl")
            run_phase(ctx, lambda c: emit_ffn_ln(c, b, d, d.x2_t, T_MAIN,
                                                 router=False), "abl")
        with contextlib.ExitStack() as es:
            b = alloc_ab(ctx, es, cst)
            b.wi_ab = b.wsl
            run_phase(ctx, lambda c: emit_ab(c, b, d), "ab")
        with contextlib.ExitStack() as es:
            b = alloc_ffn_ln(ctx, es, cst, "f2l")
            run_phase(ctx, lambda c: emit_ffn_ln(c, b, d, d.x3_t, OWN), "f2l")
        with contextlib.ExitStack() as es:
            b = alloc_ffn_moe(ctx, es, cst, "f2m")
            run_phase(ctx, lambda c: emit_ffn_moe(c, b, d, d.x3_t, d.y_t, OWN),
                      "f2m")

        def ph_fin(c):
            pass

        run_phase(ctx, ph_fin, "fin")
    return nc


def _rope_tables(positions):
    inv = 1.0 / (10000.0 ** (np.arange(0, DK, 2, dtype=np.float64) / DK))
    ang = inv[:, None] * positions[None, :].astype(np.float64)
    return np.cos(ang).astype(np.float32), np.sin(ang).astype(np.float32)


def _bmask(parity):
    out = np.zeros((16, 128, 512), dtype=np.float32)
    tk = np.arange(128)[:, None]
    tq = np.arange(512)[None, :]
    for c in range(8):
        dd = tq - tk - 128 * c + 512
        m = ((dd >= 0) & (dd < 512)).astype(np.float32)
        out[c] = m
        if parity == 1:
            out[8 + c] = m
        else:
            out[8 + c] = m if c >= 4 else 0.0
    return out.reshape(16 * 128, 512).astype(bfdt)


def _colchunk(w):
    K, D = w.shape
    return np.ascontiguousarray(
        w.reshape(K // 128, 128, D // 128, 128).transpose(1, 2, 0, 3)
        .reshape(128, -1))


def _blocked(xt):
    Hh, ncols = xt.shape
    nblk = ncols // 512
    return np.ascontiguousarray(
        xt.reshape(16, 128, nblk, 512).transpose(1, 2, 0, 3).reshape(128, -1))


def _perm_deinterleave():
    p = np.zeros(H, dtype=np.int64)
    for h in range(NH):
        base = h * DK
        p[base:base + 64] = base + 2 * np.arange(64)
        p[base + 64:base + DK] = base + 2 * np.arange(64) + 1
    return p


def make_in_maps(L0_attn_a_x, L0_attn_a_Wq, L0_attn_a_Wf, L0_attn_a_Wi,
                 L0_attn_a_Wo, L0_attn_a_g_weight, L0_attn_b_Wq, L0_attn_b_Wk,
                 L0_attn_b_Wv, L0_attn_b_Wo, L0_ffn_router_w, L0_ffn_w_gate,
                 L0_ffn_w_up, L0_ffn_w_down, L0_ffn_sh_gate, L0_ffn_sh_up,
                 L0_ffn_sh_down, L0_ffn_sh_gate_w):
    x = np.asarray(L0_attn_a_x, dtype=np.float32)
    B, S, _ = x.shape
    perm = _perm_deinterleave()

    wg_all = np.empty((H, E * IM), np.float32)
    wu_all = np.empty((H, E * IM), np.float32)
    wd_all = np.empty((IM, E * H), np.float32)
    for e in range(E):
        wg_all[:, e * IM:(e + 1) * IM] = np.asarray(L0_ffn_w_gate[e], np.float32)
        wu_all[:, e * IM:(e + 1) * IM] = np.asarray(L0_ffn_w_up[e], np.float32)
        wd_all[:, e * H:(e + 1) * H] = np.asarray(L0_ffn_w_down[e], np.float32)
    com = {
        "wa_q": _colchunk(np.asarray(L0_attn_a_Wq, dtype=np.float32)),
        "wa_f": _colchunk(np.asarray(L0_attn_a_Wf, dtype=np.float32)),
        "wa_i": _colchunk(np.asarray(L0_attn_a_Wi, dtype=np.float32)),
        "wa_o": _colchunk(np.asarray(L0_attn_a_Wo, dtype=np.float32)),
        "ga_t": np.ascontiguousarray(
            np.asarray(L0_attn_a_g_weight, dtype=np.float32).reshape(FC, 128).T),
        "wb_q": _colchunk(np.ascontiguousarray(
            np.asarray(L0_attn_b_Wq, dtype=np.float32).T[:, perm])),
        "wb_k": _colchunk(np.ascontiguousarray(
            np.asarray(L0_attn_b_Wk, dtype=np.float32).T[:, perm])),
        "wb_v": np.ascontiguousarray(np.asarray(L0_attn_b_Wv, dtype=np.float32).T),
        "wb_o": _colchunk(np.ascontiguousarray(
            np.asarray(L0_attn_b_Wo, dtype=np.float32).T)),
        "f_rw": np.ascontiguousarray(
            np.asarray(L0_ffn_router_w, dtype=np.float32).T
            .reshape(16, 128, 16).transpose(1, 0, 2).reshape(128, 256)),
        "f_wg": _colchunk(wg_all),
        "f_wu": _colchunk(wu_all),
        "f_wd": _colchunk(wd_all),
        "f_sg": _colchunk(np.ascontiguousarray(L0_ffn_sh_gate, dtype=np.float32)),
        "f_su": _colchunk(np.ascontiguousarray(L0_ffn_sh_up, dtype=np.float32)),
        "f_sd": _colchunk(np.ascontiguousarray(L0_ffn_sh_down, dtype=np.float32)),
        "f_sgw": np.ascontiguousarray(np.asarray(L0_ffn_sh_gate_w, dtype=np.float32).reshape(FC, 128).T),
        "c_ident": np.eye(128, dtype=np.float32).astype(bfdt),
        "c_ones_col": np.ones((128, 1), dtype=np.float32).astype(bfdt),
        "c_ones_row": np.ones((1, 128), dtype=np.float32),
        "c_maskaa": (SC_MASK * np.triu(np.ones((128, 128), dtype=np.float32))).astype(bfdt),
    }
    emat = np.zeros((16, 16 * 128), dtype=np.float32)
    for e in range(16):
        emat[e, e * 128:(e + 1) * 128] = 1.0
    com["c_emat"] = emat.astype(bfdt)
    psw = np.zeros((128, 128), dtype=np.float32)
    psw[(np.arange(128) + 64) % 128, np.arange(128)] = 1.0
    com["c_pswap"] = psw.astype(bfdt)
    com["c_bias"] = np.tile(np.array([[LN_EPS, 1.0, LNC, 0.0]], np.float32),
                            (128, 1))
    com["c_bias2"] = np.tile(np.array([[LNC - SH, -SH, SH, 0.0]], np.float32),
                             (128, 1))

    in_maps = []
    for c in range(8):
        bb, odd = c // 2, c % 2
        seq = x[bb]
        if odd:
            chain = seq
        else:
            chain = np.concatenate([np.zeros((2048, H), np.float32), seq[0:2048]],
                                   axis=0)
        m = dict(com)
        m["x_t"] = _blocked(np.ascontiguousarray(chain.T))
        start = 2048 * odd
        pos = np.arange(MAIN0, NT) - (0 if odd else 2048)
        pos = np.maximum(pos, 0)
        rc, rs = _rope_tables(pos)
        m["ropec"] = np.concatenate([rc, rc], axis=0).astype(bfdt)
        m["ropes"] = np.concatenate([-rs, rs], axis=0).astype(bfdt)
        m["bmask"] = np.ascontiguousarray(
            _bmask(odd).reshape(16, 128, 512).transpose(1, 0, 2)
            .reshape(128, 16 * 512))
        in_maps.append(m)
    return in_maps, (B, S)


def kernel(**inputs):
    in_maps, (B, S) = make_in_maps(**inputs)
    nc = build_program()
    res = run_bass_kernel_spmd(nc, in_maps, core_ids=list(range(8)))
    out = np.empty((B, S, H), dtype=np.float32)
    for c in range(8):
        bb, odd = c // 2, c % 2
        y = res.results[c]["y_t"].reshape(128, 4, 16, 512)
        out[bb, odd * 2048:(odd + 1) * 2048, :] = (
            y.transpose(1, 3, 2, 0).reshape(2048, 2048))
    return out


# version 10
# speedup vs baseline: 1.0873x; 1.0199x over previous
import contextlib
import math

import numpy as np
import ml_dtypes

import concourse.bass as bass
import concourse.mybir as mybir
from concourse.bass_utils import run_bass_kernel_spmd

F32 = mybir.dt.float32
BF16 = mybir.dt.bfloat16
AF = mybir.ActivationFunctionType
ALU = mybir.AluOpType
AX = mybir.AxisListType
bfdt = ml_dtypes.bfloat16

H = 2048
FC = 16
NH = 16
DK = 128
CH = 128
NCH = 4
NT = 4096
MAIN0 = 1536
T_MAIN = 2560
OWN0 = 2048
OWN = 2048
E = 16
IM = 1408
JM = 11
ISH = 5632
JSH = 44
LN_EPS = 1e-5
AA_EPS = 1e-5
LNC = math.log(DK ** -0.5)
BQS = 1.0 / math.sqrt(DK)
BIGNEG = 1.0e30
SH = 52 * math.log(2.0)
SC_MASK = 2.0 ** 104

ROLES = ["SP", "POOL", "PE", "ACT", "DVE"]
SEMS = (["ld" + c for c in "ABCD"] + ["st" + c for c in "ABC"] +
        ["wl" + c for c in "ABCDEFGHIJ"] + ["mm", "ev", "dv", "bar"])


def _region(ap):
    h = ap.tensor if hasattr(ap, "tensor") else ap
    if not hasattr(ap, "ap"):
        shp = h.shape
        return (h.name, 0, 128, 0, int(np.prod(shp)))
    name = h.name
    shp = list(h.shape)
    off = ap.offset
    dims = [list(x) for x in ap.ap]
    memtype = getattr(h, "__class__", None).__name__
    if memtype in ("SBTensorHandle", "PSumTensorHandle") and len(shp) == 2:
        rowlen = shp[1]
        p0 = off // rowlen
        c0 = off % rowlen
        pspan = 0
        cspan = 0
        for (st, cnt) in dims:
            if cnt <= 1:
                continue
            if st != 0 and st % rowlen == 0:
                pspan += (st // rowlen) * (cnt - 1)
            else:
                cspan += abs(st) * (cnt - 1)
        c1 = c0 + cspan + 1
        if memtype == "PSumTensorHandle":
            bank = 512 if h.dtype == mybir.dt.float32 else 1024
            c0 = (c0 // bank) * bank
            c1 = ((c1 + bank - 1) // bank) * bank
            return (name, 0, 128, c0, c1)
        return (name, p0, p0 + pspan + 1, c0, c1)
    span = 0
    for (st, cnt) in dims:
        if cnt > 1:
            span += abs(st) * (cnt - 1)
    return (name, 0, 1 << 30, off, off + span + 1)


def _overlap(a, b):
    return (a[0] == b[0] and a[1] < b[2] and b[1] < a[2]
            and a[3] < b[4] and b[3] < a[4])


class Ctx:
    ENG_SEM = {"PE": "mm", "ACT": "ev", "DVE": "dv"}

    def __init__(self, nc, block, sems):
        self.nc = nc
        self.block = block
        self.sem = sems
        self.cnt = {k: 0 for k in SEMS}
        self.role = None
        self.eng = None
        self.hiw = {}
        self.twr = {}
        self.trd = {}
        self.pe_pending = []
        self.dma_owner = {}

    def t_clear(self):
        self.twr.clear()
        self.trd.clear()
        self.pe_pending = []

    def _dep_read(self, ap, deps):
        r = _region(ap)
        for w in self.twr.get(r[0], ()):
            if _overlap(r, (r[0], w[0], w[1], w[2], w[3])):
                if w[5] is None:
                    raise RuntimeError(
                        f"read of {r[0]} depends on unmarked PE write")
                deps.append((w[4], w[5]))

    def _note_read(self, ap, sem, cntv):
        r = _region(ap)
        ent = [r[1], r[2], r[3], r[4], sem, cntv]
        self.trd.setdefault(r[0], []).append(ent)
        return ent

    def _dep_write(self, ap, deps, is_pe):
        r = _region(ap)
        ws = self.twr.get(r[0], ())
        for w in ws:
            if _overlap(r, (r[0], w[0], w[1], w[2], w[3])):
                if is_pe and w[6]:
                    continue
                if w[5] is None:
                    raise RuntimeError(
                        f"write of {r[0]} overlaps unmarked PE write")
                deps.append((w[4], w[5]))
        for rd in self.trd.get(r[0], ()):
            if _overlap(r, (r[0], rd[0], rd[1], rd[2], rd[3])):
                if rd[5] is None:
                    raise RuntimeError(
                        f"write of {r[0]} overlaps PE read not yet covered "
                        f"by a PE mark")
                deps.append((rd[4], rd[5]))

    def _note_write(self, ap, sem, cntv, is_pe):
        r = _region(ap)
        lst = self.twr.setdefault(r[0], [])
        lst[:] = [w for w in lst
                  if not (r[1] <= w[0] and w[1] <= r[2]
                          and r[3] <= w[2] and w[3] <= r[4])]
        ent = [r[1], r[2], r[3], r[4], sem, cntv, is_pe]
        lst.append(ent)
        rl = self.trd.get(r[0])
        if rl:
            rl[:] = [x for x in rl
                     if not (r[1] <= x[0] and x[1] <= r[2]
                             and r[3] <= x[2] and x[3] <= r[4])]
        return ent

    def _emit_op(self, engine, reads, writes, waits, mark, emit_fn,
                 is_dma=False):
        own = self.ENG_SEM.get(engine)
        deps = list(waits)
        for ap in reads:
            if ap is not None and not isinstance(ap, (int, float)):
                self._dep_read(ap, deps)
        for ap in writes:
            self._dep_write(ap, deps, engine == "PE")
        if is_dma:
            sem = mark
            prev = self.dma_owner.setdefault(sem, engine)
            assert prev == engine, f"dma sem {sem} used from {prev} and {engine}"
            self.cnt[sem] += 16
            cntv = self.cnt[sem]
        elif engine == "PE":
            if mark:
                self.cnt["mm"] += 1
                cntv = self.cnt["mm"]
            else:
                cntv = None
            sem = "mm"
        else:
            sem = own
            self.cnt[sem] += 1
            cntv = self.cnt[sem]
        for ap in reads:
            if ap is not None and not isinstance(ap, (int, float)):
                e = self._note_read(ap, sem,
                                    None if (engine == "PE" and not is_dma)
                                    else cntv)
                if engine == "PE" and not is_dma:
                    self.pe_pending.append(e)
        ents = [self._note_write(ap, sem, cntv, engine == "PE" and not is_dma)
                for ap in writes]
        if engine == "PE" and not is_dma:
            if cntv is None:
                self.pe_pending.extend(ents)
            else:
                for e in self.pe_pending:
                    if e[5] is None:
                        e[5] = cntv
                self.pe_pending = []
                for e in ents:
                    e[5] = cntv
        if self.role == engine:
            self.w(*[(s, v) for (s, v) in deps if v])
            ins = emit_fn()
            if is_dma:
                ins.then_inc(self.sem[sem], 16)
            elif engine == "PE":
                if cntv is not None:
                    ins.then_inc(self.sem[sem], 1)
            else:
                ins.then_inc(self.sem[sem], 1)
        return cntv if cntv is not None else self.cnt.get(sem, 0)

    def w(self, *waits):
        for (s, v) in waits:
            if v <= 0:
                continue
            if self.hiw.get(s, -1) >= v:
                continue
            self.eng.wait_ge(self.sem[s], v)
            self.hiw[s] = v

    def now(self):
        return [("mm", self.cnt["mm"]), ("ev", self.cnt["ev"]),
                ("dv", self.cnt["dv"])]

    def dma(self, q, out, in_, sem, waits=()):
        return self._emit_op(q, [in_], [out], waits, sem,
                             lambda: self.eng.dma_start(out=out, in_=in_),
                             is_dma=True)

    def ldw(self, wt, waits=()):
        deps = list(waits)
        self._dep_read(wt, deps)
        self.pe_pending.append(self._note_read(wt, "mm", None))
        if self.role == "PE":
            self.w(*[(s, v) for (s, v) in deps if v])
            self.eng.ldweights(wt)

    def mm(self, out, lhsT, rhs, start, stop, mark=None, waits=()):
        return self._emit_op(
            "PE", [lhsT, rhs], [out], waits, mark,
            lambda: self.eng.matmul(out, lhsT, rhs, start=start, stop=stop))

    def tr(self, out, in_, ident, mark=None, waits=()):
        return self._emit_op(
            "PE", [in_, ident], [out], waits, mark,
            lambda: self.eng.transpose(out, in_, ident))

    def act(self, out, in_, func=AF.Copy, bias=None, scale=None, mark=None,
            waits=()):
        kw = {}
        if bias is not None:
            kw["bias"] = bias
        if scale is not None:
            kw["scale"] = scale
        return self._emit_op(
            "ACT", [in_, bias if hasattr(bias, "tensor") else None], [out],
            waits, mark, lambda: self.eng.activation(out, in_, func, **kw))

    def amul(self, out, in_, mul, mark=None, waits=()):
        return self._emit_op(
            "ACT", [in_, mul if hasattr(mul, "tensor") else None], [out],
            waits, mark, lambda: self.eng.mul(out, in_, mul))

    def aadd(self, out, in_, add, mark=None, waits=()):
        return self._emit_op(
            "ACT", [in_, add if hasattr(add, "tensor") else None], [out],
            waits, mark, lambda: self.eng.add(out, in_, add))

    def tt(self, out, a, b, op, mark=None, waits=()):
        return self._emit_op(
            "DVE", [a, b], [out], waits, mark,
            lambda: self.eng.tensor_tensor(out, a, b, op))

    def ts(self, out, in0, s1, op0, mark=None, waits=()):
        return self._emit_op(
            "DVE", [in0, s1 if hasattr(s1, "tensor") else None], [out],
            waits, mark,
            lambda: self.eng.tensor_scalar(out, in0, s1, None, op0))

    def red(self, out, in_, op, mark=None, waits=()):
        return self._emit_op(
            "DVE", [in_], [out], waits, mark,
            lambda: self.eng.tensor_reduce(out, in_, AX.X, op))

    def scan(self, out, d0, d1, init, op0, op1, mark=None, waits=()):
        return self._emit_op(
            "DVE", [d0, d1], [out], waits, mark,
            lambda: self.eng.tensor_tensor_scan(out, d0, d1, init, op0, op1))

    def recip(self, out, in_, mark=None, waits=()):
        return self._emit_op(
            "DVE", [in_], [out], waits, mark,
            lambda: self.eng.reciprocal(out, in_))

    def sel(self, out, mask, on_true, on_false, mark=None, waits=()):
        return self._emit_op(
            "DVE", [mask, on_true, on_false], [out], waits, mark,
            lambda: self.eng.select(out, mask, on_true, on_false, add_drain=True))

    def vmemset(self, ap, val, mark=None, waits=()):
        return self._emit_op(
            "DVE", [], [ap], waits, mark,
            lambda: self.eng.memset(ap, val))

    def c(self, sem):
        return self.cnt[sem]


def run_phase(ctx, fn, name=""):
    start = dict(ctx.cnt)
    final = None
    for role in ROLES:
        ctx.cnt = dict(start)
        ctx.hiw = {}
        ctx.t_clear()
        ctx.dma_owner = {}
        ctx.role = role

        def body(eng):
            ctx.eng = eng
            fn(ctx)
            own = {"PE": ["mm"], "ACT": ["ev"], "DVE": ["dv"],
                   "SP": [], "POOL": []}[ctx.role]
            own = own + [s for (s, r) in sorted(ctx.dma_owner.items())
                         if r == ctx.role]
            for s in own:
                if ctx.cnt[s]:
                    eng.wait_ge(ctx.sem[s], ctx.cnt[s])
            tgt = ctx.cnt["bar"] + 5
            eng.nop().then_inc(ctx.sem["bar"], 1)
            eng.wait_ge(ctx.sem["bar"], tgt)
            ctx.cnt["bar"] = tgt

        getattr(ctx.block, {"SP": "sync", "POOL": "gpsimd", "PE": "tensor",
                            "ACT": "scalar", "DVE": "vector"}[role])(body)
        if final is None:
            final = dict(ctx.cnt)
        else:
            assert final == ctx.cnt, f"role divergence in {name}"
    ctx.cnt = final


def hdl(t):
    return t.tensor if hasattr(t, "tensor") else t


def ap_cols_rep(sb, col0, ncols, rep):
    h = hdl(sb)
    return bass.AP(h, col0, [[h.shape[1], 128], [1, ncols], [0, rep]])


def ap_stride(sb, off, step, count):
    h = hdl(sb)
    return bass.AP(h, off, [[h.shape[1], 128], [step, count]])


def wload(ctx, q, dst_sb, dst_free0, src, Ka, j, sem, k0=0, k1=None):
    if k1 is None:
        k1 = Ka
    s0 = j * Ka * 128 + k0 * 128
    return ctx.dma(q, dst_sb[:, dst_free0 + 0:dst_free0 + (k1 - k0) * 128],
                   src[:, s0:s0 + (k1 - k0) * 128], sem)


def emit_ln(ctx, b, xsrc, c0, ncols):
    ps, rows = b.ps, b.rows
    s0, s1, c_inv, c_mean = getattr(b, "lnb", (0, 512, 1024, 1536))
    ldsem = getattr(b, "ldsem", "ldA")
    n = ncols
    blk = c0 // 512
    ctx.dma("SP", b.xblk[:, 0:FC * n], xsrc[:, blk * 8192:blk * 8192 + FC * n],
            ldsem)
    ctx.act(b.st16[:, :FC * n], b.xblk[:, :FC * n], AF.Square)
    for a in range(FC):
        ctx.mm(ps[0:1, s1:s1 + n], b.ones_col[:, :],
               b.st16[:, a * n:(a + 1) * n],
               start=(a == 0), stop=(a == FC - 1),
               mark=("mm" if a == FC - 1 else None))
    ctx.act(b.st16[:, :FC * n], b.xblk[:, :FC * n], AF.Copy)
    for a in range(FC):
        ctx.mm(ps[0:1, s0:s0 + n], b.ones_col[:, :], b.st16[:, a * n:(a + 1) * n],
               start=(a == 0), stop=(a == FC - 1),
               mark=("mm" if a == FC - 1 else None))
    inv_h = 1.0 / H
    ctx.amul(rows[0:1, 0:n], ps[0:1, s0:s0 + n], inv_h)
    ctx.amul(rows[0:1, n:2 * n], ps[0:1, s1:s1 + n], inv_h)
    ctx.tt(rows[0:1, 2 * n:3 * n], rows[0:1, 0:n], rows[0:1, 0:n], ALU.mult)
    ctx.tt(rows[0:1, n:2 * n], rows[0:1, n:2 * n], rows[0:1, 2 * n:3 * n],
           ALU.subtract)
    ctx.act(rows[0:1, 2 * n:3 * n], rows[0:1, n:2 * n], AF.Ln, bias=LN_EPS)
    ctx.act(rows[0:1, n:2 * n], rows[0:1, 2 * n:3 * n], AF.Exp, scale=-0.5)
    ctx.tt(rows[0:1, 2 * n:3 * n], rows[0:1, 0:n], rows[0:1, n:2 * n],
           ALU.mult)
    ctx.mm(ps[:, c_inv:c_inv + n], b.ones_row[:, :], rows[0:1, n:2 * n],
           start=True, stop=True, mark="mm")
    ctx.mm(ps[:, c_mean:c_mean + n], b.ones_row[:, :], rows[0:1, 2 * n:3 * n],
           start=True, stop=True, mark="mm")
    dvv = 0
    for a in range(FC):
        ctx.tt(b.tmp[:, 0:n], b.xblk[:, a * n:(a + 1) * n],
               ps[:, c_inv:c_inv + n], ALU.mult)
        dvv = ctx.tt(b.xln[:, a * n:(a + 1) * n], b.tmp[:, 0:n],
                     ps[:, c_mean:c_mean + n], ALU.subtract)
    return dvv


def alloc_aam(ctx, es, cst):
    nc = ctx.nc
    sb = nc.sbuf_tensor
    b = type("B", (), {})()
    b.__dict__.update(cst.__dict__)
    b.xlnb = [es.enter_context(sb(f"aam_xln{i}", [128, FC * 512], BF16))
              for i in range(2)]
    b.wqf = es.enter_context(sb("aam_wqf", [128, 4 * 3 * H], BF16))
    b.Hst = es.enter_context(sb("aam_H", [128, NH * 128], F32))
    b.fs = [es.enter_context(sb(f"aam_fs{i}", [128, 1024], F32))
            for i in range(2)]
    b.qs = [es.enter_context(sb(f"aam_qs{i}", [128, 1024], F32))
            for i in range(2)]
    b.vfm = [es.enter_context(sb(f"aam_vfm{i}", [128, 1024], BF16))
             for i in range(2)]
    for nm in ["tmp", "tm2", "F1", "ks", "gs", "As"]:
        setattr(b, nm, es.enter_context(sb("aam_" + nm, [128, 1024], F32)))
    for nm in ["qt", "kt", "kh", "khT", "sm", "vtok"]:
        setattr(b, nm, es.enter_context(sb("aam_" + nm, [128, 1024], BF16)))
    b.Hb = es.enter_context(sb("aam_Hb", [128, 256], BF16))
    b.cm = es.enter_context(sb("aam_cm", [128, 32], F32))
    b.osbp = [es.enter_context(sb(f"aam_osbp{i}", [128, 512], BF16))
              for i in range(2)]
    b.wc = [es.enter_context(sb(f"aam_wc{i}", [128, 2816], BF16))
            for i in range(2)]
    b.ps = es.enter_context(nc.psum_tensor("aam_ps", [128, 7 * 512], F32))
    b.psb = es.enter_context(nc.psum_tensor("aam_psb", [128, 1024], BF16))
    return b


def aam_A(ctx, b, hh, hp, is_main, bs):
    wslot = (hp % 2) * (6 * H)
    hoff = hh * 2048
    ho5 = hh * 512

    def proj(woff, dst_ev, func):
        pbk = b.pb()
        for i in range(FC):
            wt = b.wqf[:, wslot + woff + i * 128:
                       wslot + woff + (i + 1) * 128]
            ctx.ldw(wt)
            ctx.mm(b.ps[:, pbk * 512:(pbk + 1) * 512], wt,
                   b.xlnb[bs][:, i * 512:(i + 1) * 512],
                   start=(i == 0), stop=(i == FC - 1),
                   mark=("mm" if i == FC - 1 else None))
        ctx.act(dst_ev, b.ps[:, pbk * 512:(pbk + 1) * 512], func)

    if is_main:
        proj(hoff, b.qs[bs][:, ho5:ho5 + 512], AF.Silu)
    proj(4096 + hoff, b.fs[bs][:, ho5:ho5 + 512], AF.Copy)
    proj(8192 + hoff, b.vfm[bs][:, ho5:ho5 + 512], AF.Copy)


def aam_B(ctx, b, is_main, bs):
    fs, qs = b.fs[bs], b.qs[bs]
    ctx.act(b.tmp[:, :], fs[:, :], AF.Abs)
    ctx.act(b.tmp[:, :], b.tmp[:, :], AF.Exp, scale=-1.0)
    ctx.act(b.tm2[:, :], b.tmp[:, :], AF.Ln, bias=1.0)
    ctx.ts(b.gs[:, :], fs[:, :], 0.0, ALU.min)
    ctx.tt(b.gs[:, :], b.gs[:, :], b.tm2[:, :], ALU.subtract)
    for a in range(8):
        ctx.scan(b.As[:, a * CH:(a + 1) * CH],
                 b.gs[:, a * CH:(a + 1) * CH],
                 b.gs[:, a * CH:(a + 1) * CH], 0.0, ALU.add, ALU.bypass)
    for hh in range(2):
        co = hh * 16
        ACm = b.cm[:, co:co + 4]
        mcol = b.cm[:, co + 4:co + 8]
        ctx.act(ACm, ap_stride(b.As, hh * 512 + CH - 1, CH, NCH), AF.Copy)
        ctx.amul(mcol, ACm, 0.5)
        ctx.act(b.cm[:, co + 8:co + 12], ACm, AF.Exp)
        ctx.act(b.cm[:, co + 12:co + 16], mcol, AF.Exp, bias=SH)
        t3 = b.tmp[:, hh * 512:(hh + 1) * 512].rearrange(
            "p (c n) -> p c n", c=NCH)
        a3 = b.As[:, hh * 512:(hh + 1) * 512].rearrange(
            "p (c n) -> p c n", c=NCH)
        ctx.tt(t3, a3, ap_cols_rep(b.cm, co + 4, NCH, CH), ALU.subtract)
    if is_main:
        ctx.act(b.F1[:, :], b.tmp[:, :], AF.Exp, bias=LNC - SH)
        ctx.tt(b.qt[:, :], qs[:, :], b.F1[:, :], ALU.mult)
    ctx.act(b.F1[:, :], b.tmp[:, :], AF.Exp, scale=-1.0, bias=-SH)
    ctx.act(b.tm2[:, :], b.gs[:, :], AF.Exp)
    ctx.act(b.ks[:, :], b.tm2[:, :], AF.Copy, scale=-1.0, bias=1.0)
    ctx.tt(b.kt[:, :], b.ks[:, :], b.F1[:, :], ALU.mult)
    for hh in range(2):
        co = hh * 16
        k3 = b.kh[:, hh * 512:(hh + 1) * 512].rearrange(
            "p (c n) -> p c n", c=NCH)
        kt3 = b.kt[:, hh * 512:(hh + 1) * 512].rearrange(
            "p (c n) -> p c n", c=NCH)
        ctx.tt(k3, kt3, ap_cols_rep(b.cm, co + 12, NCH, CH), ALU.mult)


def aam_C(ctx, b, d, hp, blk, is_main, bs):
    for hh in range(2):
        for cc in range(NCH):
            sl = (hh * 4 + cc) * 128
            ctx.tr(b.psb[:, sl:sl + 128],
                   b.vfm[bs][:, sl:sl + 128], b.ident[:, :], mark="mm")
            ctx.act(b.vtok[:, sl:sl + 128], b.psb[:, sl:sl + 128], AF.Copy)
    for hh in range(2):
        for cc in range(NCH):
            sl = (hh * 4 + cc) * 128
            ctx.tr(b.psb[:, sl:sl + 128],
                   b.kh[:, sl:sl + 128], b.ident[:, :], mark="mm")
            ctx.act(b.khT[:, sl:sl + 128], b.psb[:, sl:sl + 128], AF.Copy)
    for cc in range(NCH):
        for hh in range(2):
            h = hp * 2 + hh
            o = hh * 512 + cc * 128
            co = hh * 16
            Hh = b.Hst[:, h * 128:(h + 1) * 128]
            vt = b.vtok[:, o:o + 128]
            ao = (2 + hh) * 512
            if is_main:
                ctx.amul(b.Hb[:, hh * 128:(hh + 1) * 128], Hh,
                         b.cm[:, co + 12 + cc:co + 13 + cc])
                ctx.ldw(b.Hb[:, hh * 128:(hh + 1) * 128])
                ctx.mm(b.ps[:, ao:ao + 128], b.Hb[:, hh * 128:(hh + 1) * 128],
                       b.qt[:, o:o + 128], start=True, stop=False)
                ctx.ldw(b.kt[:, o:o + 128])
                ctx.mm(b.ps[:, 4 * 512:4 * 512 + 128], b.kt[:, o:o + 128],
                       b.qt[:, o:o + 128], start=True, stop=True, mark="mm")
                ctx.tt(b.sm[:, o:o + 128], b.ps[:, 4 * 512:4 * 512 + 128],
                       b.maskaa[:, :], ALU.mult)
                ctx.ldw(vt)
                ctx.mm(b.ps[:, ao:ao + 128], vt, b.sm[:, o:o + 128],
                       start=False, stop=True, mark="mm")
                ctx.act(b.osbp[hh][:, cc * 128:(cc + 1) * 128],
                        b.ps[:, ao:ao + 128], AF.Copy)
            ctx.ldw(b.khT[:, o:o + 128])
            ctx.mm(b.ps[:, hh * 512:hh * 512 + 128], b.khT[:, o:o + 128],
                   vt, start=True, stop=True, mark="mm")
            ctx.amul(Hh, Hh, b.cm[:, co + 8 + cc:co + 9 + cc])
            ctx.tt(Hh, Hh, b.ps[:, hh * 512:hh * 512 + 128], ALU.add)
    if is_main:
        for hh in range(2):
            h = hp * 2 + hh
            ctx.dma("SP", d.osb_t[:, (blk - 3) * 8192 + h * 512:
                                  (blk - 3) * 8192 + (h + 1) * 512],
                    b.osbp[hh][:, :], "stA" if hh == 0 else "stB")


def emit_aam(ctx, b, d):
    ctx.vmemset(b.Hst[:, :], 0.0)
    pbc = [0]

    def pb():
        pbc[0] ^= 1
        return 5 + pbc[0]
    b.pb = pb

    def load_w(hp):
        wslot = (hp % 2) * (6 * H)
        for hh in range(2):
            h = hp * 2 + hh
            wload(ctx, "POOL", b.wqf, wslot + hh * 2048, d.wa_q, FC, h,
                  "wlA" if hp % 2 == 0 else "wlB")
            wload(ctx, "POOL", b.wqf, wslot + 4096 + hh * 2048, d.wa_f, FC, h,
                  "wlC" if hp % 2 == 0 else "wlD")
            wload(ctx, "POOL", b.wqf, wslot + 8192 + hh * 2048, d.wa_i, FC, h,
                  "wlE" if hp % 2 == 0 else "wlF")

    tasks = [(d.f_wd, d.wd_bf, j * 1408, 1408) for j in range(E * FC)]
    tasks += [(d.f_sd, d.sd_bf, j * 5632 + hf * 2816, 2816)
              for j in range(FC) for hf in range(2)]
    tcur = [0]

    def cast_some(n):
        for _ in range(n):
            if tcur[0] >= len(tasks):
                return
            srcd, dstd, off, w = tasks[tcur[0]]
            sl = tcur[0] % 2
            ctx.dma("POOL", b.wc[sl][:, 0:w], srcd[:, off:off + w], "wlG")
            ctx.dma("SP", dstd[:, off:off + w], b.wc[sl][:, 0:w], "stC")
            tcur[0] += 1

    ctx.dma("SP", b.xlnb[0][:, :], d.xln_t[:, 0:8192], "ldA")
    slots = [(hp, blk) for hp in range(NH // 2) for blk in range(8)]
    prev = None
    for si, (hp, blk) in enumerate(slots):
        cast_some(5)
        if blk == 0:
            load_w(hp)
        is_main = blk >= 3
        bs = blk % 2
        for hh in range(2):
            aam_A(ctx, b, hh, hp, is_main, bs)
        if si + 1 < len(slots):
            nxt = slots[si + 1][1]
            ctx.dma("SP", b.xlnb[nxt % 2][:, :],
                    d.xln_t[:, nxt * 8192:(nxt + 1) * 8192],
                    "ldA" if nxt % 2 == 0 else "ldB")
        if prev is not None:
            aam_C(ctx, b, d, prev[0], prev[1], prev[1] >= 3, prev[1] % 2)
        aam_B(ctx, b, is_main, bs)
        prev = (hp, blk)
    aam_C(ctx, b, d, prev[0], prev[1], True, prev[1] % 2)
    cast_some(len(tasks))


def alloc_awo(ctx, es, cst):
    nc = ctx.nc
    sb = nc.sbuf_tensor
    b = type("B", (), {})()
    b.__dict__.update(cst.__dict__)
    b.osb = [es.enter_context(sb(f"awo_osb{i}", [128, FC * 512], BF16))
             for i in range(2)]
    b.st16 = es.enter_context(sb("awo_st16", [128, FC * 512], BF16))
    b.wo = es.enter_context(sb("awo_wo", [128, 2 * H], BF16))
    b.xr = [es.enter_context(sb(f"awo_xr{i}", [128, 512], F32))
            for i in range(2)]
    b.ob = [es.enter_context(sb(f"awo_ob{i}", [128, 512], F32))
            for i in range(2)]
    b.rows = es.enter_context(sb("awo_rows", [1, 2 * 512], F32))
    b.ps = es.enter_context(nc.psum_tensor("awo_ps", [128, 7 * 512], F32))
    return b


def emit_awo(ctx, b, d):
    ctx.dma("SP", b.osb[0][:, :], d.osb_t[:, 0:8192], "ldA")
    for blk in range(5):
        par = blk % 2
        if blk + 1 < 5:
            ctx.dma("SP", b.osb[(blk + 1) % 2][:, :],
                    d.osb_t[:, (blk + 1) * 8192:(blk + 2) * 8192],
                    "ldA" if (blk + 1) % 2 == 0 else "ldB")
        osb = b.osb[par]
        ctx.act(b.st16[:, :], osb[:, :], AF.Square)
        for a in range(FC):
            ctx.mm(b.ps[0:1, 0:512], b.ones_col[:, :],
                   b.st16[:, a * 512:(a + 1) * 512],
                   start=(a == 0), stop=(a == FC - 1),
                   mark=("mm" if a == FC - 1 else None))
        ctx.amul(b.rows[0:1, 0:512], b.ps[0:1, 0:512], 1.0 / H)
        ctx.act(b.rows[0:1, 512:1024], b.rows[0:1, 0:512], AF.Ln, bias=AA_EPS)
        ctx.act(b.rows[0:1, 0:512], b.rows[0:1, 512:1024], AF.Exp, scale=-0.5)
        ctx.mm(b.ps[:, 4 * 512:5 * 512], b.ones_row[:, :],
               b.rows[0:1, 0:512], start=True, stop=True, mark="mm")
        o3 = osb[:, :].rearrange("p (a n) -> p a n", a=FC)
        ctx.tt(o3, o3, bc_mid(b.ps, 4 * 512, FC, 512), ALU.mult)
        ctx.tt(o3, o3, ap_cols_rep(b.gat, 0, FC, 512), ALU.mult)
        for j in range(FC):
            jp = j % 2
            wload(ctx, "POOL", b.wo, jp * H, d.wa_o, FC, j,
                  "wlG" if jp == 0 else "wlH")
            pbk = 5 + jp
            for i in range(FC):
                wt = b.wo[:, jp * H + i * 128: jp * H + (i + 1) * 128]
                ctx.ldw(wt)
                ctx.mm(b.ps[:, pbk * 512:(pbk + 1) * 512], wt,
                       osb[:, i * 512:(i + 1) * 512],
                       start=(i == 0), stop=(i == FC - 1),
                       mark=("mm" if i == FC - 1 else None))
            ctx.dma("ACT", b.xr[jp][:, :],
                    d.x_t[:, (blk + 3) * 8192 + j * 512:
                          (blk + 3) * 8192 + (j + 1) * 512],
                    "ldC" if jp == 0 else "ldD")
            ctx.tt(b.ob[jp][:, :], b.xr[jp][:, :],
                   b.ps[:, pbk * 512:(pbk + 1) * 512], ALU.add)
            ctx.dma("SP", d.x1_t[:, blk * 8192 + j * 512:
                                 blk * 8192 + (j + 1) * 512],
                    b.ob[jp][:, :], "stA" if jp == 0 else "stB")


def ap3_m(cm):
    h = hdl(cm)
    return bass.AP(h, NCH, [[h.shape[1], 128], [1, NCH], [0, CH]])


def bc_mid(t, off, rep, n):
    h = hdl(t)
    return bass.AP(h, off, [[h.shape[1], 128], [0, rep], [1, n]])


def alloc_ffn_ln(ctx, es, cst, tag):
    nc = ctx.nc
    sb = nc.sbuf_tensor
    com = type("C", (), {})()
    com.ps = es.enter_context(nc.psum_tensor(tag + "_ps", [128, 7 * 512], F32))
    com.psb = es.enter_context(nc.psum_tensor(tag + "_psb", [128, 1024], BF16))
    com.tmpall = es.enter_context(sb(tag + "_tmpa", [128, FC * 512], F32))
    sets = []
    for par in range(2):
        b = type("B", (), {})()
        b.__dict__.update(cst.__dict__)
        t = tag + str(par)
        b.xblk = es.enter_context(sb(t + "_xblk", [128, FC * 512], F32))
        b.st16 = es.enter_context(sb(t + "_st16", [128, FC * 512], BF16))
        b.xln = es.enter_context(sb(t + "_xln", [128, FC * 512], BF16))
        b.rows = es.enter_context(sb(t + "_rows", [1, 3 * 512], F32))
        b.lgb = es.enter_context(sb(t + "_lgb", [16, 512], BF16))
        b.et64 = es.enter_context(sb(t + "_et64", [128, 64], F32))
        b.wk64 = es.enter_context(sb(t + "_wk64", [128, 64], F32))
        b.eq64 = es.enter_context(sb(t + "_eq64", [128, 64], F32))
        b.mxa = es.enter_context(sb(t + "_mxa", [128, 24], F32))
        b.cmb64 = es.enter_context(sb(t + "_cmb64", [128, 64], BF16))
        b.ps = com.ps
        b.psb = com.psb
        b.tmpall = com.tmpall
        b.lnb = (0, 512, 4 * 512, 5 * 512) if par == 0 else \
                (1024, 1536, 4 * 512, 5 * 512)
        b.ldsem = "ldA" if par == 0 else "ldB"
        b.stsem = "wlH" if par == 0 else "wlI"
        b.rtb = 6 * 512
        sets.append(b)
    return sets


def ln_s1(ctx, B, src, blk):
    s0, s1 = B.lnb[0], B.lnb[1]
    n = 512
    ctx.dma("SP", B.xblk[:, 0:FC * n],
            src[:, blk * 8192:blk * 8192 + FC * n], B.ldsem)
    ctx.act(B.st16[:, :FC * n], B.xblk[:, :FC * n], AF.Square)
    for a in range(FC):
        ctx.mm(B.ps[0:1, s1:s1 + n], B.ones_col[:, :],
               B.st16[:, a * n:(a + 1) * n],
               start=(a == 0), stop=(a == FC - 1),
               mark=("mm" if a == FC - 1 else None))
    ctx.act(B.st16[:, :FC * n], B.xblk[:, :FC * n], AF.Copy)
    for a in range(FC):
        ctx.mm(B.ps[0:1, s0:s0 + n], B.ones_col[:, :],
               B.st16[:, a * n:(a + 1) * n],
               start=(a == 0), stop=(a == FC - 1),
               mark=("mm" if a == FC - 1 else None))


def ln_s2(ctx, B):
    s0, s1, c_inv, c_mean = B.lnb
    rows = B.rows
    n = 512
    ctx.amul(rows[0:1, 0:n], B.ps[0:1, s0:s0 + n], 1.0 / H)
    ctx.amul(rows[0:1, n:2 * n], B.ps[0:1, s1:s1 + n], 1.0 / H)
    ctx.tt(rows[0:1, 2 * n:3 * n], rows[0:1, 0:n], rows[0:1, 0:n], ALU.mult)
    ctx.tt(rows[0:1, n:2 * n], rows[0:1, n:2 * n], rows[0:1, 2 * n:3 * n],
           ALU.subtract)
    ctx.act(rows[0:1, 2 * n:3 * n], rows[0:1, n:2 * n], AF.Ln, bias=LN_EPS)
    ctx.act(rows[0:1, n:2 * n], rows[0:1, 2 * n:3 * n], AF.Exp, scale=-0.5)
    ctx.tt(rows[0:1, 2 * n:3 * n], rows[0:1, 0:n], rows[0:1, n:2 * n],
           ALU.mult)
    ctx.mm(B.ps[:, c_inv:c_inv + n], B.ones_row[:, :], rows[0:1, n:2 * n],
           start=True, stop=True, mark="mm")
    ctx.mm(B.ps[:, c_mean:c_mean + n], B.ones_row[:, :],
           rows[0:1, 2 * n:3 * n], start=True, stop=True, mark="mm")


def ln_s3(ctx, B):
    c_inv, c_mean = B.lnb[2], B.lnb[3]
    x3 = B.xblk[:, :].rearrange("p (a n) -> p a n", a=FC)
    t3 = B.tmpall[:, :].rearrange("p (a n) -> p a n", a=FC)
    o3 = B.xln[:, :].rearrange("p (a n) -> p a n", a=FC)
    ctx.tt(t3, x3, bc_mid(B.ps, c_inv, FC, 512), ALU.mult)
    ctx.tt(o3, t3, bc_mid(B.ps, c_mean, FC, 512), ALU.subtract)


def ln_s4(ctx, B, dst, blk, router):
    ctx.dma("POOL", dst[:, blk * 8192:(blk + 1) * 8192],
            B.xln[:, :FC * 512], B.stsem)
    if not router:
        return
    for i in range(FC):
        ctx.mm(B.ps[0:16, B.rtb:B.rtb + 512], B.rw[:, i * 16:(i + 1) * 16],
               B.xln[:, i * 512:(i + 1) * 512],
               start=(i == 0), stop=(i == FC - 1),
               mark=("mm" if i == FC - 1 else None))
    ctx.act(B.lgb[:, :], B.ps[0:16, B.rtb:B.rtb + 512], AF.Copy)


def ln_s5(ctx, B, c0):
    for ts4 in range(4):
        ctx.tr(B.psb[0:128, ts4 * 16:(ts4 + 1) * 16],
               B.lgb[:, ts4 * 128:(ts4 + 1) * 128],
               B.ident[0:16, 0:16], mark="mm")
    ctx.act(B.et64[:, :], B.psb[0:128, 0:64], AF.Exp)
    ctx.act(B.wk64[:, :], B.et64[:, :], AF.Copy)
    w3 = B.wk64[:, :].rearrange("p (a n) -> p a n", a=4)
    e3 = B.eq64[:, :].rearrange("p (a n) -> p a n", a=4)
    for it in range(4):
        ctx.red(B.mxa[:, it * 4:(it + 1) * 4], w3, ALU.max)
        if it < 3:
            ctx.tt(e3, w3, ap_cols_rep(B.mxa, it * 4, 4, 16), ALU.is_equal)
            ctx.ts(B.eq64[:, :], B.eq64[:, :], -BIGNEG, ALU.mult)
            ctx.tt(B.wk64[:, :], B.wk64[:, :], B.eq64[:, :], ALU.add)
    t3 = B.et64[:, :].rearrange("p (a n) -> p a n", a=4)
    ctx.tt(e3, t3, ap_cols_rep(B.mxa, 12, 4, 16), ALU.is_ge)
    ctx.tt(B.eq64[:, :], B.et64[:, :], B.eq64[:, :], ALU.mult)
    ctx.red(B.mxa[:, 16:20], e3, ALU.add)
    ctx.ts(B.mxa[:, 16:20], B.mxa[:, 16:20], 1e-9, ALU.max)
    ctx.recip(B.mxa[:, 20:24], B.mxa[:, 16:20])
    c3 = B.cmb64[:, :].rearrange("p (a n) -> p a n", a=4)
    ctx.tt(c3, e3, ap_cols_rep(B.mxa, 20, 4, 16), ALU.mult)
    for ts4 in range(4):
        ctx.tr(B.psb[0:16, 512 + ts4 * 128:512 + (ts4 + 1) * 128],
               B.cmb64[:, ts4 * 16:(ts4 + 1) * 16],
               B.ident[:, :], mark="mm")
    ctx.act(B.comb_t[0:16, c0:c0 + 512], B.psb[0:16, 512:1024], AF.Copy)


def emit_ffn_ln(ctx, sets, d, src, n_tok, router=True, dst=None):
    if dst is None:
        dst = d.xln_t
    nblk = n_tok // 512
    ln_s1(ctx, sets[0], src, 0)
    if nblk > 1:
        ln_s1(ctx, sets[1], src, 1)
    for blk in range(nblk):
        B = sets[blk % 2]
        ln_s2(ctx, B)
        ln_s3(ctx, B)
        if blk + 2 < nblk:
            ln_s1(ctx, sets[blk % 2], src, blk + 2)
        ln_s4(ctx, B, dst, blk, router)
        if router:
            ln_s5(ctx, B, blk * 512)


def alloc_ffn_moe(ctx, es, cst, tag):
    nc = ctx.nc
    b = type("B", (), {})()
    b.__dict__.update(cst.__dict__)
    sb = nc.sbuf_tensor
    b.xg = es.enter_context(sb(tag + "_xg", [128, FC * 1024], BF16))
    b.acc = es.enter_context(sb(tag + "_acc", [128, FC * 1024], F32))
    b.big = es.enter_context(sb(tag + "_big", [128, JSH * 512], BF16))
    b.ws = es.enter_context(sb(tag + "_ws", [128, 4 * 2048], BF16))
    b.wsd = es.enter_context(sb(tag + "_wsd", [128, 2 * 22 * 128], BF16))
    b.cbc = es.enter_context(sb(tag + "_cbc", [128, 1024], BF16))
    b.sg1 = es.enter_context(sb(tag + "_sg1", [128, 4 * 512], BF16))
    b.sgb = es.enter_context(sb(tag + "_sgb", [128, 512], BF16))
    b.sgw = es.enter_context(sb(tag + "_sgw", [128, 16], BF16))
    b.tmp = es.enter_context(sb(tag + "_tmpm", [128, 1024], F32))
    b.ob = es.enter_context(sb(tag + "_ob", [128, 1024], F32))
    b.rows = es.enter_context(sb(tag + "_rowm", [1, 3 * 512], F32))
    b.ps = es.enter_context(nc.psum_tensor(tag + "_psm", [128, 8 * 512], F32))
    return b


def emit_ffn_moe(ctx, b, d, src, dst, n_tok):
    groups = []
    t0 = 0
    while t0 < n_tok:
        tg = min(1024, n_tok - t0)
        groups.append((t0, tg))
        t0 += tg
    ctx.dma("POOL", b.sgw[:, :], d.f_sgw[:, :], "wlG")

    def load_xg(gi):
        if gi >= len(groups):
            return
        g0n, tgn = groups[gi]
        for t2 in range(tgn // 512):
            ctx.dma("SP", b.xg[:, t2 * 8192:(t2 + 1) * 8192],
                    d.xln_t[:, (g0n // 512 + t2) * 8192:
                            (g0n // 512 + t2 + 1) * 8192],
                    "ldB" if t2 == 0 else "ldA")

    load_xg(0)
    for gi, (g0, tg) in enumerate(groups):
        nb = tg // 512
        for e in range(E):
            ctx.ldw(b.emat[:, e * 128:(e + 1) * 128])
            for t2 in range(nb):
                ctx.mm(b.ps[:, t2 * 512:(t2 + 1) * 512],
                       b.emat[:, e * 128:(e + 1) * 128],
                       b.comb_t[0:16, g0 + t2 * 512:g0 + (t2 + 1) * 512],
                       start=True, stop=True,
                       mark=("mm" if t2 == nb - 1 else None))
            ctx.act(b.cbc[:, 0:tg], b.ps[:, 0:tg], AF.Copy)
            for jj in range(JM):
                p = jj % 2
                wload(ctx, "POOL", b.ws, p * 2048, d.f_wg, FC, e * JM + jj,
                      "wlA" if p == 0 else "wlB")
                for i in range(FC):
                    ctx.ldw(b.ws[:, p * 2048 + i * 128: p * 2048 + (i + 1) * 128])
                    for t2 in range(nb):
                        ctx.mm(b.ps[:, (p * 2 + t2) * 512:(p * 2 + t2 + 1) * 512],
                               b.ws[:, p * 2048 + i * 128: p * 2048 + (i + 1) * 128],
                               b.xg[:, t2 * 8192 + i * 512: t2 * 8192 + (i + 1) * 512],
                               start=(i == 0), stop=(i == FC - 1),
                               mark=("mm" if (i == FC - 1 and t2 == nb - 1) else None))
                for t2 in range(nb):
                    ctx.act(b.sg1[:, (p * 2 + t2) * 512:(p * 2 + t2 + 1) * 512],
                            b.ps[:, (p * 2 + t2) * 512:(p * 2 + t2 + 1) * 512],
                            AF.Silu)
                wload(ctx, "POOL", b.ws, (2 + p) * 2048, d.f_wu, FC,
                      e * JM + jj, "wlC" if p == 0 else "wlD")
                for i in range(FC):
                    ctx.ldw(b.ws[:, (2 + p) * 2048 + i * 128:
                                 (2 + p) * 2048 + (i + 1) * 128])
                    for t2 in range(nb):
                        ctx.mm(b.ps[:, (4 + p * 2 + t2) * 512:
                                    (4 + p * 2 + t2 + 1) * 512],
                               b.ws[:, (2 + p) * 2048 + i * 128:
                                    (2 + p) * 2048 + (i + 1) * 128],
                               b.xg[:, t2 * 8192 + i * 512: t2 * 8192 + (i + 1) * 512],
                               start=(i == 0), stop=(i == FC - 1),
                               mark=("mm" if (i == FC - 1 and t2 == nb - 1) else None))
                for t2 in range(nb):
                    ctx.tt(b.tmp[:, t2 * 512:(t2 + 1) * 512],
                           b.sg1[:, (p * 2 + t2) * 512:(p * 2 + t2 + 1) * 512],
                           b.ps[:, (4 + p * 2 + t2) * 512:
                                (4 + p * 2 + t2 + 1) * 512], ALU.mult)
                    ctx.tt(b.big[:, t2 * 5632 + jj * 512: t2 * 5632 + (jj + 1) * 512],
                           b.tmp[:, t2 * 512:(t2 + 1) * 512],
                           b.cbc[:, t2 * 512:(t2 + 1) * 512], ALU.mult)
            for j2 in range(FC):
                p = j2 % 2
                wload(ctx, "SP", b.ws, p * 2048, d.wd_bf, JM, e * FC + j2,
                      "wlE" if p == 0 else "wlF")
                for k in range(JM):
                    ctx.ldw(b.ws[:, p * 2048 + k * 128: p * 2048 + (k + 1) * 128])
                    for t2 in range(nb):
                        ctx.mm(b.ps[:, (p * 2 + t2) * 512:(p * 2 + t2 + 1) * 512],
                               b.ws[:, p * 2048 + k * 128: p * 2048 + (k + 1) * 128],
                               b.big[:, t2 * 5632 + k * 512: t2 * 5632 + (k + 1) * 512],
                               start=(k == 0), stop=(k == JM - 1),
                               mark=("mm" if (k == JM - 1 and t2 == nb - 1) else None))
                for t2 in range(nb):
                    if e == 0:
                        ctx.act(b.acc[:, t2 * 8192 + j2 * 512: t2 * 8192 + (j2 + 1) * 512],
                                b.ps[:, (p * 2 + t2) * 512:(p * 2 + t2 + 1) * 512],
                                AF.Copy)
                    else:
                        ctx.tt(b.acc[:, t2 * 8192 + j2 * 512: t2 * 8192 + (j2 + 1) * 512],
                               b.acc[:, t2 * 8192 + j2 * 512: t2 * 8192 + (j2 + 1) * 512],
                               b.ps[:, (p * 2 + t2) * 512:(p * 2 + t2 + 1) * 512],
                               ALU.add)
        for t2 in range(nb):
            for i in range(FC):
                ctx.mm(b.ps[0:1, 6 * 512:6 * 512 + 512], b.sgw[:, i:i + 1],
                       b.xg[:, t2 * 8192 + i * 512: t2 * 8192 + (i + 1) * 512],
                       start=(i == 0), stop=(i == FC - 1),
                       mark=("mm" if i == FC - 1 else None))
            ctx.act(b.rows[0:1, 0:512], b.ps[0:1, 6 * 512:6 * 512 + 512],
                    AF.Exp, scale=-1.0)
            ctx.ts(b.rows[0:1, 512:1024], b.rows[0:1, 0:512], 1.0, ALU.add)
            ctx.recip(b.rows[0:1, 0:512], b.rows[0:1, 512:1024])
            ctx.mm(b.ps[:, 6 * 512:6 * 512 + 512], b.ones_row[:, :],
                   b.rows[0:1, 0:512], start=True, stop=True, mark="mm")
            ctx.act(b.sgb[:, :], b.ps[:, 6 * 512:6 * 512 + 512], AF.Copy)
            for jj in range(JSH):
                p = jj % 2
                wload(ctx, "POOL", b.ws, p * 2048, d.f_sg, FC, jj,
                      "wlA" if p == 0 else "wlB")
                for i in range(FC):
                    ctx.ldw(b.ws[:, p * 2048 + i * 128: p * 2048 + (i + 1) * 128])
                    ctx.mm(b.ps[:, p * 512:(p + 1) * 512],
                           b.ws[:, p * 2048 + i * 128: p * 2048 + (i + 1) * 128],
                           b.xg[:, t2 * 8192 + i * 512: t2 * 8192 + (i + 1) * 512],
                           start=(i == 0), stop=(i == FC - 1),
                           mark=("mm" if i == FC - 1 else None))
                ctx.act(b.sg1[:, p * 512:(p + 1) * 512],
                        b.ps[:, p * 512:(p + 1) * 512], AF.Silu)
                wload(ctx, "POOL", b.ws, (2 + p) * 2048, d.f_su, FC, jj,
                      "wlC" if p == 0 else "wlD")
                for i in range(FC):
                    ctx.ldw(b.ws[:, (2 + p) * 2048 + i * 128:
                                 (2 + p) * 2048 + (i + 1) * 128])
                    ctx.mm(b.ps[:, (2 + p) * 512:(3 + p) * 512],
                           b.ws[:, (2 + p) * 2048 + i * 128:
                                (2 + p) * 2048 + (i + 1) * 128],
                           b.xg[:, t2 * 8192 + i * 512: t2 * 8192 + (i + 1) * 512],
                           start=(i == 0), stop=(i == FC - 1),
                           mark=("mm" if i == FC - 1 else None))
                ctx.tt(b.big[:, jj * 512:(jj + 1) * 512],
                       b.sg1[:, p * 512:(p + 1) * 512],
                       b.ps[:, (2 + p) * 512:(3 + p) * 512], ALU.mult)
            if t2 == nb - 1:
                load_xg(gi + 1)
            for j2 in range(FC):
                p = j2 % 2
                for hf in range(2):
                    k0, k1 = (0, 22) if hf == 0 else (22, JSH)
                    ph = (j2 * 2 + hf) % 2
                    wload(ctx, "SP", b.wsd, ph * 22 * 128, d.sd_bf, JSH, j2,
                          "wlE" if ph == 0 else "wlF", k0=k0, k1=k1)
                    for k in range(k0, k1):
                        ctx.ldw(b.wsd[:, ph * 22 * 128 + (k - k0) * 128:
                                      ph * 22 * 128 + (k - k0 + 1) * 128])
                        ctx.mm(b.ps[:, (4 + p) * 512:(5 + p) * 512],
                               b.wsd[:, ph * 22 * 128 + (k - k0) * 128:
                                     ph * 22 * 128 + (k - k0 + 1) * 128],
                               b.big[:, k * 512:(k + 1) * 512],
                               start=(k == 0), stop=(k == JSH - 1),
                               mark=("mm" if k == JSH - 1 else None))
                ctx.tt(b.tmp[:, 0:512], b.ps[:, (4 + p) * 512:(5 + p) * 512],
                       b.sgb[:, :], ALU.mult)
                ctx.tt(b.acc[:, t2 * 8192 + j2 * 512: t2 * 8192 + (j2 + 1) * 512],
                       b.acc[:, t2 * 8192 + j2 * 512: t2 * 8192 + (j2 + 1) * 512],
                       b.tmp[:, 0:512], ALU.add)
        for j2 in range(FC):
            buf = b.ob if j2 % 2 == 0 else b.tmp
            for t2 in range(tg // 512):
                ctx.dma("ACT", buf[:, t2 * 512:(t2 + 1) * 512],
                        src[:, (g0 // 512 + t2) * 8192 + j2 * 512:
                            (g0 // 512 + t2) * 8192 + (j2 + 1) * 512],
                        "ldC" if j2 % 2 == 0 else "ldD")
                ctx.tt(buf[:, t2 * 512:(t2 + 1) * 512],
                       buf[:, t2 * 512:(t2 + 1) * 512],
                       b.acc[:, t2 * 8192 + j2 * 512:
                             t2 * 8192 + (j2 + 1) * 512], ALU.add)
                ctx.dma("SP", dst[:, (g0 // 512 + t2) * 8192 + j2 * 512:
                                  (g0 // 512 + t2) * 8192 + (j2 + 1) * 512],
                        buf[:, t2 * 512:(t2 + 1) * 512],
                        "stA" if j2 % 2 == 0 else "stB")


def alloc_ab(ctx, es, cst):
    nc = ctx.nc
    b = type("B", (), {})()
    b.__dict__.update(cst.__dict__)
    sb = nc.sbuf_tensor
    b.xlnb = [es.enter_context(sb(f"ab_xln{i}", [128, FC * 512], BF16))
              for i in range(2)]
    b.q = es.enter_context(sb("ab_q", [128, FC * 512], BF16))
    b.kring = es.enter_context(sb("ab_kring", [128, FC * 1024], BF16))
    b.vring = es.enter_context(sb("ab_vring", [128, 8 * H], BF16))
    b.osb = es.enter_context(sb("ab_osb", [128, NH * 512], BF16))
    b.ropec = es.enter_context(sb("ab_ropec", [128, T_MAIN], BF16))
    b.ropes = es.enter_context(sb("ab_ropes", [128, T_MAIN], BF16))
    b.pswap = es.enter_context(sb("ab_pswap", [128, 128], BF16))
    b.kdr = es.enter_context(sb("ab_kdr", [128, 512], BF16))
    b.bmask = es.enter_context(sb("ab_bmask", [128, 8 * 512], BF16))
    b.pt = es.enter_context(sb("ab_pt", [128, 8 * 512], BF16))
    b.rb = es.enter_context(sb("ab_rb", [128, 512], F32))
    b.t1 = es.enter_context(sb("ab_t1", [128, 512], F32))
    b.tmp = es.enter_context(sb("ab_tmp", [128, 512], F32))
    b.rows = es.enter_context(sb("ab_rows", [1, 3 * 512], F32))
    b.wsl = es.enter_context(sb("ab_wsl", [128, 4 * H], BF16))
    b.xr = [es.enter_context(sb(f"ab_xr{i}", [128, 512], F32))
            for i in range(2)]
    b.ob = [es.enter_context(sb(f"ab_ob{i}", [128, 512], F32))
            for i in range(2)]
    b.ps = es.enter_context(nc.psum_tensor("ab_ps", [128, 8 * 512], F32))
    return b


def emit_ab(ctx, b, d):
    ctx.dma("SP", b.ropec[:, :], d.ropec[:, :], "stC")
    ctx.dma("SP", b.ropes[:, :], d.ropes[:, :], "stC")
    ctx.dma("POOL", b.pswap[:, :], d.c_pswap[:, :], "wlI")
    ctx.dma("POOL", b.bmask[:, :], d.bmask[:, 8 * 512:16 * 512], "wlJ")
    ctx.dma("SP", b.xlnb[0][:, :], d.xln_t[:, 0:8192], "ldA")

    def rope_apply(dst, psrc):
        ctx.act(b.kdr[:, :], psrc, AF.Copy)
        ctx.ldw(b.pswap[:, :])
        ctx.mm(b.ps[:, 7 * 512:8 * 512], b.pswap[:, :],
               b.kdr[:, :], start=True, stop=True, mark="mm")
        ctx.tt(b.t1[:, :], b.kdr[:, :], b.ropec[:, rc0:rc0 + 512], ALU.mult)
        ctx.tt(b.tmp[:, :], b.ps[:, 7 * 512:8 * 512],
               b.ropes[:, rc0:rc0 + 512], ALU.mult)
        ctx.tt(dst, b.t1[:, :], b.tmp[:, :], ALU.add)

    for pb in range(5):
        c0 = pb * 512
        rc0 = c0
        qb = pb - 1
        xln = b.xlnb[pb % 2]
        rs = pb % 2
        for hh in range(NH):
            wload(ctx, "POOL", b.wsl, (hh % 2) * H, d.wb_k, FC, hh,
                  "wlA" if hh % 2 == 0 else "wlB")
            for i in range(FC):
                ctx.ldw(b.wsl[:, (hh % 2) * H + i * 128:(hh % 2) * H + (i + 1) * 128])
                ctx.mm(b.ps[:, 6 * 512:7 * 512],
                       b.wsl[:, (hh % 2) * H + i * 128:(hh % 2) * H + (i + 1) * 128],
                       xln[:, i * 512:(i + 1) * 512],
                       start=(i == 0), stop=(i == FC - 1),
                       mark=("mm" if i == FC - 1 else None))
            kd = b.kring[:, hh * 1024 + rs * 512: hh * 1024 + (rs + 1) * 512]
            rope_apply(kd, b.ps[:, 6 * 512:7 * 512])
        if pb + 1 < 5:
            ctx.dma("SP", b.xlnb[(pb + 1) % 2][:, :],
                    d.xln_t[:, (pb + 1) * 8192:(pb + 2) * 8192],
                    "ldA" if (pb + 1) % 2 == 0 else "ldB")
        for half in range(2):
            for i in range(FC):
                ctx.dma("POOL", b.wi_ab[:, (i % 2) * H:(i % 2) * H + 1024],
                        d.wb_v[i * 128:(i + 1) * 128,
                               half * 1024:(half + 1) * 1024],
                        "wlE" if i % 2 == 0 else "wlF")
                for ts4 in range(4):
                    xtile = xln[:, i * 512 + ts4 * 128:
                                i * 512 + (ts4 + 1) * 128]
                    ctx.ldw(xtile)
                    for g2 in range(2):
                        bank = ts4 * 2 + g2
                        ctx.mm(b.ps[:, bank * 512:(bank + 1) * 512], xtile,
                               b.wi_ab[:, (i % 2) * H + g2 * 512:
                                       (i % 2) * H + (g2 + 1) * 512],
                               start=(i == 0), stop=(i == FC - 1),
                               mark=("mm" if (ts4 == 3 and g2 == 1) else None))
            for ts4 in range(4):
                for g2 in range(2):
                    bank = ts4 * 2 + g2
                    ng = half * 2 + g2
                    ctx.act(b.vring[:, (rs * 4 + ts4) * H + ng * 512:
                                    (rs * 4 + ts4) * H + (ng + 1) * 512],
                            b.ps[:, bank * 512:(bank + 1) * 512], AF.Copy)
        if pb == 0:
            continue
        for hh in range(NH):
            wload(ctx, "POOL", b.wsl, (2 + hh % 2) * H, d.wb_q, FC, hh,
                  "wlC" if hh % 2 == 0 else "wlD")
            for i in range(FC):
                ctx.ldw(b.wsl[:, (2 + hh % 2) * H + i * 128:
                              (2 + hh % 2) * H + (i + 1) * 128])
                ctx.mm(b.ps[:, 6 * 512:7 * 512],
                       b.wsl[:, (2 + hh % 2) * H + i * 128:
                             (2 + hh % 2) * H + (i + 1) * 128],
                       xln[:, i * 512:(i + 1) * 512],
                       start=(i == 0), stop=(i == FC - 1),
                       mark=("mm" if i == FC - 1 else None))
            qd = b.q[:, hh * 512:(hh + 1) * 512]
            rope_apply(qd, b.ps[:, 6 * 512:7 * 512])
        for hh in range(NH):
            for p2 in range(2):
                for q4 in range(4):
                    cc = p2 * 4 + q4
                    pbk = qb + (1 if cc >= 4 else 0)
                    slot = pbk % 2
                    ts = cc % 4
                    ktile = b.kring[:, hh * 1024 + slot * 512 + ts * 128:
                                    hh * 1024 + slot * 512 + (ts + 1) * 128]
                    ctx.ldw(ktile)
                    ctx.mm(b.ps[:, q4 * 512:(q4 + 1) * 512],
                           ktile, b.q[:, hh * 512:(hh + 1) * 512],
                           start=True, stop=True,
                           mark=("mm" if q4 == 3 else None))
                ctx.act(b.pt[:, p2 * 2048:(p2 + 1) * 2048],
                        b.ps[:, 0:2048], AF.Exp, scale=BQS)
                ctx.tt(b.pt[:, p2 * 2048:(p2 + 1) * 2048],
                       b.pt[:, p2 * 2048:(p2 + 1) * 2048],
                       b.bmask[:, p2 * 2048:(p2 + 1) * 2048], ALU.mult)
            for cc in range(8):
                pbk = qb + (1 if cc >= 4 else 0)
                slot = pbk % 2
                ts = cc % 4
                ctx.mm(b.ps[0:1, 4 * 512:4 * 512 + 512], b.ones_col[:, :],
                       b.pt[:, cc * 512:(cc + 1) * 512],
                       start=(cc == 0), stop=(cc == 7),
                       mark=("mm" if cc == 7 else None))
                vtile = b.vring[:, (slot * 4 + ts) * H + hh * 128:
                                (slot * 4 + ts) * H + (hh + 1) * 128]
                ctx.ldw(vtile)
                ctx.mm(b.ps[:, 5 * 512:6 * 512], vtile,
                       b.pt[:, cc * 512:(cc + 1) * 512],
                       start=(cc == 0), stop=(cc == 7),
                       mark=("mm" if cc == 7 else None))
            ctx.recip(b.rows[0:1, 0:512], b.ps[0:1, 4 * 512:4 * 512 + 512])
            ctx.mm(b.ps[:, 4 * 512:5 * 512], b.ones_row[:, :],
                   b.rows[0:1, 0:512], start=True, stop=True, mark="mm")
            ctx.act(b.rb[:, :], b.ps[:, 4 * 512:5 * 512], AF.Copy)
            ctx.tt(b.osb[:, hh * 512:(hh + 1) * 512], b.ps[:, 5 * 512:6 * 512],
                   b.rb[:, :], ALU.mult)
        for j in range(FC):
            jp = j % 2
            wload(ctx, "POOL", b.wsl, jp * H, d.wb_o, FC, j,
                  "wlG" if jp == 0 else "wlH")
            for i in range(FC):
                ctx.ldw(b.wsl[:, jp * H + i * 128:jp * H + (i + 1) * 128])
                ctx.mm(b.ps[:, 6 * 512:7 * 512],
                       b.wsl[:, jp * H + i * 128:jp * H + (i + 1) * 128],
                       b.osb[:, i * 512:(i + 1) * 512],
                       start=(i == 0), stop=(i == FC - 1),
                       mark=("mm" if i == FC - 1 else None))
            ctx.dma("ACT", b.xr[jp][:, :],
                    d.x2_t[:, pb * 8192 + j * 512:pb * 8192 + (j + 1) * 512],
                    "ldC" if jp == 0 else "ldD")
            ctx.tt(b.ob[jp][:, :], b.xr[jp][:, :],
                   b.ps[:, 6 * 512:7 * 512], ALU.add)
            ctx.dma("SP", d.x3_t[:, qb * 8192 + j * 512:
                                 qb * 8192 + (j + 1) * 512],
                    b.ob[jp][:, :], "stA" if jp == 0 else "stB")
        if pb == 1:
            ctx.dma("POOL", b.bmask[:, :], d.bmask[:, 0:8 * 512], "wlJ")


def build_program():
    nc = bass.Bass("TRN2", target_bir_lowering=False)
    d = type("D", (), {})()
    dt = nc.dram_tensor
    d.x_t = dt("x_t", [128, 8 * 8192], F32, kind="ExternalInput").ap()
    d.wa_q = dt("wa_q", [128, H * FC], F32, kind="ExternalInput").ap()
    d.wa_f = dt("wa_f", [128, H * FC], F32, kind="ExternalInput").ap()
    d.wa_i = dt("wa_i", [128, H * FC], F32, kind="ExternalInput").ap()
    d.wa_o = dt("wa_o", [128, H * FC], F32, kind="ExternalInput").ap()
    d.ga_t = dt("ga_t", [128, FC], F32, kind="ExternalInput").ap()
    d.wb_q = dt("wb_q", [128, H * FC], F32, kind="ExternalInput").ap()
    d.wb_k = dt("wb_k", [128, H * FC], F32, kind="ExternalInput").ap()
    d.wb_v = dt("wb_v", [H, H], F32, kind="ExternalInput").ap()
    d.wb_o = dt("wb_o", [128, H * FC], F32, kind="ExternalInput").ap()
    d.f_rw = dt("f_rw", [128, FC * 16], F32, kind="ExternalInput").ap()
    d.f_wg = dt("f_wg", [128, E * IM * FC], F32, kind="ExternalInput").ap()
    d.f_wu = dt("f_wu", [128, E * IM * FC], F32, kind="ExternalInput").ap()
    d.f_wd = dt("f_wd", [128, E * H * JM], F32, kind="ExternalInput").ap()
    d.f_sg = dt("f_sg", [128, ISH * FC], F32, kind="ExternalInput").ap()
    d.f_su = dt("f_su", [128, ISH * FC], F32, kind="ExternalInput").ap()
    d.f_sd = dt("f_sd", [128, H * JSH], F32, kind="ExternalInput").ap()
    d.f_sgw = dt("f_sgw", [128, FC], F32, kind="ExternalInput").ap()
    d.c_ident = dt("c_ident", [128, 128], BF16, kind="ExternalInput").ap()
    d.c_ones_col = dt("c_ones_col", [128, 1], BF16, kind="ExternalInput").ap()
    d.c_ones_row = dt("c_ones_row", [1, 128], F32, kind="ExternalInput").ap()
    d.c_maskaa = dt("c_maskaa", [128, 128], BF16, kind="ExternalInput").ap()
    d.c_emat = dt("c_emat", [16, 16 * 128], BF16, kind="ExternalInput").ap()
    d.c_pswap = dt("c_pswap", [128, 128], BF16, kind="ExternalInput").ap()
    d.c_bias = dt("c_bias", [128, 4], F32, kind="ExternalInput").ap()
    d.c_bias2 = dt("c_bias2", [128, 4], F32, kind="ExternalInput").ap()
    d.ropec = dt("ropec", [128, T_MAIN], BF16, kind="ExternalInput").ap()
    d.ropes = dt("ropes", [128, T_MAIN], BF16, kind="ExternalInput").ap()
    d.bmask = dt("bmask", [128, 16 * 512], BF16, kind="ExternalInput").ap()
    d.x1_t = dt("x1_t", [128, 5 * 8192], F32).ap()
    d.x2_t = dt("x2_t", [128, 5 * 8192], F32).ap()
    d.x3_t = dt("x3_t", [128, 4 * 8192], F32).ap()
    d.xln_t = dt("xln_t", [128, 8 * 8192], BF16).ap()
    d.osb_t = dt("osb_t", [128, 5 * 8192], BF16).ap()
    d.wd_bf = dt("wd_bf", [128, E * H * JM], BF16).ap()
    d.sd_bf = dt("sd_bf", [128, H * JSH], BF16).ap()
    d.y_t = dt("y_t", [128, 4 * 8192], F32, kind="ExternalOutput").ap()

    with contextlib.ExitStack() as es0:
        sems = {k: es0.enter_context(nc.semaphore("s_" + k)) for k in SEMS}
        block = es0.enter_context(nc.Block())
        ctx = Ctx(nc, block, sems)

        cst = type("C", (), {})()
        cst.ident = es0.enter_context(nc.sbuf_tensor("c_ident_s", [128, 128], BF16))
        cst.ones_col = es0.enter_context(nc.sbuf_tensor("c_onec_s", [128, 1], BF16))
        cst.ones_row = es0.enter_context(nc.sbuf_tensor("c_oner_s", [1, 128], F32))
        cst.maskaa = es0.enter_context(nc.sbuf_tensor("c_maa_s", [128, 128], BF16))
        cst.emat = es0.enter_context(nc.sbuf_tensor("c_emat_s", [16, 16 * 128], BF16))
        cst.gat = es0.enter_context(nc.sbuf_tensor("c_gat_s", [128, FC], F32))
        cst.rw = es0.enter_context(nc.sbuf_tensor("c_rw_s", [128, FC * 16], BF16))
        cst.comb_t = es0.enter_context(nc.sbuf_tensor("c_combt", [16, T_MAIN], BF16))
        cst.cbias = es0.enter_context(nc.sbuf_tensor("c_cbias", [128, 4], F32))
        cst.cbias2 = es0.enter_context(nc.sbuf_tensor("c_cbias2", [128, 4], F32))
        nc.const_aps.register(F32, LN_EPS, cst.cbias[:, 0:1])
        nc.const_aps.register(F32, 1.0, cst.cbias[:, 1:2])
        nc.const_aps.register(F32, LNC, cst.cbias[:, 2:3])
        nc.const_aps.register(F32, LNC - SH, cst.cbias2[:, 0:1])
        nc.const_aps.register(F32, -SH, cst.cbias2[:, 1:2])
        nc.const_aps.register(F32, SH, cst.cbias2[:, 2:3])

        def ph_init(c):
            c.dma("SP", cst.ident[:, :], d.c_ident[:, :], "ldA")
            c.dma("SP", cst.ones_col[:, :], d.c_ones_col[:, :], "ldB")
            c.dma("SP", cst.ones_row[:, :], d.c_ones_row[:, :], "ldC")
            c.dma("SP", cst.maskaa[:, :], d.c_maskaa[:, :], "ldD")
            c.dma("SP", cst.emat[:, :], d.c_emat[:, :], "stA")
            c.dma("SP", cst.gat[:, :], d.ga_t[:, :], "stB")
            c.dma("SP", cst.cbias[:, :], d.c_bias[:, :], "stC")
            c.dma("POOL", cst.cbias2[:, :], d.c_bias2[:, :], "wlB")
            c.dma("POOL", cst.rw[:, :], d.f_rw[:, :], "wlA")

        run_phase(ctx, ph_init, "init")

        with contextlib.ExitStack() as es:
            b = alloc_ffn_ln(ctx, es, cst, "al")
            run_phase(ctx, lambda c: emit_ffn_ln(c, b, d, d.x_t, 8 * 512,
                                                 router=False), "aln")
        with contextlib.ExitStack() as es:
            b = alloc_aam(ctx, es, cst)
            run_phase(ctx, lambda c: emit_aam(c, b, d), "aam")
        with contextlib.ExitStack() as es:
            b = alloc_awo(ctx, es, cst)
            run_phase(ctx, lambda c: emit_awo(c, b, d), "awo")
        with contextlib.ExitStack() as es:
            b = alloc_ffn_ln(ctx, es, cst, "f1l")
            run_phase(ctx, lambda c: emit_ffn_ln(c, b, d, d.x1_t, T_MAIN), "f1l")
        with contextlib.ExitStack() as es:
            b = alloc_ffn_moe(ctx, es, cst, "f1m")
            run_phase(ctx, lambda c: emit_ffn_moe(c, b, d, d.x1_t, d.x2_t,
                                                  T_MAIN), "f1m")
        with contextlib.ExitStack() as es:
            b = alloc_ffn_ln(ctx, es, cst, "bl")
            run_phase(ctx, lambda c: emit_ffn_ln(c, b, d, d.x2_t, T_MAIN,
                                                 router=False), "abl")
        with contextlib.ExitStack() as es:
            b = alloc_ab(ctx, es, cst)
            b.wi_ab = b.wsl
            run_phase(ctx, lambda c: emit_ab(c, b, d), "ab")
        with contextlib.ExitStack() as es:
            b = alloc_ffn_ln(ctx, es, cst, "f2l")
            run_phase(ctx, lambda c: emit_ffn_ln(c, b, d, d.x3_t, OWN), "f2l")
        with contextlib.ExitStack() as es:
            b = alloc_ffn_moe(ctx, es, cst, "f2m")
            run_phase(ctx, lambda c: emit_ffn_moe(c, b, d, d.x3_t, d.y_t, OWN),
                      "f2m")

        def ph_fin(c):
            pass

        run_phase(ctx, ph_fin, "fin")
    return nc


def _rope_tables(positions):
    inv = 1.0 / (10000.0 ** (np.arange(0, DK, 2, dtype=np.float64) / DK))
    ang = inv[:, None] * positions[None, :].astype(np.float64)
    return np.cos(ang).astype(np.float32), np.sin(ang).astype(np.float32)


def _bmask(parity):
    out = np.zeros((16, 128, 512), dtype=np.float32)
    tk = np.arange(128)[:, None]
    tq = np.arange(512)[None, :]
    for c in range(8):
        dd = tq - tk - 128 * c + 512
        m = ((dd >= 0) & (dd < 512)).astype(np.float32)
        out[c] = m
        if parity == 1:
            out[8 + c] = m
        else:
            out[8 + c] = m if c >= 4 else 0.0
    return out.reshape(16 * 128, 512).astype(bfdt)


def _colchunk(w):
    K, D = w.shape
    return np.ascontiguousarray(
        w.reshape(K // 128, 128, D // 128, 128).transpose(1, 2, 0, 3)
        .reshape(128, -1))


def _blocked(xt):
    Hh, ncols = xt.shape
    nblk = ncols // 512
    return np.ascontiguousarray(
        xt.reshape(16, 128, nblk, 512).transpose(1, 2, 0, 3).reshape(128, -1))


def _perm_deinterleave():
    p = np.zeros(H, dtype=np.int64)
    for h in range(NH):
        base = h * DK
        p[base:base + 64] = base + 2 * np.arange(64)
        p[base + 64:base + DK] = base + 2 * np.arange(64) + 1
    return p


def make_in_maps(L0_attn_a_x, L0_attn_a_Wq, L0_attn_a_Wf, L0_attn_a_Wi,
                 L0_attn_a_Wo, L0_attn_a_g_weight, L0_attn_b_Wq, L0_attn_b_Wk,
                 L0_attn_b_Wv, L0_attn_b_Wo, L0_ffn_router_w, L0_ffn_w_gate,
                 L0_ffn_w_up, L0_ffn_w_down, L0_ffn_sh_gate, L0_ffn_sh_up,
                 L0_ffn_sh_down, L0_ffn_sh_gate_w):
    x = np.asarray(L0_attn_a_x, dtype=np.float32)
    B, S, _ = x.shape
    perm = _perm_deinterleave()

    wg_all = np.empty((H, E * IM), np.float32)
    wu_all = np.empty((H, E * IM), np.float32)
    wd_all = np.empty((IM, E * H), np.float32)
    for e in range(E):
        wg_all[:, e * IM:(e + 1) * IM] = np.asarray(L0_ffn_w_gate[e], np.float32)
        wu_all[:, e * IM:(e + 1) * IM] = np.asarray(L0_ffn_w_up[e], np.float32)
        wd_all[:, e * H:(e + 1) * H] = np.asarray(L0_ffn_w_down[e], np.float32)
    com = {
        "wa_q": _colchunk(np.asarray(L0_attn_a_Wq, dtype=np.float32)),
        "wa_f": _colchunk(np.asarray(L0_attn_a_Wf, dtype=np.float32)),
        "wa_i": _colchunk(np.asarray(L0_attn_a_Wi, dtype=np.float32)),
        "wa_o": _colchunk(np.asarray(L0_attn_a_Wo, dtype=np.float32)),
        "ga_t": np.ascontiguousarray(
            np.asarray(L0_attn_a_g_weight, dtype=np.float32).reshape(FC, 128).T),
        "wb_q": _colchunk(np.ascontiguousarray(
            np.asarray(L0_attn_b_Wq, dtype=np.float32).T[:, perm])),
        "wb_k": _colchunk(np.ascontiguousarray(
            np.asarray(L0_attn_b_Wk, dtype=np.float32).T[:, perm])),
        "wb_v": np.ascontiguousarray(np.asarray(L0_attn_b_Wv, dtype=np.float32).T),
        "wb_o": _colchunk(np.ascontiguousarray(
            np.asarray(L0_attn_b_Wo, dtype=np.float32).T)),
        "f_rw": np.ascontiguousarray(
            np.asarray(L0_ffn_router_w, dtype=np.float32).T
            .reshape(16, 128, 16).transpose(1, 0, 2).reshape(128, 256)),
        "f_wg": _colchunk(wg_all),
        "f_wu": _colchunk(wu_all),
        "f_wd": _colchunk(wd_all),
        "f_sg": _colchunk(np.ascontiguousarray(L0_ffn_sh_gate, dtype=np.float32)),
        "f_su": _colchunk(np.ascontiguousarray(L0_ffn_sh_up, dtype=np.float32)),
        "f_sd": _colchunk(np.ascontiguousarray(L0_ffn_sh_down, dtype=np.float32)),
        "f_sgw": np.ascontiguousarray(np.asarray(L0_ffn_sh_gate_w, dtype=np.float32).reshape(FC, 128).T),
        "c_ident": np.eye(128, dtype=np.float32).astype(bfdt),
        "c_ones_col": np.ones((128, 1), dtype=np.float32).astype(bfdt),
        "c_ones_row": np.ones((1, 128), dtype=np.float32),
        "c_maskaa": (SC_MASK * np.triu(np.ones((128, 128), dtype=np.float32))).astype(bfdt),
    }
    emat = np.zeros((16, 16 * 128), dtype=np.float32)
    for e in range(16):
        emat[e, e * 128:(e + 1) * 128] = 1.0
    com["c_emat"] = emat.astype(bfdt)
    psw = np.zeros((128, 128), dtype=np.float32)
    psw[(np.arange(128) + 64) % 128, np.arange(128)] = 1.0
    com["c_pswap"] = psw.astype(bfdt)
    com["c_bias"] = np.tile(np.array([[LN_EPS, 1.0, LNC, 0.0]], np.float32),
                            (128, 1))
    com["c_bias2"] = np.tile(np.array([[LNC - SH, -SH, SH, 0.0]], np.float32),
                             (128, 1))

    in_maps = []
    for c in range(8):
        bb, odd = c // 2, c % 2
        seq = x[bb]
        if odd:
            chain = seq
        else:
            chain = np.concatenate([np.zeros((2048, H), np.float32), seq[0:2048]],
                                   axis=0)
        m = dict(com)
        m["x_t"] = _blocked(np.ascontiguousarray(chain.T))
        start = 2048 * odd
        pos = np.arange(MAIN0, NT) - (0 if odd else 2048)
        pos = np.maximum(pos, 0)
        rc, rs = _rope_tables(pos)
        m["ropec"] = np.concatenate([rc, rc], axis=0).astype(bfdt)
        m["ropes"] = np.concatenate([-rs, rs], axis=0).astype(bfdt)
        m["bmask"] = np.ascontiguousarray(
            _bmask(odd).reshape(16, 128, 512).transpose(1, 0, 2)
            .reshape(128, 16 * 512))
        in_maps.append(m)
    return in_maps, (B, S)


def kernel(**inputs):
    in_maps, (B, S) = make_in_maps(**inputs)
    nc = build_program()
    res = run_bass_kernel_spmd(nc, in_maps, core_ids=list(range(8)))
    out = np.empty((B, S, H), dtype=np.float32)
    for c in range(8):
        bb, odd = c // 2, c % 2
        y = res.results[c]["y_t"].reshape(128, 4, 16, 512)
        out[bb, odd * 2048:(odd + 1) * 2048, :] = (
            y.transpose(1, 3, 2, 0).reshape(2048, 2048))
    return out


# version 11 (final)
# speedup vs baseline: 1.1315x; 1.0407x over previous
import contextlib
import math

import numpy as np
import ml_dtypes

import concourse.bass as bass
import concourse.mybir as mybir
from concourse.bass_utils import run_bass_kernel_spmd

F32 = mybir.dt.float32
BF16 = mybir.dt.bfloat16
AF = mybir.ActivationFunctionType
ALU = mybir.AluOpType
AX = mybir.AxisListType
bfdt = ml_dtypes.bfloat16

H = 2048
FC = 16
NH = 16
DK = 128
CH = 128
NCH = 4
NT = 4096
MAIN0 = 1536
T_MAIN = 2560
OWN0 = 2048
OWN = 2048
E = 16
IM = 1408
JM = 11
ISH = 5632
JSH = 44
LN_EPS = 1e-5
AA_EPS = 1e-5
LNC = math.log(DK ** -0.5)
BQS = 1.0 / math.sqrt(DK)
BIGNEG = 1.0e30
SH = 52 * math.log(2.0)
SC_MASK = 2.0 ** 104

ROLES = ["SP", "POOL", "PE", "ACT", "DVE"]
SEMS = (["ld" + c for c in "ABCD"] + ["st" + c for c in "ABC"] +
        ["wl" + c for c in "ABCDEFGHIJ"] + ["mm", "ev", "dv", "bar"])


def _region(ap):
    h = ap.tensor if hasattr(ap, "tensor") else ap
    if not hasattr(ap, "ap"):
        shp = h.shape
        return (h.name, 0, 128, 0, int(np.prod(shp)))
    name = h.name
    shp = list(h.shape)
    off = ap.offset
    dims = [list(x) for x in ap.ap]
    memtype = getattr(h, "__class__", None).__name__
    if memtype in ("SBTensorHandle", "PSumTensorHandle") and len(shp) == 2:
        rowlen = shp[1]
        p0 = off // rowlen
        c0 = off % rowlen
        pspan = 0
        cspan = 0
        for (st, cnt) in dims:
            if cnt <= 1:
                continue
            if st != 0 and st % rowlen == 0:
                pspan += (st // rowlen) * (cnt - 1)
            else:
                cspan += abs(st) * (cnt - 1)
        c1 = c0 + cspan + 1
        if memtype == "PSumTensorHandle":
            bank = 512 if h.dtype == mybir.dt.float32 else 1024
            c0 = (c0 // bank) * bank
            c1 = ((c1 + bank - 1) // bank) * bank
            return (name, 0, 128, c0, c1)
        return (name, p0, p0 + pspan + 1, c0, c1)
    span = 0
    for (st, cnt) in dims:
        if cnt > 1:
            span += abs(st) * (cnt - 1)
    return (name, 0, 1 << 30, off, off + span + 1)


def _overlap(a, b):
    return (a[0] == b[0] and a[1] < b[2] and b[1] < a[2]
            and a[3] < b[4] and b[3] < a[4])


class Ctx:
    ENG_SEM = {"PE": "mm", "ACT": "ev", "DVE": "dv"}

    def __init__(self, nc, block, sems):
        self.nc = nc
        self.block = block
        self.sem = sems
        self.cnt = {k: 0 for k in SEMS}
        self.role = None
        self.eng = None
        self.hiw = {}
        self.twr = {}
        self.trd = {}
        self.pe_pending = []
        self.dma_owner = {}

    def t_clear(self):
        self.twr.clear()
        self.trd.clear()
        self.pe_pending = []

    def _dep_read(self, ap, deps):
        r = _region(ap)
        for w in self.twr.get(r[0], ()):
            if _overlap(r, (r[0], w[0], w[1], w[2], w[3])):
                if w[5] is None:
                    raise RuntimeError(
                        f"read of {r[0]} depends on unmarked PE write")
                deps.append((w[4], w[5]))

    def _note_read(self, ap, sem, cntv):
        r = _region(ap)
        ent = [r[1], r[2], r[3], r[4], sem, cntv]
        self.trd.setdefault(r[0], []).append(ent)
        return ent

    def _dep_write(self, ap, deps, is_pe):
        r = _region(ap)
        ws = self.twr.get(r[0], ())
        for w in ws:
            if _overlap(r, (r[0], w[0], w[1], w[2], w[3])):
                if is_pe and w[6]:
                    continue
                if w[5] is None:
                    raise RuntimeError(
                        f"write of {r[0]} overlaps unmarked PE write")
                deps.append((w[4], w[5]))
        for rd in self.trd.get(r[0], ()):
            if _overlap(r, (r[0], rd[0], rd[1], rd[2], rd[3])):
                if rd[5] is None:
                    raise RuntimeError(
                        f"write of {r[0]} overlaps PE read not yet covered "
                        f"by a PE mark")
                deps.append((rd[4], rd[5]))

    def _note_write(self, ap, sem, cntv, is_pe):
        r = _region(ap)
        lst = self.twr.setdefault(r[0], [])
        lst[:] = [w for w in lst
                  if not (r[1] <= w[0] and w[1] <= r[2]
                          and r[3] <= w[2] and w[3] <= r[4])]
        ent = [r[1], r[2], r[3], r[4], sem, cntv, is_pe]
        lst.append(ent)
        rl = self.trd.get(r[0])
        if rl:
            rl[:] = [x for x in rl
                     if not (r[1] <= x[0] and x[1] <= r[2]
                             and r[3] <= x[2] and x[3] <= r[4])]
        return ent

    def _emit_op(self, engine, reads, writes, waits, mark, emit_fn,
                 is_dma=False):
        own = self.ENG_SEM.get(engine)
        deps = list(waits)
        for ap in reads:
            if ap is not None and not isinstance(ap, (int, float)):
                self._dep_read(ap, deps)
        for ap in writes:
            self._dep_write(ap, deps, engine == "PE")
        if is_dma:
            sem = mark
            prev = self.dma_owner.setdefault(sem, engine)
            assert prev == engine, f"dma sem {sem} used from {prev} and {engine}"
            self.cnt[sem] += 16
            cntv = self.cnt[sem]
        elif engine == "PE":
            if mark:
                self.cnt["mm"] += 1
                cntv = self.cnt["mm"]
            else:
                cntv = None
            sem = "mm"
        else:
            sem = own
            self.cnt[sem] += 1
            cntv = self.cnt[sem]
        for ap in reads:
            if ap is not None and not isinstance(ap, (int, float)):
                e = self._note_read(ap, sem,
                                    None if (engine == "PE" and not is_dma)
                                    else cntv)
                if engine == "PE" and not is_dma:
                    self.pe_pending.append(e)
        ents = [self._note_write(ap, sem, cntv, engine == "PE" and not is_dma)
                for ap in writes]
        if engine == "PE" and not is_dma:
            if cntv is None:
                self.pe_pending.extend(ents)
            else:
                for e in self.pe_pending:
                    if e[5] is None:
                        e[5] = cntv
                self.pe_pending = []
                for e in ents:
                    e[5] = cntv
        if self.role == engine:
            self.w(*[(s, v) for (s, v) in deps if v])
            ins = emit_fn()
            if is_dma:
                ins.then_inc(self.sem[sem], 16)
            elif engine == "PE":
                if cntv is not None:
                    ins.then_inc(self.sem[sem], 1)
            else:
                ins.then_inc(self.sem[sem], 1)
        return cntv if cntv is not None else self.cnt.get(sem, 0)

    def w(self, *waits):
        for (s, v) in waits:
            if v <= 0:
                continue
            if self.hiw.get(s, -1) >= v:
                continue
            self.eng.wait_ge(self.sem[s], v)
            self.hiw[s] = v

    def now(self):
        return [("mm", self.cnt["mm"]), ("ev", self.cnt["ev"]),
                ("dv", self.cnt["dv"])]

    def dma(self, q, out, in_, sem, waits=()):
        return self._emit_op(q, [in_], [out], waits, sem,
                             lambda: self.eng.dma_start(out=out, in_=in_),
                             is_dma=True)

    def ldw(self, wt, waits=()):
        deps = list(waits)
        self._dep_read(wt, deps)
        self.pe_pending.append(self._note_read(wt, "mm", None))
        if self.role == "PE":
            self.w(*[(s, v) for (s, v) in deps if v])
            self.eng.ldweights(wt)

    def mm(self, out, lhsT, rhs, start, stop, mark=None, waits=()):
        return self._emit_op(
            "PE", [lhsT, rhs], [out], waits, mark,
            lambda: self.eng.matmul(out, lhsT, rhs, start=start, stop=stop))

    def tr(self, out, in_, ident, mark=None, waits=()):
        return self._emit_op(
            "PE", [in_, ident], [out], waits, mark,
            lambda: self.eng.transpose(out, in_, ident))

    def act(self, out, in_, func=AF.Copy, bias=None, scale=None, mark=None,
            waits=()):
        kw = {}
        if bias is not None:
            kw["bias"] = bias
        if scale is not None:
            kw["scale"] = scale
        return self._emit_op(
            "ACT", [in_, bias if hasattr(bias, "tensor") else None], [out],
            waits, mark, lambda: self.eng.activation(out, in_, func, **kw))

    def amul(self, out, in_, mul, mark=None, waits=()):
        return self._emit_op(
            "ACT", [in_, mul if hasattr(mul, "tensor") else None], [out],
            waits, mark, lambda: self.eng.mul(out, in_, mul))

    def aadd(self, out, in_, add, mark=None, waits=()):
        return self._emit_op(
            "ACT", [in_, add if hasattr(add, "tensor") else None], [out],
            waits, mark, lambda: self.eng.add(out, in_, add))

    def tt(self, out, a, b, op, mark=None, waits=()):
        return self._emit_op(
            "DVE", [a, b], [out], waits, mark,
            lambda: self.eng.tensor_tensor(out, a, b, op))

    def ts(self, out, in0, s1, op0, mark=None, waits=()):
        return self._emit_op(
            "DVE", [in0, s1 if hasattr(s1, "tensor") else None], [out],
            waits, mark,
            lambda: self.eng.tensor_scalar(out, in0, s1, None, op0))

    def red(self, out, in_, op, mark=None, waits=()):
        return self._emit_op(
            "DVE", [in_], [out], waits, mark,
            lambda: self.eng.tensor_reduce(out, in_, AX.X, op))

    def scan(self, out, d0, d1, init, op0, op1, mark=None, waits=()):
        return self._emit_op(
            "DVE", [d0, d1], [out], waits, mark,
            lambda: self.eng.tensor_tensor_scan(out, d0, d1, init, op0, op1))

    def recip(self, out, in_, mark=None, waits=()):
        return self._emit_op(
            "DVE", [in_], [out], waits, mark,
            lambda: self.eng.reciprocal(out, in_))

    def sel(self, out, mask, on_true, on_false, mark=None, waits=()):
        return self._emit_op(
            "DVE", [mask, on_true, on_false], [out], waits, mark,
            lambda: self.eng.select(out, mask, on_true, on_false, add_drain=True))

    def vmemset(self, ap, val, mark=None, waits=()):
        return self._emit_op(
            "DVE", [], [ap], waits, mark,
            lambda: self.eng.memset(ap, val))

    def c(self, sem):
        return self.cnt[sem]


def run_phase(ctx, fn, name=""):
    start = dict(ctx.cnt)
    final = None
    for role in ROLES:
        ctx.cnt = dict(start)
        ctx.hiw = {}
        ctx.t_clear()
        ctx.dma_owner = {}
        ctx.role = role

        def body(eng):
            ctx.eng = eng
            fn(ctx)
            own = {"PE": ["mm"], "ACT": ["ev"], "DVE": ["dv"],
                   "SP": [], "POOL": []}[ctx.role]
            own = own + [s for (s, r) in sorted(ctx.dma_owner.items())
                         if r == ctx.role]
            for s in own:
                if ctx.cnt[s]:
                    eng.wait_ge(ctx.sem[s], ctx.cnt[s])
            tgt = ctx.cnt["bar"] + 5
            eng.nop().then_inc(ctx.sem["bar"], 1)
            eng.wait_ge(ctx.sem["bar"], tgt)
            ctx.cnt["bar"] = tgt

        getattr(ctx.block, {"SP": "sync", "POOL": "gpsimd", "PE": "tensor",
                            "ACT": "scalar", "DVE": "vector"}[role])(body)
        if final is None:
            final = dict(ctx.cnt)
        else:
            assert final == ctx.cnt, f"role divergence in {name}"
    ctx.cnt = final


def hdl(t):
    return t.tensor if hasattr(t, "tensor") else t


def ap_cols_rep(sb, col0, ncols, rep):
    h = hdl(sb)
    return bass.AP(h, col0, [[h.shape[1], 128], [1, ncols], [0, rep]])


def ap_stride(sb, off, step, count):
    h = hdl(sb)
    return bass.AP(h, off, [[h.shape[1], 128], [step, count]])


def wload(ctx, q, dst_sb, dst_free0, src, Ka, j, sem, k0=0, k1=None):
    if k1 is None:
        k1 = Ka
    s0 = j * Ka * 128 + k0 * 128
    return ctx.dma(q, dst_sb[:, dst_free0 + 0:dst_free0 + (k1 - k0) * 128],
                   src[:, s0:s0 + (k1 - k0) * 128], sem)


def emit_ln(ctx, b, xsrc, c0, ncols):
    ps, rows = b.ps, b.rows
    s0, s1, c_inv, c_mean = getattr(b, "lnb", (0, 512, 1024, 1536))
    ldsem = getattr(b, "ldsem", "ldA")
    n = ncols
    blk = c0 // 512
    ctx.dma("SP", b.xblk[:, 0:FC * n], xsrc[:, blk * 8192:blk * 8192 + FC * n],
            ldsem)
    ctx.act(b.st16[:, :FC * n], b.xblk[:, :FC * n], AF.Square)
    for a in range(FC):
        ctx.mm(ps[0:1, s1:s1 + n], b.ones_col[:, :],
               b.st16[:, a * n:(a + 1) * n],
               start=(a == 0), stop=(a == FC - 1),
               mark=("mm" if a == FC - 1 else None))
    ctx.act(b.st16[:, :FC * n], b.xblk[:, :FC * n], AF.Copy)
    for a in range(FC):
        ctx.mm(ps[0:1, s0:s0 + n], b.ones_col[:, :], b.st16[:, a * n:(a + 1) * n],
               start=(a == 0), stop=(a == FC - 1),
               mark=("mm" if a == FC - 1 else None))
    inv_h = 1.0 / H
    ctx.amul(rows[0:1, 0:n], ps[0:1, s0:s0 + n], inv_h)
    ctx.amul(rows[0:1, n:2 * n], ps[0:1, s1:s1 + n], inv_h)
    ctx.tt(rows[0:1, 2 * n:3 * n], rows[0:1, 0:n], rows[0:1, 0:n], ALU.mult)
    ctx.tt(rows[0:1, n:2 * n], rows[0:1, n:2 * n], rows[0:1, 2 * n:3 * n],
           ALU.subtract)
    ctx.act(rows[0:1, 2 * n:3 * n], rows[0:1, n:2 * n], AF.Ln, bias=LN_EPS)
    ctx.act(rows[0:1, n:2 * n], rows[0:1, 2 * n:3 * n], AF.Exp, scale=-0.5)
    ctx.tt(rows[0:1, 2 * n:3 * n], rows[0:1, 0:n], rows[0:1, n:2 * n],
           ALU.mult)
    ctx.mm(ps[:, c_inv:c_inv + n], b.ones_row[:, :], rows[0:1, n:2 * n],
           start=True, stop=True, mark="mm")
    ctx.mm(ps[:, c_mean:c_mean + n], b.ones_row[:, :], rows[0:1, 2 * n:3 * n],
           start=True, stop=True, mark="mm")
    dvv = 0
    for a in range(FC):
        ctx.tt(b.tmp[:, 0:n], b.xblk[:, a * n:(a + 1) * n],
               ps[:, c_inv:c_inv + n], ALU.mult)
        dvv = ctx.tt(b.xln[:, a * n:(a + 1) * n], b.tmp[:, 0:n],
                     ps[:, c_mean:c_mean + n], ALU.subtract)
    return dvv


def alloc_aam(ctx, es, cst):
    nc = ctx.nc
    sb = nc.sbuf_tensor
    b = type("B", (), {})()
    b.__dict__.update(cst.__dict__)
    b.xlnb = [es.enter_context(sb(f"aam_xln{i}", [128, FC * 512], BF16))
              for i in range(2)]
    b.wqf = es.enter_context(sb("aam_wqf", [128, 4 * 3 * H], BF16))
    b.Hst = es.enter_context(sb("aam_H", [128, NH * 128], F32))
    b.fs = [es.enter_context(sb(f"aam_fs{i}", [128, 1024], F32))
            for i in range(2)]
    b.qs = [es.enter_context(sb(f"aam_qs{i}", [128, 1024], F32))
            for i in range(2)]
    b.vfm = [es.enter_context(sb(f"aam_vfm{i}", [128, 1024], BF16))
             for i in range(2)]
    for nm in ["tmp", "tm2", "F1", "ks", "gs", "As"]:
        setattr(b, nm, es.enter_context(sb("aam_" + nm, [128, 1024], F32)))
    for nm in ["qt", "kt", "kh", "khT", "sm", "vtok"]:
        setattr(b, nm, es.enter_context(sb("aam_" + nm, [128, 1024], BF16)))
    b.Hb = es.enter_context(sb("aam_Hb", [128, 256], BF16))
    b.cm = es.enter_context(sb("aam_cm", [128, 32], F32))
    b.osbp = [es.enter_context(sb(f"aam_osbp{i}", [128, 512], BF16))
              for i in range(2)]
    b.wc = [es.enter_context(sb(f"aam_wc{i}", [128, 2816], BF16))
            for i in range(2)]
    b.ps = es.enter_context(nc.psum_tensor("aam_ps", [128, 7 * 512], F32))
    b.psb = es.enter_context(nc.psum_tensor("aam_psb", [128, 1024], BF16))
    return b


def aam_A(ctx, b, hh, hp, is_main, bs):
    wslot = (hp % 2) * (6 * H)
    hoff = hh * 2048
    ho5 = hh * 512

    def proj(woff, dst_ev, func):
        pbk = b.pb()
        for i in range(FC):
            wt = b.wqf[:, wslot + woff + i * 128:
                       wslot + woff + (i + 1) * 128]
            ctx.ldw(wt)
            ctx.mm(b.ps[:, pbk * 512:(pbk + 1) * 512], wt,
                   b.xlnb[bs][:, i * 512:(i + 1) * 512],
                   start=(i == 0), stop=(i == FC - 1),
                   mark=("mm" if i == FC - 1 else None))
        ctx.act(dst_ev, b.ps[:, pbk * 512:(pbk + 1) * 512], func)

    if is_main:
        proj(hoff, b.qs[bs][:, ho5:ho5 + 512], AF.Silu)
    proj(4096 + hoff, b.fs[bs][:, ho5:ho5 + 512], AF.Copy)
    proj(8192 + hoff, b.vfm[bs][:, ho5:ho5 + 512], AF.Copy)


def aam_B(ctx, b, is_main, bs):
    fs, qs = b.fs[bs], b.qs[bs]
    ctx.act(b.tmp[:, :], fs[:, :], AF.Abs)
    ctx.act(b.tmp[:, :], b.tmp[:, :], AF.Exp, scale=-1.0)
    ctx.act(b.tm2[:, :], b.tmp[:, :], AF.Ln, bias=1.0)
    ctx.ts(b.gs[:, :], fs[:, :], 0.0, ALU.min)
    ctx.tt(b.gs[:, :], b.gs[:, :], b.tm2[:, :], ALU.subtract)
    for a in range(8):
        ctx.scan(b.As[:, a * CH:(a + 1) * CH],
                 b.gs[:, a * CH:(a + 1) * CH],
                 b.gs[:, a * CH:(a + 1) * CH], 0.0, ALU.add, ALU.bypass)
    for hh in range(2):
        co = hh * 16
        ACm = b.cm[:, co:co + 4]
        mcol = b.cm[:, co + 4:co + 8]
        ctx.act(ACm, ap_stride(b.As, hh * 512 + CH - 1, CH, NCH), AF.Copy)
        ctx.amul(mcol, ACm, 0.5)
        ctx.act(b.cm[:, co + 8:co + 12], ACm, AF.Exp)
        ctx.act(b.cm[:, co + 12:co + 16], mcol, AF.Exp, bias=SH)
        t3 = b.tmp[:, hh * 512:(hh + 1) * 512].rearrange(
            "p (c n) -> p c n", c=NCH)
        a3 = b.As[:, hh * 512:(hh + 1) * 512].rearrange(
            "p (c n) -> p c n", c=NCH)
        ctx.tt(t3, a3, ap_cols_rep(b.cm, co + 4, NCH, CH), ALU.subtract)
    if is_main:
        ctx.act(b.F1[:, :], b.tmp[:, :], AF.Exp, bias=LNC - SH)
        ctx.tt(b.qt[:, :], qs[:, :], b.F1[:, :], ALU.mult)
    ctx.act(b.F1[:, :], b.tmp[:, :], AF.Exp, scale=-1.0, bias=-SH)
    ctx.act(b.tm2[:, :], b.gs[:, :], AF.Exp)
    ctx.act(b.ks[:, :], b.tm2[:, :], AF.Copy, scale=-1.0, bias=1.0)
    ctx.tt(b.kt[:, :], b.ks[:, :], b.F1[:, :], ALU.mult)
    for hh in range(2):
        co = hh * 16
        k3 = b.kh[:, hh * 512:(hh + 1) * 512].rearrange(
            "p (c n) -> p c n", c=NCH)
        kt3 = b.kt[:, hh * 512:(hh + 1) * 512].rearrange(
            "p (c n) -> p c n", c=NCH)
        ctx.tt(k3, kt3, ap_cols_rep(b.cm, co + 12, NCH, CH), ALU.mult)


def aam_C(ctx, b, d, hp, blk, is_main, bs):
    for hh in range(2):
        for cc in range(NCH):
            sl = (hh * 4 + cc) * 128
            ctx.tr(b.psb[:, sl:sl + 128],
                   b.vfm[bs][:, sl:sl + 128], b.ident[:, :], mark="mm")
            ctx.act(b.vtok[:, sl:sl + 128], b.psb[:, sl:sl + 128], AF.Copy)
    for hh in range(2):
        for cc in range(NCH):
            sl = (hh * 4 + cc) * 128
            ctx.tr(b.psb[:, sl:sl + 128],
                   b.kh[:, sl:sl + 128], b.ident[:, :], mark="mm")
            ctx.act(b.khT[:, sl:sl + 128], b.psb[:, sl:sl + 128], AF.Copy)
    for cc in range(NCH):
        for hh in range(2):
            h = hp * 2 + hh
            o = hh * 512 + cc * 128
            co = hh * 16
            Hh = b.Hst[:, h * 128:(h + 1) * 128]
            vt = b.vtok[:, o:o + 128]
            ao = (2 + hh) * 512
            if is_main:
                ctx.amul(b.Hb[:, hh * 128:(hh + 1) * 128], Hh,
                         b.cm[:, co + 12 + cc:co + 13 + cc])
                ctx.ldw(b.Hb[:, hh * 128:(hh + 1) * 128])
                ctx.mm(b.ps[:, ao:ao + 128], b.Hb[:, hh * 128:(hh + 1) * 128],
                       b.qt[:, o:o + 128], start=True, stop=False)
                ctx.ldw(b.kt[:, o:o + 128])
                ctx.mm(b.ps[:, 4 * 512:4 * 512 + 128], b.kt[:, o:o + 128],
                       b.qt[:, o:o + 128], start=True, stop=True, mark="mm")
                ctx.tt(b.sm[:, o:o + 128], b.ps[:, 4 * 512:4 * 512 + 128],
                       b.maskaa[:, :], ALU.mult)
                ctx.ldw(vt)
                ctx.mm(b.ps[:, ao:ao + 128], vt, b.sm[:, o:o + 128],
                       start=False, stop=True, mark="mm")
                ctx.act(b.osbp[hh][:, cc * 128:(cc + 1) * 128],
                        b.ps[:, ao:ao + 128], AF.Copy)
            ctx.ldw(b.khT[:, o:o + 128])
            ctx.mm(b.ps[:, hh * 512:hh * 512 + 128], b.khT[:, o:o + 128],
                   vt, start=True, stop=True, mark="mm")
            ctx.ts(Hh, Hh, b.cm[:, co + 8 + cc:co + 9 + cc], ALU.mult)
            ctx.tt(Hh, Hh, b.ps[:, hh * 512:hh * 512 + 128], ALU.add)
    if is_main:
        for hh in range(2):
            h = hp * 2 + hh
            ctx.dma("SP", d.osb_t[:, (blk - 3) * 8192 + h * 512:
                                  (blk - 3) * 8192 + (h + 1) * 512],
                    b.osbp[hh][:, :], "stA" if hh == 0 else "stB")


def emit_aam(ctx, b, d):
    ctx.vmemset(b.Hst[:, :], 0.0)
    pbc = [0]

    def pb():
        pbc[0] ^= 1
        return 5 + pbc[0]
    b.pb = pb

    def load_w(hp):
        wslot = (hp % 2) * (6 * H)
        for hh in range(2):
            h = hp * 2 + hh
            wload(ctx, "POOL", b.wqf, wslot + hh * 2048, d.wa_q, FC, h,
                  "wlA" if hp % 2 == 0 else "wlB")
            wload(ctx, "POOL", b.wqf, wslot + 4096 + hh * 2048, d.wa_f, FC, h,
                  "wlC" if hp % 2 == 0 else "wlD")
            wload(ctx, "POOL", b.wqf, wslot + 8192 + hh * 2048, d.wa_i, FC, h,
                  "wlE" if hp % 2 == 0 else "wlF")

    tasks = [(d.f_wd, d.wd_bf, j * 1408, 1408) for j in range(E * FC)]
    tasks += [(d.f_sd, d.sd_bf, j * 5632 + hf * 2816, 2816)
              for j in range(FC) for hf in range(2)]
    tcur = [0]

    def cast_some(n):
        for _ in range(n):
            if tcur[0] >= len(tasks):
                return
            srcd, dstd, off, w = tasks[tcur[0]]
            sl = tcur[0] % 2
            ctx.dma("POOL", b.wc[sl][:, 0:w], srcd[:, off:off + w], "wlG")
            ctx.dma("SP", dstd[:, off:off + w], b.wc[sl][:, 0:w], "stC")
            tcur[0] += 1

    ctx.dma("SP", b.xlnb[0][:, :], d.xln_t[:, 0:8192], "ldA")
    load_w(0)
    slots = [(hp, blk) for hp in range(NH // 2) for blk in range(8)]
    prev = None
    for si, (hp, blk) in enumerate(slots):
        if blk == 4 and hp + 1 < NH // 2:
            load_w(hp + 1)
        is_main = blk >= 3
        bs = blk % 2
        for hh in range(2):
            aam_A(ctx, b, hh, hp, is_main, bs)
        if si + 1 < len(slots):
            nxt = slots[si + 1][1]
            ctx.dma("SP", b.xlnb[nxt % 2][:, :],
                    d.xln_t[:, nxt * 8192:(nxt + 1) * 8192],
                    "ldA" if nxt % 2 == 0 else "ldB")
        if prev is not None:
            aam_C(ctx, b, d, prev[0], prev[1], prev[1] >= 3, prev[1] % 2)
        aam_B(ctx, b, is_main, bs)
        cast_some(5)
        prev = (hp, blk)
    aam_C(ctx, b, d, prev[0], prev[1], True, prev[1] % 2)
    cast_some(len(tasks))


def alloc_awo(ctx, es, cst):
    nc = ctx.nc
    sb = nc.sbuf_tensor
    b = type("B", (), {})()
    b.__dict__.update(cst.__dict__)
    b.osb = [es.enter_context(sb(f"awo_osb{i}", [128, FC * 512], BF16))
             for i in range(2)]
    b.st16 = es.enter_context(sb("awo_st16", [128, FC * 512], BF16))
    b.wo = es.enter_context(sb("awo_wo", [128, 2 * H], BF16))
    b.xr = [es.enter_context(sb(f"awo_xr{i}", [128, 512], F32))
            for i in range(2)]
    b.ob = [es.enter_context(sb(f"awo_ob{i}", [128, 512], F32))
            for i in range(2)]
    b.rows = es.enter_context(sb("awo_rows", [1, 2 * 512], F32))
    b.ps = es.enter_context(nc.psum_tensor("awo_ps", [128, 7 * 512], F32))
    return b


def emit_awo(ctx, b, d):
    ctx.dma("SP", b.osb[0][:, :], d.osb_t[:, 0:8192], "ldA")
    for blk in range(5):
        par = blk % 2
        if blk + 1 < 5:
            ctx.dma("SP", b.osb[(blk + 1) % 2][:, :],
                    d.osb_t[:, (blk + 1) * 8192:(blk + 2) * 8192],
                    "ldA" if (blk + 1) % 2 == 0 else "ldB")
        osb = b.osb[par]
        ctx.act(b.st16[:, :], osb[:, :], AF.Square)
        for a in range(FC):
            ctx.mm(b.ps[0:1, 0:512], b.ones_col[:, :],
                   b.st16[:, a * 512:(a + 1) * 512],
                   start=(a == 0), stop=(a == FC - 1),
                   mark=("mm" if a == FC - 1 else None))
        ctx.amul(b.rows[0:1, 0:512], b.ps[0:1, 0:512], 1.0 / H)
        ctx.act(b.rows[0:1, 512:1024], b.rows[0:1, 0:512], AF.Ln, bias=AA_EPS)
        ctx.act(b.rows[0:1, 0:512], b.rows[0:1, 512:1024], AF.Exp, scale=-0.5)
        ctx.mm(b.ps[:, 4 * 512:5 * 512], b.ones_row[:, :],
               b.rows[0:1, 0:512], start=True, stop=True, mark="mm")
        o3 = osb[:, :].rearrange("p (a n) -> p a n", a=FC)
        ctx.tt(o3, o3, bc_mid(b.ps, 4 * 512, FC, 512), ALU.mult)
        ctx.tt(o3, o3, ap_cols_rep(b.gat, 0, FC, 512), ALU.mult)
        for j in range(FC):
            jp = j % 2
            wload(ctx, "POOL", b.wo, jp * H, d.wa_o, FC, j,
                  "wlG" if jp == 0 else "wlH")
            pbk = 5 + jp
            for i in range(FC):
                wt = b.wo[:, jp * H + i * 128: jp * H + (i + 1) * 128]
                ctx.ldw(wt)
                ctx.mm(b.ps[:, pbk * 512:(pbk + 1) * 512], wt,
                       osb[:, i * 512:(i + 1) * 512],
                       start=(i == 0), stop=(i == FC - 1),
                       mark=("mm" if i == FC - 1 else None))
            ctx.dma("ACT", b.xr[jp][:, :],
                    d.x_t[:, (blk + 3) * 8192 + j * 512:
                          (blk + 3) * 8192 + (j + 1) * 512],
                    "ldC" if jp == 0 else "ldD")
            ctx.tt(b.ob[jp][:, :], b.xr[jp][:, :],
                   b.ps[:, pbk * 512:(pbk + 1) * 512], ALU.add)
            ctx.dma("SP", d.x1_t[:, blk * 8192 + j * 512:
                                 blk * 8192 + (j + 1) * 512],
                    b.ob[jp][:, :], "stA" if jp == 0 else "stB")


def ap3_m(cm):
    h = hdl(cm)
    return bass.AP(h, NCH, [[h.shape[1], 128], [1, NCH], [0, CH]])


def bc_mid(t, off, rep, n):
    h = hdl(t)
    return bass.AP(h, off, [[h.shape[1], 128], [0, rep], [1, n]])


def alloc_ffn_ln(ctx, es, cst, tag):
    nc = ctx.nc
    sb = nc.sbuf_tensor
    com = type("C", (), {})()
    com.ps = es.enter_context(nc.psum_tensor(tag + "_ps", [128, 7 * 512], F32))
    com.psb = es.enter_context(nc.psum_tensor(tag + "_psb", [128, 1024], BF16))
    com.tmpall = es.enter_context(sb(tag + "_tmpa", [128, FC * 512], F32))
    sets = []
    for par in range(2):
        b = type("B", (), {})()
        b.__dict__.update(cst.__dict__)
        t = tag + str(par)
        b.xblk = es.enter_context(sb(t + "_xblk", [128, FC * 512], F32))
        b.st16 = es.enter_context(sb(t + "_st16", [128, FC * 512], BF16))
        b.xln = es.enter_context(sb(t + "_xln", [128, FC * 512], BF16))
        b.rows = es.enter_context(sb(t + "_rows", [1, 3 * 512], F32))
        b.lgb = es.enter_context(sb(t + "_lgb", [16, 512], BF16))
        b.et64 = es.enter_context(sb(t + "_et64", [128, 64], F32))
        b.wk64 = es.enter_context(sb(t + "_wk64", [128, 64], F32))
        b.eq64 = es.enter_context(sb(t + "_eq64", [128, 64], F32))
        b.mxa = es.enter_context(sb(t + "_mxa", [128, 24], F32))
        b.cmb64 = es.enter_context(sb(t + "_cmb64", [128, 64], BF16))
        b.ps = com.ps
        b.psb = com.psb
        b.tmpall = com.tmpall
        b.lnb = (0, 512, 4 * 512, 5 * 512) if par == 0 else \
                (1024, 1536, 4 * 512, 5 * 512)
        b.ldsem = "ldA" if par == 0 else "ldB"
        b.stsem = "wlH" if par == 0 else "wlI"
        b.rtb = 6 * 512
        sets.append(b)
    return sets


def ln_s1(ctx, B, src, blk):
    s0, s1 = B.lnb[0], B.lnb[1]
    n = 512
    ctx.dma("SP", B.xblk[:, 0:FC * n],
            src[:, blk * 8192:blk * 8192 + FC * n], B.ldsem)
    ctx.act(B.st16[:, :FC * n], B.xblk[:, :FC * n], AF.Square)
    for a in range(FC):
        ctx.mm(B.ps[0:1, s1:s1 + n], B.ones_col[:, :],
               B.st16[:, a * n:(a + 1) * n],
               start=(a == 0), stop=(a == FC - 1),
               mark=("mm" if a == FC - 1 else None))
    ctx.act(B.st16[:, :FC * n], B.xblk[:, :FC * n], AF.Copy)
    for a in range(FC):
        ctx.mm(B.ps[0:1, s0:s0 + n], B.ones_col[:, :],
               B.st16[:, a * n:(a + 1) * n],
               start=(a == 0), stop=(a == FC - 1),
               mark=("mm" if a == FC - 1 else None))


def ln_s2(ctx, B):
    s0, s1, c_inv, c_mean = B.lnb
    rows = B.rows
    n = 512
    ctx.amul(rows[0:1, 0:n], B.ps[0:1, s0:s0 + n], 1.0 / H)
    ctx.amul(rows[0:1, n:2 * n], B.ps[0:1, s1:s1 + n], 1.0 / H)
    ctx.tt(rows[0:1, 2 * n:3 * n], rows[0:1, 0:n], rows[0:1, 0:n], ALU.mult)
    ctx.tt(rows[0:1, n:2 * n], rows[0:1, n:2 * n], rows[0:1, 2 * n:3 * n],
           ALU.subtract)
    ctx.act(rows[0:1, 2 * n:3 * n], rows[0:1, n:2 * n], AF.Ln, bias=LN_EPS)
    ctx.act(rows[0:1, n:2 * n], rows[0:1, 2 * n:3 * n], AF.Exp, scale=-0.5)
    ctx.tt(rows[0:1, 2 * n:3 * n], rows[0:1, 0:n], rows[0:1, n:2 * n],
           ALU.mult)
    ctx.mm(B.ps[:, c_inv:c_inv + n], B.ones_row[:, :], rows[0:1, n:2 * n],
           start=True, stop=True, mark="mm")
    ctx.mm(B.ps[:, c_mean:c_mean + n], B.ones_row[:, :],
           rows[0:1, 2 * n:3 * n], start=True, stop=True, mark="mm")


def ln_s3(ctx, B):
    c_inv, c_mean = B.lnb[2], B.lnb[3]
    x3 = B.xblk[:, :].rearrange("p (a n) -> p a n", a=FC)
    t3 = B.tmpall[:, :].rearrange("p (a n) -> p a n", a=FC)
    o3 = B.xln[:, :].rearrange("p (a n) -> p a n", a=FC)
    ctx.tt(t3, x3, bc_mid(B.ps, c_inv, FC, 512), ALU.mult)
    ctx.tt(o3, t3, bc_mid(B.ps, c_mean, FC, 512), ALU.subtract)


def ln_s4(ctx, B, dst, blk, router):
    ctx.dma("POOL", dst[:, blk * 8192:(blk + 1) * 8192],
            B.xln[:, :FC * 512], B.stsem)
    if not router:
        return
    for i in range(FC):
        ctx.mm(B.ps[0:16, B.rtb:B.rtb + 512], B.rw[:, i * 16:(i + 1) * 16],
               B.xln[:, i * 512:(i + 1) * 512],
               start=(i == 0), stop=(i == FC - 1),
               mark=("mm" if i == FC - 1 else None))
    ctx.act(B.lgb[:, :], B.ps[0:16, B.rtb:B.rtb + 512], AF.Copy)


def ln_s5(ctx, B, c0):
    for ts4 in range(4):
        ctx.tr(B.psb[0:128, ts4 * 16:(ts4 + 1) * 16],
               B.lgb[:, ts4 * 128:(ts4 + 1) * 128],
               B.ident[0:16, 0:16], mark="mm")
    ctx.act(B.et64[:, :], B.psb[0:128, 0:64], AF.Exp)
    ctx.act(B.wk64[:, :], B.et64[:, :], AF.Copy)
    w3 = B.wk64[:, :].rearrange("p (a n) -> p a n", a=4)
    e3 = B.eq64[:, :].rearrange("p (a n) -> p a n", a=4)
    for it in range(4):
        ctx.red(B.mxa[:, it * 4:(it + 1) * 4], w3, ALU.max)
        if it < 3:
            ctx.tt(e3, w3, ap_cols_rep(B.mxa, it * 4, 4, 16), ALU.is_equal)
            ctx.ts(B.eq64[:, :], B.eq64[:, :], -BIGNEG, ALU.mult)
            ctx.tt(B.wk64[:, :], B.wk64[:, :], B.eq64[:, :], ALU.add)
    t3 = B.et64[:, :].rearrange("p (a n) -> p a n", a=4)
    ctx.tt(e3, t3, ap_cols_rep(B.mxa, 12, 4, 16), ALU.is_ge)
    ctx.tt(B.eq64[:, :], B.et64[:, :], B.eq64[:, :], ALU.mult)
    ctx.red(B.mxa[:, 16:20], e3, ALU.add)
    ctx.ts(B.mxa[:, 16:20], B.mxa[:, 16:20], 1e-9, ALU.max)
    ctx.recip(B.mxa[:, 20:24], B.mxa[:, 16:20])
    c3 = B.cmb64[:, :].rearrange("p (a n) -> p a n", a=4)
    ctx.tt(c3, e3, ap_cols_rep(B.mxa, 20, 4, 16), ALU.mult)
    for ts4 in range(4):
        ctx.tr(B.psb[0:16, 512 + ts4 * 128:512 + (ts4 + 1) * 128],
               B.cmb64[:, ts4 * 16:(ts4 + 1) * 16],
               B.ident[:, :], mark="mm")
    ctx.act(B.comb_t[0:16, c0:c0 + 512], B.psb[0:16, 512:1024], AF.Copy)


def emit_ffn_ln(ctx, sets, d, src, n_tok, router=True, dst=None):
    if dst is None:
        dst = d.xln_t
    nblk = n_tok // 512
    ln_s1(ctx, sets[0], src, 0)
    if nblk > 1:
        ln_s1(ctx, sets[1], src, 1)
    for blk in range(nblk):
        B = sets[blk % 2]
        ln_s2(ctx, B)
        ln_s3(ctx, B)
        if blk + 2 < nblk:
            ln_s1(ctx, sets[blk % 2], src, blk + 2)
        ln_s4(ctx, B, dst, blk, router)
        if router:
            ln_s5(ctx, B, blk * 512)


def alloc_ffn_moe(ctx, es, cst, tag):
    nc = ctx.nc
    b = type("B", (), {})()
    b.__dict__.update(cst.__dict__)
    sb = nc.sbuf_tensor
    b.xg = es.enter_context(sb(tag + "_xg", [128, FC * 1024], BF16))
    b.acc = es.enter_context(sb(tag + "_acc", [128, FC * 1024], F32))
    b.big = es.enter_context(sb(tag + "_big", [128, JSH * 512], BF16))
    b.ws = es.enter_context(sb(tag + "_ws", [128, 4 * 2048], BF16))
    b.wsd = es.enter_context(sb(tag + "_wsd", [128, 2 * 22 * 128], BF16))
    b.cbc = es.enter_context(sb(tag + "_cbc", [128, 1024], BF16))
    b.sg1 = es.enter_context(sb(tag + "_sg1", [128, 4 * 512], BF16))
    b.sgb = es.enter_context(sb(tag + "_sgb", [128, 512], BF16))
    b.sgw = es.enter_context(sb(tag + "_sgw", [128, 16], BF16))
    b.tmp = es.enter_context(sb(tag + "_tmpm", [128, 1024], F32))
    b.ob = es.enter_context(sb(tag + "_ob", [128, 1024], F32))
    b.rows = es.enter_context(sb(tag + "_rowm", [1, 3 * 512], F32))
    b.ps = es.enter_context(nc.psum_tensor(tag + "_psm", [128, 8 * 512], F32))
    return b


def emit_ffn_moe(ctx, b, d, src, dst, n_tok):
    groups = []
    t0 = 0
    while t0 < n_tok:
        tg = min(1024, n_tok - t0)
        groups.append((t0, tg))
        t0 += tg
    ctx.dma("POOL", b.sgw[:, :], d.f_sgw[:, :], "wlG")

    def load_xg(gi):
        if gi >= len(groups):
            return
        g0n, tgn = groups[gi]
        for t2 in range(tgn // 512):
            ctx.dma("SP", b.xg[:, t2 * 8192:(t2 + 1) * 8192],
                    d.xln_t[:, (g0n // 512 + t2) * 8192:
                            (g0n // 512 + t2 + 1) * 8192],
                    "ldB" if t2 == 0 else "ldA")

    load_xg(0)
    for gi, (g0, tg) in enumerate(groups):
        nb = tg // 512
        for e in range(E):
            ctx.ldw(b.emat[:, e * 128:(e + 1) * 128])
            for t2 in range(nb):
                ctx.mm(b.ps[:, t2 * 512:(t2 + 1) * 512],
                       b.emat[:, e * 128:(e + 1) * 128],
                       b.comb_t[0:16, g0 + t2 * 512:g0 + (t2 + 1) * 512],
                       start=True, stop=True,
                       mark=("mm" if t2 == nb - 1 else None))
            ctx.act(b.cbc[:, 0:tg], b.ps[:, 0:tg], AF.Copy)
            for jj in range(JM):
                p = jj % 2
                wload(ctx, "POOL", b.ws, p * 2048, d.f_wg, FC, e * JM + jj,
                      "wlA" if p == 0 else "wlB")
                for i in range(FC):
                    ctx.ldw(b.ws[:, p * 2048 + i * 128: p * 2048 + (i + 1) * 128])
                    for t2 in range(nb):
                        ctx.mm(b.ps[:, (p * 2 + t2) * 512:(p * 2 + t2 + 1) * 512],
                               b.ws[:, p * 2048 + i * 128: p * 2048 + (i + 1) * 128],
                               b.xg[:, t2 * 8192 + i * 512: t2 * 8192 + (i + 1) * 512],
                               start=(i == 0), stop=(i == FC - 1),
                               mark=("mm" if (i == FC - 1 and t2 == nb - 1) else None))
                for t2 in range(nb):
                    ctx.act(b.sg1[:, (p * 2 + t2) * 512:(p * 2 + t2 + 1) * 512],
                            b.ps[:, (p * 2 + t2) * 512:(p * 2 + t2 + 1) * 512],
                            AF.Silu)
                wload(ctx, "POOL", b.ws, (2 + p) * 2048, d.f_wu, FC,
                      e * JM + jj, "wlC" if p == 0 else "wlD")
                for i in range(FC):
                    ctx.ldw(b.ws[:, (2 + p) * 2048 + i * 128:
                                 (2 + p) * 2048 + (i + 1) * 128])
                    for t2 in range(nb):
                        ctx.mm(b.ps[:, (4 + p * 2 + t2) * 512:
                                    (4 + p * 2 + t2 + 1) * 512],
                               b.ws[:, (2 + p) * 2048 + i * 128:
                                    (2 + p) * 2048 + (i + 1) * 128],
                               b.xg[:, t2 * 8192 + i * 512: t2 * 8192 + (i + 1) * 512],
                               start=(i == 0), stop=(i == FC - 1),
                               mark=("mm" if (i == FC - 1 and t2 == nb - 1) else None))
                for t2 in range(nb):
                    ctx.tt(b.tmp[:, t2 * 512:(t2 + 1) * 512],
                           b.sg1[:, (p * 2 + t2) * 512:(p * 2 + t2 + 1) * 512],
                           b.ps[:, (4 + p * 2 + t2) * 512:
                                (4 + p * 2 + t2 + 1) * 512], ALU.mult)
                    ctx.tt(b.big[:, t2 * 5632 + jj * 512: t2 * 5632 + (jj + 1) * 512],
                           b.tmp[:, t2 * 512:(t2 + 1) * 512],
                           b.cbc[:, t2 * 512:(t2 + 1) * 512], ALU.mult)
            for j2 in range(FC):
                p = j2 % 2
                wload(ctx, "SP", b.ws, p * 2048, d.wd_bf, JM, e * FC + j2,
                      "wlE" if p == 0 else "wlF")
                for k in range(JM):
                    ctx.ldw(b.ws[:, p * 2048 + k * 128: p * 2048 + (k + 1) * 128])
                    for t2 in range(nb):
                        ctx.mm(b.ps[:, (p * 2 + t2) * 512:(p * 2 + t2 + 1) * 512],
                               b.ws[:, p * 2048 + k * 128: p * 2048 + (k + 1) * 128],
                               b.big[:, t2 * 5632 + k * 512: t2 * 5632 + (k + 1) * 512],
                               start=(k == 0), stop=(k == JM - 1),
                               mark=("mm" if (k == JM - 1 and t2 == nb - 1) else None))
                for t2 in range(nb):
                    if e == 0:
                        ctx.act(b.acc[:, t2 * 8192 + j2 * 512: t2 * 8192 + (j2 + 1) * 512],
                                b.ps[:, (p * 2 + t2) * 512:(p * 2 + t2 + 1) * 512],
                                AF.Copy)
                    else:
                        ctx.tt(b.acc[:, t2 * 8192 + j2 * 512: t2 * 8192 + (j2 + 1) * 512],
                               b.acc[:, t2 * 8192 + j2 * 512: t2 * 8192 + (j2 + 1) * 512],
                               b.ps[:, (p * 2 + t2) * 512:(p * 2 + t2 + 1) * 512],
                               ALU.add)
        for t2 in range(nb):
            for i in range(FC):
                ctx.mm(b.ps[0:1, 6 * 512:6 * 512 + 512], b.sgw[:, i:i + 1],
                       b.xg[:, t2 * 8192 + i * 512: t2 * 8192 + (i + 1) * 512],
                       start=(i == 0), stop=(i == FC - 1),
                       mark=("mm" if i == FC - 1 else None))
            ctx.act(b.rows[0:1, 0:512], b.ps[0:1, 6 * 512:6 * 512 + 512],
                    AF.Exp, scale=-1.0)
            ctx.ts(b.rows[0:1, 512:1024], b.rows[0:1, 0:512], 1.0, ALU.add)
            ctx.recip(b.rows[0:1, 0:512], b.rows[0:1, 512:1024])
            ctx.mm(b.ps[:, 6 * 512:6 * 512 + 512], b.ones_row[:, :],
                   b.rows[0:1, 0:512], start=True, stop=True, mark="mm")
            ctx.act(b.sgb[:, :], b.ps[:, 6 * 512:6 * 512 + 512], AF.Copy)
            for jj in range(JSH):
                p = jj % 2
                wload(ctx, "POOL", b.ws, p * 2048, d.f_sg, FC, jj,
                      "wlA" if p == 0 else "wlB")
                for i in range(FC):
                    ctx.ldw(b.ws[:, p * 2048 + i * 128: p * 2048 + (i + 1) * 128])
                    ctx.mm(b.ps[:, p * 512:(p + 1) * 512],
                           b.ws[:, p * 2048 + i * 128: p * 2048 + (i + 1) * 128],
                           b.xg[:, t2 * 8192 + i * 512: t2 * 8192 + (i + 1) * 512],
                           start=(i == 0), stop=(i == FC - 1),
                           mark=("mm" if i == FC - 1 else None))
                ctx.act(b.sg1[:, p * 512:(p + 1) * 512],
                        b.ps[:, p * 512:(p + 1) * 512], AF.Silu)
                wload(ctx, "POOL", b.ws, (2 + p) * 2048, d.f_su, FC, jj,
                      "wlC" if p == 0 else "wlD")
                for i in range(FC):
                    ctx.ldw(b.ws[:, (2 + p) * 2048 + i * 128:
                                 (2 + p) * 2048 + (i + 1) * 128])
                    ctx.mm(b.ps[:, (2 + p) * 512:(3 + p) * 512],
                           b.ws[:, (2 + p) * 2048 + i * 128:
                                (2 + p) * 2048 + (i + 1) * 128],
                           b.xg[:, t2 * 8192 + i * 512: t2 * 8192 + (i + 1) * 512],
                           start=(i == 0), stop=(i == FC - 1),
                           mark=("mm" if i == FC - 1 else None))
                ctx.tt(b.big[:, jj * 512:(jj + 1) * 512],
                       b.sg1[:, p * 512:(p + 1) * 512],
                       b.ps[:, (2 + p) * 512:(3 + p) * 512], ALU.mult)
            if t2 == nb - 1:
                load_xg(gi + 1)
            for j2 in range(FC):
                p = j2 % 2
                for hf in range(2):
                    k0, k1 = (0, 22) if hf == 0 else (22, JSH)
                    ph = (j2 * 2 + hf) % 2
                    wload(ctx, "SP", b.wsd, ph * 22 * 128, d.sd_bf, JSH, j2,
                          "wlE" if ph == 0 else "wlF", k0=k0, k1=k1)
                    for k in range(k0, k1):
                        ctx.ldw(b.wsd[:, ph * 22 * 128 + (k - k0) * 128:
                                      ph * 22 * 128 + (k - k0 + 1) * 128])
                        ctx.mm(b.ps[:, (4 + p) * 512:(5 + p) * 512],
                               b.wsd[:, ph * 22 * 128 + (k - k0) * 128:
                                     ph * 22 * 128 + (k - k0 + 1) * 128],
                               b.big[:, k * 512:(k + 1) * 512],
                               start=(k == 0), stop=(k == JSH - 1),
                               mark=("mm" if k == JSH - 1 else None))
                ctx.tt(b.tmp[:, 0:512], b.ps[:, (4 + p) * 512:(5 + p) * 512],
                       b.sgb[:, :], ALU.mult)
                ctx.tt(b.acc[:, t2 * 8192 + j2 * 512: t2 * 8192 + (j2 + 1) * 512],
                       b.acc[:, t2 * 8192 + j2 * 512: t2 * 8192 + (j2 + 1) * 512],
                       b.tmp[:, 0:512], ALU.add)
        for j2 in range(FC):
            buf = b.ob if j2 % 2 == 0 else b.tmp
            for t2 in range(tg // 512):
                ctx.dma("ACT", buf[:, t2 * 512:(t2 + 1) * 512],
                        src[:, (g0 // 512 + t2) * 8192 + j2 * 512:
                            (g0 // 512 + t2) * 8192 + (j2 + 1) * 512],
                        "ldC" if j2 % 2 == 0 else "ldD")
                ctx.tt(buf[:, t2 * 512:(t2 + 1) * 512],
                       buf[:, t2 * 512:(t2 + 1) * 512],
                       b.acc[:, t2 * 8192 + j2 * 512:
                             t2 * 8192 + (j2 + 1) * 512], ALU.add)
                ctx.dma("SP", dst[:, (g0 // 512 + t2) * 8192 + j2 * 512:
                                  (g0 // 512 + t2) * 8192 + (j2 + 1) * 512],
                        buf[:, t2 * 512:(t2 + 1) * 512],
                        "stA" if j2 % 2 == 0 else "stB")


def alloc_ab(ctx, es, cst):
    nc = ctx.nc
    b = type("B", (), {})()
    b.__dict__.update(cst.__dict__)
    sb = nc.sbuf_tensor
    b.xlnb = [es.enter_context(sb(f"ab_xln{i}", [128, FC * 512], BF16))
              for i in range(2)]
    b.q = es.enter_context(sb("ab_q", [128, FC * 512], BF16))
    b.kring = es.enter_context(sb("ab_kring", [128, FC * 1024], BF16))
    b.vring = es.enter_context(sb("ab_vring", [128, 8 * H], BF16))
    b.osb = es.enter_context(sb("ab_osb", [128, NH * 512], BF16))
    b.ropec = es.enter_context(sb("ab_ropec", [128, T_MAIN], BF16))
    b.ropes = es.enter_context(sb("ab_ropes", [128, T_MAIN], BF16))
    b.pswap = es.enter_context(sb("ab_pswap", [128, 128], BF16))
    b.kdr = es.enter_context(sb("ab_kdr", [128, 512], BF16))
    b.bmask = es.enter_context(sb("ab_bmask", [128, 8 * 512], BF16))
    b.pt = es.enter_context(sb("ab_pt", [128, 8 * 512], BF16))
    b.rb = es.enter_context(sb("ab_rb", [128, 512], F32))
    b.t1 = es.enter_context(sb("ab_t1", [128, 512], F32))
    b.tmp = es.enter_context(sb("ab_tmp", [128, 512], F32))
    b.rows = es.enter_context(sb("ab_rows", [1, 3 * 512], F32))
    b.wsl = es.enter_context(sb("ab_wsl", [128, 4 * H], BF16))
    b.xr = [es.enter_context(sb(f"ab_xr{i}", [128, 512], F32))
            for i in range(2)]
    b.ob = [es.enter_context(sb(f"ab_ob{i}", [128, 512], F32))
            for i in range(2)]
    b.ps = es.enter_context(nc.psum_tensor("ab_ps", [128, 8 * 512], F32))
    return b


def emit_ab(ctx, b, d):
    ctx.dma("SP", b.ropec[:, :], d.ropec[:, :], "stC")
    ctx.dma("SP", b.ropes[:, :], d.ropes[:, :], "stC")
    ctx.dma("POOL", b.pswap[:, :], d.c_pswap[:, :], "wlI")
    ctx.dma("POOL", b.bmask[:, :], d.bmask[:, 8 * 512:16 * 512], "wlJ")
    ctx.dma("SP", b.xlnb[0][:, :], d.xln_t[:, 0:8192], "ldA")

    def rope_apply(dst, psrc):
        ctx.act(b.kdr[:, :], psrc, AF.Copy)
        ctx.ldw(b.pswap[:, :])
        ctx.mm(b.ps[:, 7 * 512:8 * 512], b.pswap[:, :],
               b.kdr[:, :], start=True, stop=True, mark="mm")
        ctx.tt(b.t1[:, :], b.kdr[:, :], b.ropec[:, rc0:rc0 + 512], ALU.mult)
        ctx.tt(b.tmp[:, :], b.ps[:, 7 * 512:8 * 512],
               b.ropes[:, rc0:rc0 + 512], ALU.mult)
        ctx.tt(dst, b.t1[:, :], b.tmp[:, :], ALU.add)

    for pb in range(5):
        c0 = pb * 512
        rc0 = c0
        qb = pb - 1
        xln = b.xlnb[pb % 2]
        rs = pb % 2
        for hh in range(NH):
            wload(ctx, "POOL", b.wsl, (hh % 2) * H, d.wb_k, FC, hh,
                  "wlA" if hh % 2 == 0 else "wlB")
            for i in range(FC):
                ctx.ldw(b.wsl[:, (hh % 2) * H + i * 128:(hh % 2) * H + (i + 1) * 128])
                ctx.mm(b.ps[:, 6 * 512:7 * 512],
                       b.wsl[:, (hh % 2) * H + i * 128:(hh % 2) * H + (i + 1) * 128],
                       xln[:, i * 512:(i + 1) * 512],
                       start=(i == 0), stop=(i == FC - 1),
                       mark=("mm" if i == FC - 1 else None))
            kd = b.kring[:, hh * 1024 + rs * 512: hh * 1024 + (rs + 1) * 512]
            rope_apply(kd, b.ps[:, 6 * 512:7 * 512])
        if pb + 1 < 5:
            ctx.dma("SP", b.xlnb[(pb + 1) % 2][:, :],
                    d.xln_t[:, (pb + 1) * 8192:(pb + 2) * 8192],
                    "ldA" if (pb + 1) % 2 == 0 else "ldB")
        for half in range(2):
            for i in range(FC):
                ctx.dma("POOL", b.wi_ab[:, (i % 2) * H:(i % 2) * H + 1024],
                        d.wb_v[i * 128:(i + 1) * 128,
                               half * 1024:(half + 1) * 1024],
                        "wlE" if i % 2 == 0 else "wlF")
                for ts4 in range(4):
                    xtile = xln[:, i * 512 + ts4 * 128:
                                i * 512 + (ts4 + 1) * 128]
                    ctx.ldw(xtile)
                    for g2 in range(2):
                        bank = ts4 * 2 + g2
                        ctx.mm(b.ps[:, bank * 512:(bank + 1) * 512], xtile,
                               b.wi_ab[:, (i % 2) * H + g2 * 512:
                                       (i % 2) * H + (g2 + 1) * 512],
                               start=(i == 0), stop=(i == FC - 1),
                               mark=("mm" if (ts4 == 3 and g2 == 1) else None))
            for ts4 in range(4):
                for g2 in range(2):
                    bank = ts4 * 2 + g2
                    ng = half * 2 + g2
                    ctx.act(b.vring[:, (rs * 4 + ts4) * H + ng * 512:
                                    (rs * 4 + ts4) * H + (ng + 1) * 512],
                            b.ps[:, bank * 512:(bank + 1) * 512], AF.Copy)
        if pb == 0:
            continue
        for hh in range(NH):
            wload(ctx, "POOL", b.wsl, (2 + hh % 2) * H, d.wb_q, FC, hh,
                  "wlC" if hh % 2 == 0 else "wlD")
            for i in range(FC):
                ctx.ldw(b.wsl[:, (2 + hh % 2) * H + i * 128:
                              (2 + hh % 2) * H + (i + 1) * 128])
                ctx.mm(b.ps[:, 6 * 512:7 * 512],
                       b.wsl[:, (2 + hh % 2) * H + i * 128:
                             (2 + hh % 2) * H + (i + 1) * 128],
                       xln[:, i * 512:(i + 1) * 512],
                       start=(i == 0), stop=(i == FC - 1),
                       mark=("mm" if i == FC - 1 else None))
            qd = b.q[:, hh * 512:(hh + 1) * 512]
            rope_apply(qd, b.ps[:, 6 * 512:7 * 512])
        for hh in range(NH):
            for p2 in range(2):
                for q4 in range(4):
                    cc = p2 * 4 + q4
                    pbk = qb + (1 if cc >= 4 else 0)
                    slot = pbk % 2
                    ts = cc % 4
                    ktile = b.kring[:, hh * 1024 + slot * 512 + ts * 128:
                                    hh * 1024 + slot * 512 + (ts + 1) * 128]
                    ctx.ldw(ktile)
                    ctx.mm(b.ps[:, q4 * 512:(q4 + 1) * 512],
                           ktile, b.q[:, hh * 512:(hh + 1) * 512],
                           start=True, stop=True,
                           mark=("mm" if q4 == 3 else None))
                ctx.act(b.pt[:, p2 * 2048:(p2 + 1) * 2048],
                        b.ps[:, 0:2048], AF.Exp, scale=BQS)
                ctx.tt(b.pt[:, p2 * 2048:(p2 + 1) * 2048],
                       b.pt[:, p2 * 2048:(p2 + 1) * 2048],
                       b.bmask[:, p2 * 2048:(p2 + 1) * 2048], ALU.mult)
            for cc in range(8):
                pbk = qb + (1 if cc >= 4 else 0)
                slot = pbk % 2
                ts = cc % 4
                ctx.mm(b.ps[0:1, 4 * 512:4 * 512 + 512], b.ones_col[:, :],
                       b.pt[:, cc * 512:(cc + 1) * 512],
                       start=(cc == 0), stop=(cc == 7),
                       mark=("mm" if cc == 7 else None))
                vtile = b.vring[:, (slot * 4 + ts) * H + hh * 128:
                                (slot * 4 + ts) * H + (hh + 1) * 128]
                ctx.ldw(vtile)
                ctx.mm(b.ps[:, 5 * 512:6 * 512], vtile,
                       b.pt[:, cc * 512:(cc + 1) * 512],
                       start=(cc == 0), stop=(cc == 7),
                       mark=("mm" if cc == 7 else None))
            ctx.recip(b.rows[0:1, 0:512], b.ps[0:1, 4 * 512:4 * 512 + 512])
            ctx.mm(b.ps[:, 4 * 512:5 * 512], b.ones_row[:, :],
                   b.rows[0:1, 0:512], start=True, stop=True, mark="mm")
            ctx.act(b.rb[:, :], b.ps[:, 4 * 512:5 * 512], AF.Copy)
            ctx.tt(b.osb[:, hh * 512:(hh + 1) * 512], b.ps[:, 5 * 512:6 * 512],
                   b.rb[:, :], ALU.mult)
        for j in range(FC):
            jp = j % 2
            wload(ctx, "POOL", b.wsl, jp * H, d.wb_o, FC, j,
                  "wlG" if jp == 0 else "wlH")
            for i in range(FC):
                ctx.ldw(b.wsl[:, jp * H + i * 128:jp * H + (i + 1) * 128])
                ctx.mm(b.ps[:, 6 * 512:7 * 512],
                       b.wsl[:, jp * H + i * 128:jp * H + (i + 1) * 128],
                       b.osb[:, i * 512:(i + 1) * 512],
                       start=(i == 0), stop=(i == FC - 1),
                       mark=("mm" if i == FC - 1 else None))
            ctx.dma("ACT", b.xr[jp][:, :],
                    d.x2_t[:, pb * 8192 + j * 512:pb * 8192 + (j + 1) * 512],
                    "ldC" if jp == 0 else "ldD")
            ctx.tt(b.ob[jp][:, :], b.xr[jp][:, :],
                   b.ps[:, 6 * 512:7 * 512], ALU.add)
            ctx.dma("SP", d.x3_t[:, qb * 8192 + j * 512:
                                 qb * 8192 + (j + 1) * 512],
                    b.ob[jp][:, :], "stA" if jp == 0 else "stB")
        if pb == 1:
            ctx.dma("POOL", b.bmask[:, :], d.bmask[:, 0:8 * 512], "wlJ")


def build_program():
    nc = bass.Bass("TRN2", target_bir_lowering=False)
    d = type("D", (), {})()
    dt = nc.dram_tensor
    d.x_t = dt("x_t", [128, 8 * 8192], F32, kind="ExternalInput").ap()
    d.wa_q = dt("wa_q", [128, H * FC], F32, kind="ExternalInput").ap()
    d.wa_f = dt("wa_f", [128, H * FC], F32, kind="ExternalInput").ap()
    d.wa_i = dt("wa_i", [128, H * FC], F32, kind="ExternalInput").ap()
    d.wa_o = dt("wa_o", [128, H * FC], F32, kind="ExternalInput").ap()
    d.ga_t = dt("ga_t", [128, FC], F32, kind="ExternalInput").ap()
    d.wb_q = dt("wb_q", [128, H * FC], F32, kind="ExternalInput").ap()
    d.wb_k = dt("wb_k", [128, H * FC], F32, kind="ExternalInput").ap()
    d.wb_v = dt("wb_v", [H, H], F32, kind="ExternalInput").ap()
    d.wb_o = dt("wb_o", [128, H * FC], F32, kind="ExternalInput").ap()
    d.f_rw = dt("f_rw", [128, FC * 16], F32, kind="ExternalInput").ap()
    d.f_wg = dt("f_wg", [128, E * IM * FC], F32, kind="ExternalInput").ap()
    d.f_wu = dt("f_wu", [128, E * IM * FC], F32, kind="ExternalInput").ap()
    d.f_wd = dt("f_wd", [128, E * H * JM], F32, kind="ExternalInput").ap()
    d.f_sg = dt("f_sg", [128, ISH * FC], F32, kind="ExternalInput").ap()
    d.f_su = dt("f_su", [128, ISH * FC], F32, kind="ExternalInput").ap()
    d.f_sd = dt("f_sd", [128, H * JSH], F32, kind="ExternalInput").ap()
    d.f_sgw = dt("f_sgw", [128, FC], F32, kind="ExternalInput").ap()
    d.c_ident = dt("c_ident", [128, 128], BF16, kind="ExternalInput").ap()
    d.c_ones_col = dt("c_ones_col", [128, 1], BF16, kind="ExternalInput").ap()
    d.c_ones_row = dt("c_ones_row", [1, 128], F32, kind="ExternalInput").ap()
    d.c_maskaa = dt("c_maskaa", [128, 128], BF16, kind="ExternalInput").ap()
    d.c_emat = dt("c_emat", [16, 16 * 128], BF16, kind="ExternalInput").ap()
    d.c_pswap = dt("c_pswap", [128, 128], BF16, kind="ExternalInput").ap()
    d.c_bias = dt("c_bias", [128, 4], F32, kind="ExternalInput").ap()
    d.c_bias2 = dt("c_bias2", [128, 4], F32, kind="ExternalInput").ap()
    d.ropec = dt("ropec", [128, T_MAIN], BF16, kind="ExternalInput").ap()
    d.ropes = dt("ropes", [128, T_MAIN], BF16, kind="ExternalInput").ap()
    d.bmask = dt("bmask", [128, 16 * 512], BF16, kind="ExternalInput").ap()
    d.x1_t = dt("x1_t", [128, 5 * 8192], F32).ap()
    d.x2_t = dt("x2_t", [128, 5 * 8192], F32).ap()
    d.x3_t = dt("x3_t", [128, 4 * 8192], F32).ap()
    d.xln_t = dt("xln_t", [128, 8 * 8192], BF16).ap()
    d.osb_t = dt("osb_t", [128, 5 * 8192], BF16).ap()
    d.wd_bf = dt("wd_bf", [128, E * H * JM], BF16).ap()
    d.sd_bf = dt("sd_bf", [128, H * JSH], BF16).ap()
    d.y_t = dt("y_t", [128, 4 * 8192], F32, kind="ExternalOutput").ap()

    with contextlib.ExitStack() as es0:
        sems = {k: es0.enter_context(nc.semaphore("s_" + k)) for k in SEMS}
        block = es0.enter_context(nc.Block())
        ctx = Ctx(nc, block, sems)

        cst = type("C", (), {})()
        cst.ident = es0.enter_context(nc.sbuf_tensor("c_ident_s", [128, 128], BF16))
        cst.ones_col = es0.enter_context(nc.sbuf_tensor("c_onec_s", [128, 1], BF16))
        cst.ones_row = es0.enter_context(nc.sbuf_tensor("c_oner_s", [1, 128], F32))
        cst.maskaa = es0.enter_context(nc.sbuf_tensor("c_maa_s", [128, 128], BF16))
        cst.emat = es0.enter_context(nc.sbuf_tensor("c_emat_s", [16, 16 * 128], BF16))
        cst.gat = es0.enter_context(nc.sbuf_tensor("c_gat_s", [128, FC], F32))
        cst.rw = es0.enter_context(nc.sbuf_tensor("c_rw_s", [128, FC * 16], BF16))
        cst.comb_t = es0.enter_context(nc.sbuf_tensor("c_combt", [16, T_MAIN], BF16))
        cst.cbias = es0.enter_context(nc.sbuf_tensor("c_cbias", [128, 4], F32))
        cst.cbias2 = es0.enter_context(nc.sbuf_tensor("c_cbias2", [128, 4], F32))
        nc.const_aps.register(F32, LN_EPS, cst.cbias[:, 0:1])
        nc.const_aps.register(F32, 1.0, cst.cbias[:, 1:2])
        nc.const_aps.register(F32, LNC, cst.cbias[:, 2:3])
        nc.const_aps.register(F32, LNC - SH, cst.cbias2[:, 0:1])
        nc.const_aps.register(F32, -SH, cst.cbias2[:, 1:2])
        nc.const_aps.register(F32, SH, cst.cbias2[:, 2:3])

        def ph_init(c):
            c.dma("SP", cst.ident[:, :], d.c_ident[:, :], "ldA")
            c.dma("SP", cst.ones_col[:, :], d.c_ones_col[:, :], "ldB")
            c.dma("SP", cst.ones_row[:, :], d.c_ones_row[:, :], "ldC")
            c.dma("SP", cst.maskaa[:, :], d.c_maskaa[:, :], "ldD")
            c.dma("SP", cst.emat[:, :], d.c_emat[:, :], "stA")
            c.dma("SP", cst.gat[:, :], d.ga_t[:, :], "stB")
            c.dma("SP", cst.cbias[:, :], d.c_bias[:, :], "stC")
            c.dma("POOL", cst.cbias2[:, :], d.c_bias2[:, :], "wlB")
            c.dma("POOL", cst.rw[:, :], d.f_rw[:, :], "wlA")

        run_phase(ctx, ph_init, "init")

        with contextlib.ExitStack() as es:
            b = alloc_ffn_ln(ctx, es, cst, "al")
            run_phase(ctx, lambda c: emit_ffn_ln(c, b, d, d.x_t, 8 * 512,
                                                 router=False), "aln")
        with contextlib.ExitStack() as es:
            b = alloc_aam(ctx, es, cst)
            run_phase(ctx, lambda c: emit_aam(c, b, d), "aam")
        with contextlib.ExitStack() as es:
            b = alloc_awo(ctx, es, cst)
            run_phase(ctx, lambda c: emit_awo(c, b, d), "awo")
        with contextlib.ExitStack() as es:
            b = alloc_ffn_ln(ctx, es, cst, "f1l")
            run_phase(ctx, lambda c: emit_ffn_ln(c, b, d, d.x1_t, T_MAIN), "f1l")
        with contextlib.ExitStack() as es:
            b = alloc_ffn_moe(ctx, es, cst, "f1m")
            run_phase(ctx, lambda c: emit_ffn_moe(c, b, d, d.x1_t, d.x2_t,
                                                  T_MAIN), "f1m")
        with contextlib.ExitStack() as es:
            b = alloc_ffn_ln(ctx, es, cst, "bl")
            run_phase(ctx, lambda c: emit_ffn_ln(c, b, d, d.x2_t, T_MAIN,
                                                 router=False), "abl")
        with contextlib.ExitStack() as es:
            b = alloc_ab(ctx, es, cst)
            b.wi_ab = b.wsl
            run_phase(ctx, lambda c: emit_ab(c, b, d), "ab")
        with contextlib.ExitStack() as es:
            b = alloc_ffn_ln(ctx, es, cst, "f2l")
            run_phase(ctx, lambda c: emit_ffn_ln(c, b, d, d.x3_t, OWN), "f2l")
        with contextlib.ExitStack() as es:
            b = alloc_ffn_moe(ctx, es, cst, "f2m")
            run_phase(ctx, lambda c: emit_ffn_moe(c, b, d, d.x3_t, d.y_t, OWN),
                      "f2m")

        def ph_fin(c):
            pass

        run_phase(ctx, ph_fin, "fin")
    return nc


def _rope_tables(positions):
    inv = 1.0 / (10000.0 ** (np.arange(0, DK, 2, dtype=np.float64) / DK))
    ang = inv[:, None] * positions[None, :].astype(np.float64)
    return np.cos(ang).astype(np.float32), np.sin(ang).astype(np.float32)


def _bmask(parity):
    out = np.zeros((16, 128, 512), dtype=np.float32)
    tk = np.arange(128)[:, None]
    tq = np.arange(512)[None, :]
    for c in range(8):
        dd = tq - tk - 128 * c + 512
        m = ((dd >= 0) & (dd < 512)).astype(np.float32)
        out[c] = m
        if parity == 1:
            out[8 + c] = m
        else:
            out[8 + c] = m if c >= 4 else 0.0
    return out.reshape(16 * 128, 512).astype(bfdt)


def _colchunk(w):
    K, D = w.shape
    return np.ascontiguousarray(
        w.reshape(K // 128, 128, D // 128, 128).transpose(1, 2, 0, 3)
        .reshape(128, -1))


def _blocked(xt):
    Hh, ncols = xt.shape
    nblk = ncols // 512
    return np.ascontiguousarray(
        xt.reshape(16, 128, nblk, 512).transpose(1, 2, 0, 3).reshape(128, -1))


def _perm_deinterleave():
    p = np.zeros(H, dtype=np.int64)
    for h in range(NH):
        base = h * DK
        p[base:base + 64] = base + 2 * np.arange(64)
        p[base + 64:base + DK] = base + 2 * np.arange(64) + 1
    return p


def make_in_maps(L0_attn_a_x, L0_attn_a_Wq, L0_attn_a_Wf, L0_attn_a_Wi,
                 L0_attn_a_Wo, L0_attn_a_g_weight, L0_attn_b_Wq, L0_attn_b_Wk,
                 L0_attn_b_Wv, L0_attn_b_Wo, L0_ffn_router_w, L0_ffn_w_gate,
                 L0_ffn_w_up, L0_ffn_w_down, L0_ffn_sh_gate, L0_ffn_sh_up,
                 L0_ffn_sh_down, L0_ffn_sh_gate_w):
    x = np.asarray(L0_attn_a_x, dtype=np.float32)
    B, S, _ = x.shape
    perm = _perm_deinterleave()

    wg_all = np.empty((H, E * IM), np.float32)
    wu_all = np.empty((H, E * IM), np.float32)
    wd_all = np.empty((IM, E * H), np.float32)
    for e in range(E):
        wg_all[:, e * IM:(e + 1) * IM] = np.asarray(L0_ffn_w_gate[e], np.float32)
        wu_all[:, e * IM:(e + 1) * IM] = np.asarray(L0_ffn_w_up[e], np.float32)
        wd_all[:, e * H:(e + 1) * H] = np.asarray(L0_ffn_w_down[e], np.float32)
    com = {
        "wa_q": _colchunk(np.asarray(L0_attn_a_Wq, dtype=np.float32)),
        "wa_f": _colchunk(np.asarray(L0_attn_a_Wf, dtype=np.float32)),
        "wa_i": _colchunk(np.asarray(L0_attn_a_Wi, dtype=np.float32)),
        "wa_o": _colchunk(np.asarray(L0_attn_a_Wo, dtype=np.float32)),
        "ga_t": np.ascontiguousarray(
            np.asarray(L0_attn_a_g_weight, dtype=np.float32).reshape(FC, 128).T),
        "wb_q": _colchunk(np.ascontiguousarray(
            np.asarray(L0_attn_b_Wq, dtype=np.float32).T[:, perm])),
        "wb_k": _colchunk(np.ascontiguousarray(
            np.asarray(L0_attn_b_Wk, dtype=np.float32).T[:, perm])),
        "wb_v": np.ascontiguousarray(np.asarray(L0_attn_b_Wv, dtype=np.float32).T),
        "wb_o": _colchunk(np.ascontiguousarray(
            np.asarray(L0_attn_b_Wo, dtype=np.float32).T)),
        "f_rw": np.ascontiguousarray(
            np.asarray(L0_ffn_router_w, dtype=np.float32).T
            .reshape(16, 128, 16).transpose(1, 0, 2).reshape(128, 256)),
        "f_wg": _colchunk(wg_all),
        "f_wu": _colchunk(wu_all),
        "f_wd": _colchunk(wd_all),
        "f_sg": _colchunk(np.ascontiguousarray(L0_ffn_sh_gate, dtype=np.float32)),
        "f_su": _colchunk(np.ascontiguousarray(L0_ffn_sh_up, dtype=np.float32)),
        "f_sd": _colchunk(np.ascontiguousarray(L0_ffn_sh_down, dtype=np.float32)),
        "f_sgw": np.ascontiguousarray(np.asarray(L0_ffn_sh_gate_w, dtype=np.float32).reshape(FC, 128).T),
        "c_ident": np.eye(128, dtype=np.float32).astype(bfdt),
        "c_ones_col": np.ones((128, 1), dtype=np.float32).astype(bfdt),
        "c_ones_row": np.ones((1, 128), dtype=np.float32),
        "c_maskaa": (SC_MASK * np.triu(np.ones((128, 128), dtype=np.float32))).astype(bfdt),
    }
    emat = np.zeros((16, 16 * 128), dtype=np.float32)
    for e in range(16):
        emat[e, e * 128:(e + 1) * 128] = 1.0
    com["c_emat"] = emat.astype(bfdt)
    psw = np.zeros((128, 128), dtype=np.float32)
    psw[(np.arange(128) + 64) % 128, np.arange(128)] = 1.0
    com["c_pswap"] = psw.astype(bfdt)
    com["c_bias"] = np.tile(np.array([[LN_EPS, 1.0, LNC, 0.0]], np.float32),
                            (128, 1))
    com["c_bias2"] = np.tile(np.array([[LNC - SH, -SH, SH, 0.0]], np.float32),
                             (128, 1))

    in_maps = []
    for c in range(8):
        bb, odd = c // 2, c % 2
        seq = x[bb]
        if odd:
            chain = seq
        else:
            chain = np.concatenate([np.zeros((2048, H), np.float32), seq[0:2048]],
                                   axis=0)
        m = dict(com)
        m["x_t"] = _blocked(np.ascontiguousarray(chain.T))
        start = 2048 * odd
        pos = np.arange(MAIN0, NT) - (0 if odd else 2048)
        pos = np.maximum(pos, 0)
        rc, rs = _rope_tables(pos)
        m["ropec"] = np.concatenate([rc, rc], axis=0).astype(bfdt)
        m["ropes"] = np.concatenate([-rs, rs], axis=0).astype(bfdt)
        m["bmask"] = np.ascontiguousarray(
            _bmask(odd).reshape(16, 128, 512).transpose(1, 0, 2)
            .reshape(128, 16 * 512))
        in_maps.append(m)
    return in_maps, (B, S)


def kernel(**inputs):
    in_maps, (B, S) = make_in_maps(**inputs)
    nc = build_program()
    res = run_bass_kernel_spmd(nc, in_maps, core_ids=list(range(8)))
    out = np.empty((B, S, H), dtype=np.float32)
    for c in range(8):
        bb, odd = c // 2, c % 2
        y = res.results[c]["y_t"].reshape(128, 4, 16, 512)
        out[bb, odd * 2048:(odd + 1) * 2048, :] = (
            y.transpose(1, 3, 2, 0).reshape(2048, 2048))
    return out
